# version 24
# baseline (speedup 1.0000x reference)
import numpy as np
import ml_dtypes
import concourse.bass as bass
import concourse.mybir as mybir
from concourse.bass_utils import run_bass_kernel_spmd
from concourse.bass_core_rust import IndirectOffsetOnAxis

F32 = mybir.dt.float32
BF16 = mybir.dt.bfloat16
I32 = mybir.dt.int32
AF = mybir.ActivationFunctionType
ALU = mybir.AluOpType
AX = mybir.AxisListType
bfdt = ml_dtypes.bfloat16

ENGINES = ["tensor", "vector", "scalar", "gpsimd", "sync"]
NEG = -1e30


class Op:
    __slots__ = ["eng", "fn", "reads", "writes", "seq", "dma", "tag", "_oc", "_dc", "sreads",
                 "fence_oc", "fence_dc", "_slot", "_dcs"]

    def __init__(self, eng, fn, reads, writes, dma=False, tag="", sreads=()):
        self.eng = eng
        self.fn = fn
        self.reads = reads
        self.writes = writes
        self.dma = dma
        self.tag = tag
        self.sreads = sreads


def _toks(args):
    out = []
    for a in args:
        if a is None or isinstance(a, float):
            continue
        if isinstance(a, (list, tuple)):
            out += _toks(a)
            continue
        t = getattr(a, "tensor", a)
        name = getattr(t, "name", None)
        if name is None:
            out.append((str(t), 0, 1 << 30, 0, 1 << 40))
            continue
        try:
            isz = mybir.dt.size(a.dtype)
            ap = a.ap
            pbase = getattr(a, "base_partition", 0) or 0
            pcnt = int(ap[0][1]) if len(ap) else 128
            flo = int(a.offset) * isz
            ext = 1
            for st, cnt in list(ap)[1:]:
                ext += abs(int(st)) * (int(cnt) - 1)
            fhi = flo + ext * isz
            out.append((name, pbase, pbase + pcnt, flo, fhi))
        except Exception:
            out.append((name, 0, 1 << 30, 0, 1 << 40))
    return out


def _overlap(x, y):
    return (x[0] == y[0] and x[1] < y[2] and y[1] < x[2]
            and x[3] < y[4] and y[3] < x[4])


def _covered(x, y):
    return (x[0] == y[0] and y[1] <= x[1] and x[2] <= y[2]
            and y[3] <= x[3] and x[4] <= y[4])


class Prog:
    def __init__(self, nc):
        self.nc = nc
        self.ops = []
        self._rr = 0

    def add(self, eng, fn, reads, writes, dma=False, tag="", sreads=()):
        sr = _toks(sreads)
        self.ops.append(Op(eng, fn, _toks(reads), _toks(writes), dma, tag, sr))

    def dma(self, eng, out, in_, tag="", slow=False):
        try:
            if out.dtype != in_.dtype:
                eng = "gpsimd"
            else:
                eng = ("sync", "scalar", "gpsimd")[self._rr % 3]
                self._rr += 1
        except Exception:
            pass
        if slow:
            self.add(eng, lambda g: g.dma_start(out=out, in_=in_, allow_slow_non_contiguous=True),
                     [in_], [out], dma=True, tag=tag or "dma")
        else:
            self.add(eng, lambda g: g.dma_start(out=out, in_=in_), [in_], [out], dma=True, tag=tag or "dma")

    def gather(self, out, in_, idx_ap, tag="", accum=False):
        op = ALU.add if accum else ALU.bypass
        self.add("gpsimd",
                 lambda g: g.indirect_dma_start(out, None, in_, IndirectOffsetOnAxis(idx_ap, 0), compute_op=op),
                 [in_, idx_ap], [out], dma=True, tag=tag or "gather")

    def scatter(self, out, in_, idx_ap, tag="", accum=False):
        op = ALU.add if accum else ALU.bypass
        self.add("gpsimd",
                 lambda g: g.indirect_dma_start(out, IndirectOffsetOnAxis(idx_ap, 0), in_, None, compute_op=op),
                 [in_, idx_ap], [out], dma=True, tag=tag or "scatter")

    def mm(self, out, lhsT, rhs, start=True, stop=True, tag=""):
        self.add("tensor", lambda g: g.matmul(out, lhsT, rhs, start=start, stop=stop, skip_group_check=True),
                 [lhsT, rhs], [out], tag=tag or "mm")

    def tr(self, out, in_, ident, tag=""):
        self.add("tensor", lambda g: g.transpose(out, in_, ident), [in_, ident], [out], tag=tag or "tr")

    def act(self, out, in_, func, bias=0.0, scale=1.0, accum_out=None, eng="scalar", tag=""):
        self.add(eng,
                 lambda g: g.activation(out, in_, func, bias=bias, scale=scale, accum_out=accum_out),
                 [in_], [out, accum_out], tag=tag or "act", sreads=[bias, scale])

    def tt(self, out, a, b, op, eng="vector", tag=""):
        self.add(eng, lambda g: g.tensor_tensor(out, a, b, op), [a, b], [out], tag=tag or "tt")

    def ts(self, out, a, s1, s2=None, op0=ALU.mult, op1=ALU.bypass, eng="vector", tag=""):
        if eng == "scalar":
            assert s2 is None and isinstance(s1, float)
            fn = (lambda g: g.mul(out, a, s1)) if op0 == ALU.mult else (lambda g: g.add(out, a, s1))
            self.add(eng, fn, [a], [out], tag=tag or "ts")
        else:
            self.add(eng, lambda g: g.tensor_scalar(out, a, s1, s2, op0, op1),
                     [a], [out], tag=tag or "ts", sreads=[s1, s2])

    def stt(self, out, in0, scalar, in1, op0, op1, eng="vector", tag=""):
        self.add(eng, lambda g: g.scalar_tensor_tensor(out, in0, scalar, in1, op0, op1),
                 [in0, in1], [out], tag=tag or "stt", sreads=[scalar])

    def copy(self, out, in_, eng="vector", tag=""):
        if eng == "scalar":
            self.add(eng, lambda g: g.copy(out, in_), [in_], [out], tag=tag or "copy")
        else:
            self.add(eng, lambda g: g.tensor_copy(out, in_), [in_], [out], tag=tag or "copy")

    def memset(self, ap, val, eng="vector", tag=""):
        self.add(eng, lambda g: g.memset(ap, val), [], [ap], tag=tag or "memset")

    def red(self, out, in_, op, negate=False, eng="vector", tag=""):
        self.add(eng, lambda g: g.tensor_reduce(out, in_, AX.X, op, negate=negate),
                 [in_], [out], tag=tag or "red")

    def recip(self, out, in_, tag=""):
        self.add("vector", lambda g: g.reciprocal(out, in_), [in_], [out], tag=tag or "recip")

    def sel(self, out, mask, a, b, tag=""):
        self.add("vector", lambda g: g.select(out, mask, a, b), [mask, a, b], [out], tag=tag or "sel")

    def scan(self, out, d0, d1, initial, op0, op1, eng="vector", tag=""):
        self.add(eng, lambda g: g.tensor_tensor_scan(out, d0, d1, initial, op0, op1),
                 [d0, d1], [out], tag=tag or "scan")

    def raw(self, eng, fn, reads, writes, tag=""):
        self.add(eng, fn, reads, writes, tag=tag)

    def fence_all(self):
        for e in ENGINES:
            op = Op(e, lambda g: g.engine_nop() if hasattr(g, "engine_nop") else None,
                    [], [], False, "fence")
            op.sreads = ()
            self.ops.append(op)

    def emit(self):
        nc = self.nc
        per_eng = {e: [] for e in ENGINES}
        for op in self.ops:
            per_eng[op.eng].append(op)
        writes_by = {}
        reads_by = {}
        edges = [set() for _ in self.ops]
        sedges = [set() for _ in self.ops]
        for i, op in enumerate(self.ops):
            for t in op.reads:
                for (w, j) in writes_by.get(t[0], ()):
                    if _overlap(t, w):
                        edges[i].add(j)
            for t in op.sreads:
                for (w, j) in writes_by.get(t[0], ()):
                    if _overlap(t, w):
                        sedges[i].add(j)
            for t in op.writes:
                for (w, j) in writes_by.get(t[0], ()):
                    if _overlap(t, w):
                        edges[i].add(j)
                for (r, j) in reads_by.get(t[0], ()):
                    if _overlap(t, r):
                        edges[i].add(j)
            for t in list(op.reads) + list(op.sreads):
                reads_by.setdefault(t[0], []).append((t, i))
            for t in op.writes:
                lw = writes_by.setdefault(t[0], [])
                lw[:] = [(w, j) for (w, j) in lw if not _covered(w, t)]
                lw.append((t, i))
                rb = reads_by.get(t[0])
                if rb:
                    rb[:] = [(r, j) for (r, j) in rb if not _covered(r, t)]
        NSLOT = 4
        has_dma = {e: any(op.dma for op in per_eng[e]) for e in ENGINES}
        sems, dsems = {}, {}
        for e in ENGINES:
            sems[e] = nc.alloc_semaphore(f"sem_{e}")
            if has_dma[e]:
                dsems[e] = [nc.alloc_semaphore(f"dsem_{e}{s}") for s in range(NSLOT)]
        idx_all = {id(op): i for i, op in enumerate(self.ops)}
        xconsumed = set()
        for j, op in enumerate(self.ops):
            for i in (edges[j] | sedges[j]):
                if self.ops[i].eng != op.eng:
                    xconsumed.add(i)
        known = {e: {p: 0 for p in ENGINES} for e in ENGINES}
        knownd = {e: {} for e in ENGINES}
        for e in ENGINES:
            oc = 0
            dcs = [0] * NSLOT
            dc = 0
            for op in per_eng[e]:
                if op.tag == "fence":
                    op._oc = oc
                    continue
                if op.dma:
                    op._slot = dc % NSLOT
                    dcs[op._slot] += 1
                    op._dcs = dcs[op._slot]
                    dc += 1
                else:
                    oc += 1
                op._oc = oc
        final_dma = {}
        for e in ENGINES:
            cnt = [0] * NSLOT
            for op in per_eng[e]:
                if op.dma:
                    cnt[op._slot] += 1
            final_dma[e] = cnt
        final_oc = {e: sum(1 for op in per_eng[e] if (not op.dma and op.tag != "fence")) for e in ENGINES}

        run_oc = {e: 0 for e in ENGINES}
        run_dc = {e: [0] * NSLOT for e in ENGINES}
        for op in self.ops:
            if op.tag == "fence":
                op.fence_oc = dict(run_oc)
                op.fence_dc = {e: list(run_dc[e]) for e in ENGINES}
            elif op.dma:
                run_dc[op.eng][op._slot] += 1
            else:
                run_oc[op.eng] += 1

        def emit_engine(g, e):
            for op in per_eng[e]:
                i = idx_all[id(op)]
                wants, wantsd = {}, {}
                if op.tag == "fence":
                    for pe in ENGINES:
                        if op.fence_oc[pe] and pe != e:
                            wants[pe] = op.fence_oc[pe]
                        for s in range(NSLOT):
                            if op.fence_dc[pe][s]:
                                wantsd[(pe, s)] = op.fence_dc[pe][s] * 16
                for j in edges[i]:
                    p = self.ops[j]
                    if p.dma:
                        k = (p.eng, p._slot)
                        wantsd[k] = max(wantsd.get(k, 0), p._dcs * 16)
                    else:
                        wants[p.eng] = max(wants.get(p.eng, 0), p._oc)
                for j in sedges[i]:
                    p = self.ops[j]
                    if p.dma:
                        k = (p.eng, p._slot)
                        wantsd[k] = max(wantsd.get(k, 0), p._dcs * 16)
                    else:
                        wants[p.eng] = max(wants.get(p.eng, 0), p._oc)
                for pe, v in wants.items():
                    if known[e][pe] < v:
                        g.wait_ge(sems[pe], v)
                        known[e][pe] = v
                for (pe, s), v in wantsd.items():
                    if knownd[e].get((pe, s), 0) < v:
                        g.wait_ge(dsems[pe][s], v)
                        knownd[e][(pe, s)] = v
                if op.tag == "fence":
                    continue
                if op.dma and op._dcs > 1 and knownd[e].get((e, op._slot), 0) < (op._dcs - 1) * 16:
                    g.wait_ge(dsems[e][op._slot], (op._dcs - 1) * 16)
                    knownd[e][(e, op._slot)] = (op._dcs - 1) * 16
                inst = op.fn(g)
                if inst is not None:
                    if op.dma:
                        inst.then_inc(dsems[e][op._slot], 16)
                    else:
                        inst.then_inc(sems[e], 1)

        with nc.Block() as block:
            for e, deco in (("tensor", block.tensor), ("vector", block.vector),
                            ("scalar", block.scalar), ("gpsimd", block.gpsimd),
                            ("sync", block.sync)):
                def make(e=e):
                    def body(g):
                        emit_engine(g, e)
                        if e == "gpsimd":
                            for pe in ENGINES:
                                if has_dma[pe]:
                                    for s in range(NSLOT):
                                        if final_dma[pe][s]:
                                            g.wait_ge(dsems[pe][s], final_dma[pe][s] * 16)
                                if final_oc[pe] and pe != "gpsimd":
                                    g.wait_ge(sems[pe], final_oc[pe])
                    return body
                deco(make())
        return self



def host_consts_A():
    i = np.arange(128)
    c = {}
    c["ident_f32"] = np.eye(128, dtype=np.float32)
    c["causal_add"] = np.where(i[None, :] <= i[:, None], 0.0, NEG).astype(np.float32)
    c["causalT_add"] = np.where(i[:, None] <= i[None, :], 0.0, NEG).astype(np.float32)
    c["iota256_row"] = np.tile(np.arange(1, 257, dtype=np.float32), (128, 1))
    c["iota_row2048"] = np.tile(np.arange(2048, dtype=np.float32), (128, 1))
    c["qidx_cols"] = np.ascontiguousarray(np.arange(2048, dtype=np.float32).reshape(16, 128).T)
    return c


def wA_prep(w):
    d = {}
    for nm, key in (("wq", "L0_attn_w_q"), ("wk", "L0_attn_w_k"), ("wv", "L0_attn_w_v"),
                    ("wo", "L0_attn_w_out")):
        d[nm] = np.ascontiguousarray(np.asarray(w[key], np.float32).reshape(8, 128, 1024).transpose(1, 0, 2))
    d["scorer_w"] = np.ascontiguousarray(np.asarray(w["L0_attn_scorer_w"], np.float32).reshape(8, 128, 1).transpose(1, 0, 2))
    d["Wr"] = np.ascontiguousarray(np.asarray(w["L0_ffn_Wr"], np.float32).reshape(8, 128, 8).transpose(1, 0, 2))
    d["norm_w"] = np.ascontiguousarray(np.asarray(w["L0_attn_norm_w"], np.float32).reshape(8, 128).T)
    d["scorer_b"] = np.asarray(w["L0_attn_scorer_b"], np.float32).reshape(1, 1)
    return d


def build_phaseA():
    nc = bass.Bass("TRN2", target_bir_lowering=False, name="phA")
    B, S, D = 2, 2048, 1024
    H, G = 16, 256
    NT = S // 128
    x = nc.dram_tensor("x", [B * S, D], F32, kind="ExternalInput").ap()
    win = {}
    for nm, shp in (("wq", [128, 8, 1024]), ("wk", [128, 8, 1024]), ("wv", [128, 8, 1024]),
                    ("wo", [128, 8, 1024]), ("scorer_w", [128, 8, 1]), ("scorer_b", [1, 1]),
                    ("Wr", [128, 8, 8]), ("norm_w", [128, 8])):
        win[nm] = nc.dram_tensor(nm, shp, F32, kind="ExternalInput").ap()
    cst = {}
    for nm, shp in (("ident_f32", [128, 128]), ("causal_add", [128, 128]),
                    ("causalT_add", [128, 128]), ("iota256_row", [128, 256]),
                    ("iota_row2048", [128, 2048]), ("qidx_cols", [128, 16])):
        cst[nm] = nc.dram_tensor(nm, shp, F32, kind="ExternalInput").ap()
    xn_out = nc.dram_tensor("xn_bf16", [B * S, D], BF16, kind="ExternalOutput").ap()
    scores_out = nc.dram_tensor("scores", [B * S, 8], F32, kind="ExternalOutput").ap()
    s0_out = nc.dram_tensor("s0_fm", [B, D, S], F32, kind="ExternalOutput").ap()
    h_fm_hbm = nc.dram_tensor("h_fm_hbm", [B, D, S], BF16).ap()
    xn_fm_hbm = nc.dram_tensor("xn_fm_hbm", [B, D, S], F32).ap()
    q_fm_hbm = nc.dram_tensor("q_fm_hbm", [B, D, S], BF16).ap()
    k_fm_hbm = nc.dram_tensor("k_fm_hbm", [B, D, S], BF16).ap()
    k_tm_hbm = nc.dram_tensor("k_tm_hbm", [B, S, D], BF16).ap()
    v_tm_hbm = nc.dram_tensor("v_tm_hbm", [B, S, D], BF16).ap()
    sc_hbm = nc.dram_tensor("sc_hbm", [B, S, 1], F32).ap()
    row_hbm = nc.dram_tensor("row_hbm", [2048, 1], F32).ap()

    p = Prog(nc)
    al = nc.alloc_sbuf_tensor
    wtmp = al("wtmp", [128, 512], F32).ap()
    wtmp2 = al("wtmp2", [128, 512], F32).ap()
    wqb = al("wqb", [128, 8, 1024], BF16).ap()
    wkb = al("wkb", [128, 8, 1024], BF16).ap()
    wvb = al("wvb", [128, 8, 1024], BF16).ap()
    wob = al("wob", [128, 8, 1024], BF16).ap()
    s_scw = al("s_scw", [128, 8, 1], F32).ap()
    s_scb = al("s_scb", [1, 1], F32).ap()
    s_wr = al("s_wr", [128, 8, 8], F32).ap()
    s_nw = al("s_nw", [128, 8], F32).ap()
    s_cst = {nm: al("s_" + nm, list(cst[nm].shape), F32).ap() for nm in cst}
    identb = al("identb", [128, 128], BF16).ap()
    ones_r = al("ones_r", [1, 128], F32).ap()
    scb_col = al("scb_col", [128, 1], F32).ap()
    eps5 = al("eps5", [128, 1], F32).ap()
    eps6 = al("eps6", [128, 1], F32).ap()

    xt = al("xt", [128, 1024], F32).ap()
    xs1 = al("xs1", [128, 1], F32).ap()
    xs2 = al("xs2", [128, 1], F32).ap()
    xnt = al("xnt", [128, 1024], F32).ap()
    xnb = al("xnb", [128, 1024], BF16).ap()
    hst = xt
    hfm_t = al("hfm_t", [128, 1024], F32).ap()
    hfm_b = al("hfm_b", [128, 1024], BF16).ap()
    xnfm_t = al("xnfm_t", [128, 1024], F32).ap()
    lgt = al("lgt", [128, 8], F32).ap()
    lm = al("lm", [128, 1], F32).ap()
    ldn = al("ldn", [128, 1], F32).ap()
    sc_cols = al("sc_cols", [128, 2, 16], F32).ap()

    actbuf = al("actbuf", [128, 8, 2048], BF16).ap()
    scb_b = actbuf.bitcast(F32).rearrange("p a b -> p (a b)")[:, 0:2048]
    vtmh = al("vtmh", [128, 16, 64], BF16).ap()
    qfm_c = al("qfm_c", [128, 2048], BF16).ap()
    kfm_c = al("kfm_c", [128, 2048], BF16).ap()
    ktt = al("ktt", [128, 1024], BF16).ap()
    trc4A = al("trc4A", [128, 4, 128], BF16).ap()
    rank_c = al("rank_c", [128, 1], F32).ap()
    cmp_t = xt
    memb_c = al("memb_c", [128, 16], F32).ap()
    memb_r = al("memb_r", [1, 2048], F32).ap()
    cum_r = al("cum_r", [1, 2048], F32).ap()
    zero1 = al("zero1", [1, 1], F32).ap()
    cum_c = al("cum_c", [128, 16], F32).ap()
    onehf = al("onehf", [128, 256], F32).ap()
    oneh_t = al("oneh_t", [128, 256], BF16).ap()
    pg_c = al("pg_c", [128, 2], F32).ap()
    pg_r = al("pg_r", [1, 256], F32).ap()
    pgb = al("pgb", [128, 256], F32).ap()
    kg_tm = al("kg_tm", [128, 2, 1024], BF16).ap()
    vg_tm = al("vg_tm", [128, 2, 1024], BF16).ap()
    kg_fm = al("kg_fm", [128, 8, 256], BF16).ap()
    gadd = al("gadd", [128, 256], F32).ap()
    gaddT = al("gaddT", [128, 512], F32).ap()
    sml = al("sml", [128, 512], F32).ap()
    smg = al("smg", [128, 256], F32).ap()
    m_a = al("m_a", [128, 1], F32).ap()
    m_b = al("m_b", [128, 1], F32).ap()
    m_c = al("m_c", [128, 1], F32).ap()
    m_col = al("m_col", [128, 16], F32).ap()

    mb_s = al("mb_s", [128, 2, 512], F32).ap()
    cmb = al("cmb", [128, 512], F32).ap()
    pT = al("pT", [128, 512], BF16).ap()
    vden = al("vden", [128, 16, 65], BF16).ap()
    vgden = al("vgden", [128, 2, 65], BF16).ap()

    fac_b = al("fac_b", [64, 512], F32).ap()
    s0t = al("s0t", [128, 512], F32).ap()
    sc_rowt = al("sc_rowt", [1, 2048], F32).ap()
    mneg_r = sc_rowt[0:1, 0:1024]
    den_r = sc_rowt[0:1, 1024:1536]
    fac_r = sc_rowt[0:1, 1536:2048]

    psA = nc.alloc_psum_tensor("psA", [128, 512], F32).ap()
    psB = nc.alloc_psum_tensor("psB", [128, 512], F32).ap()
    psC = nc.alloc_psum_tensor("psC", [128, 512], F32).ap()
    psD = nc.alloc_psum_tensor("psD", [128, 512], F32).ap()
    psE = nc.alloc_psum_tensor("psE", [128, 512], F32).ap()
    psF = nc.alloc_psum_tensor("psF", [128, 512], F32).ap()
    psG = nc.alloc_psum_tensor("psG", [128, 512], F32).ap()
    psH = nc.alloc_psum_tensor("psH", [128, 512], F32).ap()

    _wi = 0
    for nm, dst in (("wq", wqb), ("wk", wkb), ("wv", wvb), ("wo", wob)):
        for c in range(8):
            for hv in range(2):
                wt = wtmp if _wi % 2 == 0 else wtmp2
                _wi += 1
                p.dma("sync", wt, win[nm][:, c, hv * 512:(hv + 1) * 512])
                p.copy(dst[:, c, hv * 512:(hv + 1) * 512], wt)
    p.dma("sync", s_scw, win["scorer_w"])
    p.dma("sync", s_scb, win["scorer_b"])
    p.dma("sync", s_wr, win["Wr"])
    p.dma("sync", s_nw, win["norm_w"])
    for nm in cst:
        p.dma("sync", s_cst[nm], cst[nm])
    p.copy(identb, s_cst["ident_f32"])
    p.memset(ones_r, 1.0)
    p.memset(eps5, 1e-5)
    p.memset(eps6, 1e-6)
    p.mm(psE[:, 0:1], ones_r, s_scb, tag="bscb")
    p.copy(scb_col, psE[:, 0:1])

    def col_to_row(col_ap, row_ap, tag=""):
        p.tr(psE[0:1, 0:128], col_ap, s_cst["ident_f32"], tag=tag or "c2r")
        p.copy(row_ap, psE[0:1, 0:128], eng="scalar")

    for b in range(B):
        for t in range(NT):
            row0 = b * S + t * 128
            p.dma("sync", xt, x[row0:row0 + 128, :])
            p.act(hfm_t, xt, AF.Identity, accum_out=xs1)
            p.ts(xs1, xs1, 1.0 / D, None, ALU.mult)
            p.ts(xnt, xt, xs1[:, 0:1], None, ALU.subtract)
            p.act(hfm_t, xnt, AF.Square, accum_out=xs2)
            p.ts(xs2, xs2, 1.0 / D, None, ALU.mult)
            p.act(xs2, xs2, AF.Ln, bias=eps5[:, 0:1])
            p.act(xs2, xs2, AF.Exp, scale=-0.5)
            p.ts(xnt, xnt, xs2[:, 0:1], None, ALU.mult)
            p.copy(xnb, xnt)
            p.dma("gpsimd", xn_out[row0:row0 + 128, :], xnb)
            p.act(hfm_t, xnt, AF.Square, accum_out=xs1)
            p.ts(xs1, xs1, 1.0 / D, None, ALU.mult)
            p.act(xs1, xs1, AF.Ln, bias=eps6[:, 0:1])
            p.act(xs1, xs1, AF.Exp, scale=-0.5)
            p.ts(hst, xnt, xs1[:, 0:1], None, ALU.mult)
            for c in range(8):
                p.tr(psA[:, :128], xnt[:, c * 128:(c + 1) * 128], s_cst["ident_f32"])
                p.copy(xnfm_t[:, c * 128:(c + 1) * 128], psA[:, :128])
                p.tr(psB[:, :128], hst[:, c * 128:(c + 1) * 128], s_cst["ident_f32"])
                p.ts(hfm_t[:, c * 128:(c + 1) * 128], psB[:, :128], s_nw[:, c:c + 1], None, ALU.mult)
            p.copy(hfm_b, hfm_t)
            p.dma("gpsimd", xn_fm_hbm[b].rearrange("(c p) s -> p c s", p=128)[:, :, t * 128:(t + 1) * 128],
                  xnfm_t.rearrange("p (c q) -> p c q", c=8))
            p.dma("gpsimd", h_fm_hbm[b].rearrange("(c p) s -> p c s", p=128)[:, :, t * 128:(t + 1) * 128],
                  hfm_b.rearrange("p (c q) -> p c q", c=8))
            for c in range(8):
                p.mm(psC[:, 0:8], xnfm_t[:, c * 128:(c + 1) * 128], s_wr[:, c, :],
                     start=(c == 0), stop=(c == 7))
            p.red(lm, psC[:, 0:8], ALU.max, negate=True)
            p.act(lgt, psC[:, 0:8], AF.Exp, bias=lm[:, 0:1], accum_out=ldn)
            p.recip(ldn, ldn)
            p.ts(lgt, lgt, ldn[:, 0:1], None, ALU.mult)
            p.dma("gpsimd", scores_out[row0:row0 + 128, :], lgt)
            for c in range(8):
                p.mm(psD[:, 0:1], hfm_t[:, c * 128:(c + 1) * 128], s_scw[:, c, :],
                     start=(c == 0), stop=(c == 7))
            p.tt(sc_cols[:, b, t:t + 1], psD[:, 0:1], scb_col, ALU.add)
            p.dma("gpsimd", sc_hbm[b, t * 128:(t + 1) * 128, :], sc_cols[:, b, t:t + 1])

    p.fence_all()
    scale = 1.0 / 8.0
    o_fm = actbuf
    for b in range(B):
        p.dma("sync", actbuf, h_fm_hbm[b].rearrange("(c p) s -> p c s", p=128))
        for tc in range(4):
            tsl = slice(tc * 512, (tc + 1) * 512)
            for e in range(8):
                for c in range(8):
                    p.mm(psA, wqb[:, c, e * 128:(e + 1) * 128], actbuf[:, c, tsl],
                         start=(c == 0), stop=(c == 7))
                p.copy(qfm_c[:, 0:512], psA)
                p.dma("gpsimd", q_fm_hbm[b, e * 128:(e + 1) * 128, tsl], qfm_c[:, 0:512])
                for c in range(8):
                    p.mm(psB, wkb[:, c, e * 128:(e + 1) * 128], actbuf[:, c, tsl],
                         start=(c == 0), stop=(c == 7))
                p.copy(kfm_c[:, 0:512], psB)
                p.dma("gpsimd", k_fm_hbm[b, e * 128:(e + 1) * 128, tsl], kfm_c[:, 0:512])
                for j in range(4):
                    p.tr(psC.bitcast(BF16)[:, 0:128], kfm_c[:, j * 128:(j + 1) * 128], identb)
                    p.copy(trc4A[:, j, :], psC.bitcast(BF16)[:, 0:128])
                p.dma("gpsimd", k_tm_hbm[b, tc * 512:(tc + 1) * 512,
                                         e * 128:(e + 1) * 128].rearrange("(j p) q -> p j q", p=128),
                      trc4A)
            for j in range(4):
                t0 = tc * 512 + j * 128
                for c in range(8):
                    p.mm(psD, actbuf[:, c, t0:t0 + 128], wvb[:, c, 0:512],
                         start=(c == 0), stop=(c == 7))
                for c in range(8):
                    p.mm(psE, actbuf[:, c, t0:t0 + 128], wvb[:, c, 512:1024],
                         start=(c == 0), stop=(c == 7))
                p.copy(ktt[:, 0:512], psD)
                p.copy(ktt[:, 512:1024], psE)
                p.dma("gpsimd", v_tm_hbm[b, t0:t0 + 128, :], ktt)
        p.dma("sync", sc_rowt, sc_hbm[b].rearrange("s one -> one s"))
        for j in range(4):
            p.mm(psF, ones_r, sc_rowt[0:1, j * 512:(j + 1) * 512], tag="bcast")
            p.copy(scb_b[:, j * 512:(j + 1) * 512], psF, eng="scalar")
        for t in range(NT):
            p.ts(cmp_t, scb_b[:, 0:1024], sc_cols[:, b, t:t + 1], None, ALU.is_gt)
            p.red(rank_c, cmp_t, ALU.add)
            p.ts(cmp_t, scb_b[:, 1024:2048], sc_cols[:, b, t:t + 1], None, ALU.is_gt)
            p.red(m_a, cmp_t, ALU.add)
            p.tt(rank_c, rank_c, m_a, ALU.add)
            p.ts(memb_c[:, t:t + 1], rank_c, float(G), None, ALU.is_lt)
            col_to_row(memb_c[:, t:t + 1], memb_r[0:1, t * 128:(t + 1) * 128])
        p.memset(zero1, 0.0)
        p.scan(cum_r, memb_r, zero1.broadcast_to((1, 2048)), 0.0, ALU.add, ALU.add)
        p.dma("gpsimd", row_hbm.rearrange("s one -> one s"), cum_r)
        for t in range(NT):
            p.dma("sync", cum_c[:, t:t + 1], row_hbm[t * 128:(t + 1) * 128, :])
        for t in range(NT):
            p.ts(onehf, s_cst["iota256_row"], cum_c[:, t:t + 1], memb_c[:, t:t + 1],
                 ALU.is_equal, ALU.mult)
            p.copy(oneh_t, onehf)
            p.mm(psG[:, 0:1], onehf[:, 0:128], s_cst["qidx_cols"][:, t:t + 1],
                 start=(t == 0), stop=(t == NT - 1), tag="pg0")
            p.mm(psH[:, 0:1], onehf[:, 128:256], s_cst["qidx_cols"][:, t:t + 1],
                 start=(t == 0), stop=(t == NT - 1), tag="pg1")
            p.dma("sync", ktt, k_tm_hbm[b, t * 128:(t + 1) * 128, :])
            p.mm(psA, oneh_t[:, 0:128], ktt[:, 0:512], start=(t == 0), stop=(t == NT - 1), tag="kg00")
            p.mm(psB, oneh_t[:, 0:128], ktt[:, 512:1024], start=(t == 0), stop=(t == NT - 1), tag="kg01")
            p.mm(psD, oneh_t[:, 128:256], ktt[:, 0:512], start=(t == 0), stop=(t == NT - 1), tag="kg10")
            p.mm(psE, oneh_t[:, 128:256], ktt[:, 512:1024], start=(t == 0), stop=(t == NT - 1), tag="kg11")
        p.copy(pg_c[:, 0:1], psG[:, 0:1])
        p.copy(pg_c[:, 1:2], psH[:, 0:1])
        p.copy(kg_tm[:, 0, 0:512], psA)
        p.copy(kg_tm[:, 0, 512:1024], psB)
        p.copy(kg_tm[:, 1, 0:512], psD)
        p.copy(kg_tm[:, 1, 512:1024], psE)
        for t in range(NT):
            p.ts(onehf, s_cst["iota256_row"], cum_c[:, t:t + 1], memb_c[:, t:t + 1],
                 ALU.is_equal, ALU.mult)
            p.copy(oneh_t, onehf)
            p.dma("sync", ktt, v_tm_hbm[b, t * 128:(t + 1) * 128, :])
            p.mm(psA, oneh_t[:, 0:128], ktt[:, 0:512], start=(t == 0), stop=(t == NT - 1), tag="vg00")
            p.mm(psB, oneh_t[:, 0:128], ktt[:, 512:1024], start=(t == 0), stop=(t == NT - 1), tag="vg01")
            p.mm(psD, oneh_t[:, 128:256], ktt[:, 0:512], start=(t == 0), stop=(t == NT - 1), tag="vg10")
            p.mm(psE, oneh_t[:, 128:256], ktt[:, 512:1024], start=(t == 0), stop=(t == NT - 1), tag="vg11")
        p.copy(vg_tm[:, 0, 0:512], psA)
        p.copy(vg_tm[:, 0, 512:1024], psB)
        p.copy(vg_tm[:, 1, 0:512], psD)
        p.copy(vg_tm[:, 1, 512:1024], psE)
        col_to_row(pg_c[:, 0:1], pg_r[0:1, 0:128])
        col_to_row(pg_c[:, 1:2], pg_r[0:1, 128:256])
        p.mm(psF[:, 0:256], ones_r, pg_r, tag="bcastpg")
        p.copy(pgb, psF[:, 0:256], eng="scalar")
        for gc in range(2):
            for c in range(8):
                p.tr(psC.bitcast(BF16)[:, 0:128], kg_tm[:, gc, c * 128:(c + 1) * 128], identb)
                p.copy(kg_fm[:, c, gc * 128:(gc + 1) * 128], psC.bitcast(BF16)[:, 0:128])
        for kt in range(16):
            p.memset(vden[:, kt, 64:65], 1.0)
        for gc in range(2):
            p.memset(vgden[:, gc, 64:65], 1.0)
        for hc in range(8):
            p.dma("sync", qfm_c, q_fm_hbm[b, hc * 128:(hc + 1) * 128, :])
            p.dma("sync", kfm_c, k_fm_hbm[b, hc * 128:(hc + 1) * 128, :])
            for hh in range(2):
                h = 2 * hc + hh
                hp = 64 * hh
                p.dma("sync", vtmh, v_tm_hbm[b].rearrange("(t p) e -> p t e", p=128)[:, :, h * 64:(h + 1) * 64])
                for kt in range(16):
                    p.copy(vden[:, kt, 0:64], vtmh[:, kt, :])
                for gc in range(2):
                    p.copy(vgden[:, gc, 0:64], vg_tm[:, gc, h * 64:(h + 1) * 64])
                for w in range(2):
                    w0 = w * 1024
                    for tq in range(8):
                        t = w * 8 + tq
                        q0 = w0 + tq * 128
                        nfull = tq * 128
                        have_ma = False
                        for kk in range(0, nfull, 512):
                            nn = min(512, nfull - kk)
                            p.mm(psA[:, :nn], qfm_c[hp:hp + 64, q0:q0 + 128],
                                 kfm_c[hp:hp + 64, w0 + kk:w0 + kk + nn])
                            p.red(m_b if have_ma else m_a, psA[:, :nn], ALU.max)
                            if have_ma:
                                p.tt(m_a, m_a, m_b, ALU.max)
                            have_ma = True
                        p.mm(psA[:, :128], qfm_c[hp:hp + 64, q0:q0 + 128], kfm_c[hp:hp + 64, q0:q0 + 128])
                        p.stt(sml[:, :128], psA[:, :128], 1.0, s_cst["causal_add"], ALU.mult, ALU.add)
                        p.red(m_b, sml[:, :128], ALU.max)
                        if have_ma:
                            p.tt(m_a, m_a, m_b, ALU.max)
                        else:
                            p.copy(m_a, m_b)
                        p.mm(psB[:, 0:256], qfm_c[hp:hp + 64, q0:q0 + 128], kg_fm[hp:hp + 64, hc, :])
                        p.ts(gadd, pgb, s_cst["qidx_cols"][:, t:t + 1], NEG, ALU.is_gt, ALU.mult)
                        p.stt(smg, psB[:, 0:256], 1.0, gadd, ALU.mult, ALU.add)
                        p.red(m_c, smg, ALU.max)
                        p.tt(m_a, m_a, m_c, ALU.max)
                        p.ts(m_col[:, t:t + 1], m_a, scale, None, ALU.mult)
                    for tq in range(8):
                        t = w * 8 + tq
                        p.tr(psE[0:1, 0:128], m_col[:, t:t + 1], s_cst["ident_f32"], tag="c2rm")
                        p.ts(mneg_r[0:1, tq * 128:(tq + 1) * 128], psE[0:1, 0:128], -1.0, None,
                             ALU.mult, eng="scalar")
                    for qc in range(2):
                        p.mm(psF, ones_r, mneg_r[0:1, qc * 512:(qc + 1) * 512], tag="bcm")
                        p.copy(mb_s[:, qc, :], psF, eng="scalar")
                    for qc in range(2):
                        qa = w0 + qc * 512
                        for gc in range(2):
                            p.mm(psB, kg_fm[hp:hp + 64, hc, gc * 128:(gc + 1) * 128],
                                 qfm_c[hp:hp + 64, qa:qa + 512], tag="mmgT")
                            p.ts(gaddT, s_cst["iota_row2048"][:, qa:qa + 512], pg_c[:, gc:gc + 1],
                                 NEG, ALU.is_lt, ALU.mult)
                            p.tt(cmb, gaddT, mb_s[:, qc, :], ALU.add)
                            p.stt(sml, psB, scale, cmb, ALU.mult, ALU.add)
                            p.act(pT, sml, AF.Exp)
                            p.mm(psC[0:65, :], vgden[:, gc, :], pT, start=(gc == 0), stop=False, tag="avg")
                        valid_kts = [kt for kt in range(8) if max(kt * 128, qc * 512) < (qc + 1) * 512]
                        for kt in valid_kts:
                            k0 = w0 + kt * 128
                            lo = max(kt * 128, qc * 512)
                            nn = (qc + 1) * 512 - lo
                            off = lo - qc * 512
                            qb0 = w0 + lo
                            p.mm(psB[:, :nn], kfm_c[hp:hp + 64, k0:k0 + 128], qfm_c[hp:hp + 64, qb0:qb0 + nn],
                                 tag="mmlT")
                            if lo == kt * 128:
                                p.tt(cmb[:, :128], s_cst["causalT_add"], mb_s[:, qc, off:off + 128], ALU.add)
                                if nn > 128:
                                    p.copy(cmb[:, 128:nn], mb_s[:, qc, off + 128:off + nn])
                                p.stt(sml[:, :nn], psB[:, :nn], scale, cmb[:, :nn], ALU.mult, ALU.add)
                            else:
                                p.stt(sml[:, :nn], psB[:, :nn], scale, mb_s[:, qc, off:off + nn],
                                      ALU.mult, ALU.add)
                            p.act(pT[:, :nn], sml[:, :nn], AF.Exp)
                            p.mm(psC[0:65, off:off + nn], vden[:, w * 8 + kt, :], pT[:, :nn],
                                 start=False, stop=(kt == valid_kts[-1]), tag="avl")
                        p.copy(den_r, psC[64:65, 0:512])
                        p.recip(fac_r, den_r)
                        p.mm(psF, ones_r, fac_r, tag="bcf")
                        p.copy(fac_b, psF[0:64, :], eng="scalar")
                        p.tt(o_fm[hp:hp + 64, hc, qa:qa + 512], psC[0:64, :], fac_b, ALU.mult)

        for tc in range(4):
            tsl = slice(tc * 512, (tc + 1) * 512)
            for e in range(8):
                for c in range(8):
                    p.mm(psA, wob[:, c, e * 128:(e + 1) * 128], o_fm[:, c, tsl],
                         start=(c == 0), stop=(c == 7))
                p.dma("sync", s0t, xn_fm_hbm[b, e * 128:(e + 1) * 128, tsl])
                p.tt(s0t, s0t, psA, ALU.add)
                p.dma("gpsimd", s0_out[b, e * 128:(e + 1) * 128, tsl], s0t)
    p.emit()
    return nc


def run_phaseA(inputs, n_cores=8):
    w = inputs
    cA = host_consts_A()
    wA = wA_prep(w)
    x = np.asarray(w["L0_attn_x"], np.float32).reshape(16 * 2048, 1024)
    nc = build_phaseA()
    in_maps = []
    for c in range(n_cores):
        m = {"x": np.ascontiguousarray(x[c * 4096:(c + 1) * 4096])}
        m.update({k: v for k, v in wA.items()})
        m.update({k: v for k, v in cA.items()})
        in_maps.append(m)
    res = run_bass_kernel_spmd(nc, in_maps, core_ids=list(range(n_cores)))
    return res.results



def hostB_consts():
    tok = np.arange(32768, dtype=np.float32).reshape(128, 256)
    return {"iota_tokB": np.ascontiguousarray(tok),
            "iota_pB": np.arange(128, dtype=np.float32).reshape(128, 1),
            "identB": np.eye(128, dtype=np.float32)}


def wB_prep(w, e):
    return {
        "W1e": np.ascontiguousarray(np.asarray(w["L0_ffn_W1"][e], np.float32).reshape(8, 128, 4096).transpose(1, 0, 2)),
        "W2e": np.ascontiguousarray(np.asarray(w["L0_ffn_W2"][e], np.float32).reshape(32, 128, 1024).transpose(1, 0, 2)),
        "b1e": np.ascontiguousarray(np.asarray(w["L0_ffn_b1"][e], np.float32).reshape(32, 128).T),
        "b2e": np.asarray(w["L0_ffn_b2"][e], np.float32).reshape(1, 1024),
    }


def build_phaseB():
    nc = bass.Bass("TRN2", target_bir_lowering=False, name="phB")
    N, D, K = 32768, 1024, 8192
    NC = 64
    xn_all = nc.dram_tensor("xn_all", [N, D], BF16, kind="ExternalInput").ap()
    sc_e = nc.dram_tensor("sc_e", [128, 256], F32, kind="ExternalInput").ap()
    W1 = nc.dram_tensor("W1e", [128, 8, 4096], F32, kind="ExternalInput").ap()
    W2 = nc.dram_tensor("W2e", [128, 32, 1024], F32, kind="ExternalInput").ap()
    b1 = nc.dram_tensor("b1e", [128, 32], F32, kind="ExternalInput").ap()
    b2 = nc.dram_tensor("b2e", [1, 1024], F32, kind="ExternalInput").ap()
    iota_tok = nc.dram_tensor("iota_tokB", [128, 256], F32, kind="ExternalInput").ap()
    iota_p = nc.dram_tensor("iota_pB", [128, 1], F32, kind="ExternalInput").ap()
    identd = nc.dram_tensor("identB", [128, 128], F32, kind="ExternalInput").ap()
    Z = nc.dram_tensor("Z", [N, D], BF16, kind="ExternalOutput").ap()
    pair_hbm = nc.dram_tensor("pair_hbm", [K + 128, 2], F32).ap()
    rowb_hbm = nc.dram_tensor("rowb_hbm", [128, 1], F32).ap()

    p = Prog(nc)
    al = nc.alloc_sbuf_tensor
    W1b = al("W1b", [128, 8, 4096], BF16).ap()
    W2b = al("W2b", [128, 32, 1024], BF16).ap()
    wtmp = al("wtmpB", [128, 2048], F32).ap()
    _hbA = None
    s_b1 = al("s_b1", [128, 32], F32).ap()
    s_b2r = al("s_b2r", [1, 1024], F32).ap()
    b2b = al("b2b", [128, 1024], F32).ap()
    s_sc = al("s_sc", [128, 256], F32).ap()
    s_iota = al("s_iota", [128, 256], F32).ap()
    s_id = al("s_idB", [128, 128], F32).ap()
    identbB = al("identbB", [128, 128], BF16).ap()
    ones_rB = al("ones_rB", [1, 128], F32).ap()
    ones_cB = al("ones_cB", [128, 1], F32).ap()
    lo_c = al("lo_c", [128, 1], F32).ap()
    hi_c = al("hi_c", [128, 1], F32).ap()
    th_c = al("th_c", [128, 1], F32).ap()
    cnt_c = al("cnt_c", [128, 1], F32).ap()
    cgt_c = al("cgt_c", [128, 1], F32).ap()
    tmpa_c = al("tmpa_c", [128, 1], F32).ap()
    cmpb = al("cmpb", [128, 256], F32).ap()
    memb = al("memb", [128, 256], F32).ap()
    ws = al("ws", [128, 256], F32).ap()
    rowtot = al("rowtot", [128, 1], F32).ap()
    rowrow = al("rowrow", [1, 128], F32).ap()
    rowcum = al("rowcum", [1, 128], F32).ap()
    excl_c = al("excl_c", [128, 1], F32).ap()
    slots = al("slots", [128, 256], F32).ap()
    slot_i = al("slot_i", [128, 256], I32).ap()
    zero1B = al("zero1B", [1, 1], F32).ap()
    zcolB = al("zcolB", [128, 1], F32).ap()
    ovf_c = al("ovf_c", [128, 1], F32).ap()
    s_iotap = al("s_iotap", [128, 1], F32).ap()
    pairt2 = [al(f"pairt{s}", [128, 2], F32).ap() for s in range(4)]
    idcol2 = [al(f"idcol{s}", [128, 1], I32).ap() for s in range(4)]
    gcol2 = [al(f"gcol{s}", [128, 1], F32).ap() for s in range(4)]
    xg2 = [al(f"xg{s}", [128, 1024], BF16).ap() for s in range(2)]
    xgf2 = al("xgf2", [128, 8, 256], BF16).ap()
    xgf2b = al("xgf2b", [128, 8, 256], BF16).ap()
    hbuf2 = al("hbuf2", [128, 32, 256], BF16).ap()
    ybuf2 = [al(f"ybuf{s}", [128, 1024], BF16).ap() for s in range(4)]
    ytmp = al("ytmp", [128, 512], F32).ap()
    ytmp2 = al("ytmp2", [128, 512], F32).ap()
    ztile = al("ztile", [128, 4, 1024], BF16).ap()
    _hbA = wtmp.bitcast(BF16).rearrange("p (h x) -> p h x", h=16)
    _hbB = ztile.rearrange("p a b -> p (a b)").rearrange("p (h x) -> p h x", h=16)

    def hbuf2b(hc):
        return (_hbA if hc < 16 else _hbB)[:, hc % 16, :]

    psA = nc.alloc_psum_tensor("psA", [128, 512], F32).ap()
    psB = nc.alloc_psum_tensor("psB", [128, 512], F32).ap()
    psC = nc.alloc_psum_tensor("psC", [128, 512], F32).ap()
    psD = nc.alloc_psum_tensor("psD", [128, 512], F32).ap()
    psE = nc.alloc_psum_tensor("psE", [128, 512], F32).ap()
    psF = nc.alloc_psum_tensor("psF", [128, 512], F32).ap()
    psG = nc.alloc_psum_tensor("psG", [128, 512], F32).ap()
    psH = nc.alloc_psum_tensor("psH", [128, 512], F32).ap()

    for c in range(8):
        for hv in range(2):
            p.dma("sync", wtmp, W1[:, c, hv * 2048:(hv + 1) * 2048])
            p.copy(W1b[:, c, hv * 2048:(hv + 1) * 2048], wtmp)
    for c in range(32):
        p.dma("sync", wtmp[:, 0:1024], W2[:, c, :])
        p.copy(W2b[:, c, :], wtmp[:, 0:1024])
    p.dma("sync", s_b1, b1)
    p.dma("sync", s_b2r, b2)
    p.dma("sync", s_sc, sc_e)
    p.dma("sync", s_iota, iota_tok)
    p.dma("sync", s_iotap, iota_p)
    p.dma("sync", s_id, identd)
    p.copy(identbB, s_id)
    p.memset(ones_rB, 1.0)
    p.memset(ones_cB, 1.0)
    p.memset(zero1B, 0.0)
    p.memset(zcolB, 0.0)
    p.mm(psA[:, 0:1024].rearrange("p q -> p q") if False else psA, ones_rB, s_b2r[0:1, 0:512], tag="b2b0")
    p.copy(b2b[:, 0:512], psA)
    p.mm(psA, ones_rB, s_b2r[0:1, 512:1024], tag="b2b1")
    p.copy(b2b[:, 512:1024], psA)
    p.memset(ztile, 0.0)
    for t in range(64):
        p.dma("auto", Z[t * 512:(t + 1) * 512, :].rearrange("(j p) q -> p j q", p=128), ztile)

    p.fence_all()
    p.memset(lo_c, 0.0)
    p.memset(hi_c, 1.0)
    p.memset(th_c, 0.5)
    for it in range(40):
        p.ts(cmpb, s_sc, th_c[:, 0:1], None, ALU.is_gt)
        p.red(rowtot, cmpb, ALU.add)
        p.mm(psB[0:1, 0:1], rowtot, ones_cB, tag="cnt")
        p.mm(psC[:, 0:1], ones_rB, psB[0:1, 0:1], tag="cntb") if False else None
        p.copy(rowrow[0:1, 0:1], psB[0:1, 0:1])
        p.mm(psC[:, 0:1], ones_rB, rowrow[0:1, 0:1], tag="cntb")
        p.copy(cnt_c, psC[:, 0:1])
        p.ts(cgt_c, cnt_c, float(K), None, ALU.is_ge)
        p.tt(tmpa_c, th_c, lo_c, ALU.subtract)
        p.stt(lo_c, tmpa_c, cgt_c[:, 0:1], lo_c, ALU.mult, ALU.add)
        p.tt(tmpa_c, hi_c, th_c, ALU.subtract)
        p.stt(hi_c, tmpa_c, cgt_c[:, 0:1], th_c, ALU.mult, ALU.add)
        p.tt(th_c, lo_c, hi_c, ALU.add)
        p.ts(th_c, th_c, 0.5, None, ALU.mult)
    p.ts(memb, s_sc, lo_c[:, 0:1], None, ALU.is_gt)
    p.scan(ws, memb, zcolB.broadcast_to((128, 256)), 0.0, ALU.add, ALU.add)
    p.red(rowtot, memb, ALU.add)
    p.tr(psD[0:1, 0:128], rowtot, s_id, tag="c2rB")
    p.copy(rowrow, psD[0:1, 0:128], eng="scalar")
    p.scan(rowcum, rowrow, zero1B.broadcast_to((1, 128)), 0.0, ALU.add, ALU.add)
    p.tt(rowcum, rowcum, rowrow, ALU.subtract)
    p.dma("gpsimd", rowb_hbm.rearrange("s one -> one s"), rowcum)
    p.dma("sync", excl_c, rowb_hbm)
    p.ts(slots, ws, excl_c[:, 0:1], -1.0, ALU.add, ALU.add)
    p.ts(ovf_c, s_iotap, float(K), None, ALU.add)
    p.tt(cmpb, slots, ovf_c.broadcast_to((128, 256)), ALU.subtract)
    p.tt(cmpb, cmpb, memb, ALU.mult)
    p.tt(slots, cmpb, ovf_c.broadcast_to((128, 256)), ALU.add)
    p.copy(slot_i, slots)
    for f in range(256):
        pt = pairt2[f % 2]
        p.copy(pt[:, 0:1], s_iota[:, f:f + 1])
        p.copy(pt[:, 1:2], s_sc[:, f:f + 1])
        p.scatter(pair_hbm, pt, slot_i[:, f:f + 1], tag="scpair")
    for op in p.ops:
        if op.tag == "scpair":
            op.dma = True
    p.fence_all()
    for pr in range(NC // 2):
        par = pr % 2
        xgf_p = xgf2 if par == 0 else xgf2b
        pass
        ptr = psA if par == 0 else psF
        w1banks = (psB, psD) if par == 0 else (psG, psH)
        for s in range(2):
            ch = pr * 2 + s
            q = par * 2 + s
            p.dma("sync", pairt2[q], pair_hbm[ch * 128:(ch + 1) * 128, :])
            p.copy(idcol2[q], pairt2[q][:, 0:1])
            p.copy(gcol2[q], pairt2[q][:, 1:2])
            p.gather(xg2[s], xn_all, idcol2[q][:, 0:1], tag="gx")
            for c in range(8):
                p.tr(ptr.bitcast(BF16)[:, 0:128], xg2[s][:, c * 128:(c + 1) * 128], identbB)
                p.copy(xgf_p[:, c, s * 128:(s + 1) * 128], ptr.bitcast(BF16)[:, 0:128])
        def hb_p(hc, par=par):
            return hbuf2[:, hc, :] if par == 0 else hbuf2b(hc)
        for hc in range(32):
            pb = w1banks[hc % 2]
            for c in range(8):
                p.mm(pb[:, 0:256], W1b[:, c, hc * 128:(hc + 1) * 128], xgf_p[:, c, :],
                     start=(c == 0), stop=(c == 7))
            p.act(hb_p(hc), pb[:, 0:256], AF.Gelu, bias=s_b1[:, hc:hc + 1])
        for s in range(2):
            q = par * 2 + s
            for half in range(2):
                pc = psC if half == 0 else psE
                yt = ytmp if half == 0 else ytmp2
                for hc in range(32):
                    p.mm(pc, hb_p(hc)[:, s * 128:(s + 1) * 128],
                         W2b[:, hc, half * 512:(half + 1) * 512],
                         start=(hc == 0), stop=(hc == 31))
                p.tt(yt, pc, b2b[:, half * 512:(half + 1) * 512], ALU.add)
                p.ts(ybuf2[q][:, half * 512:(half + 1) * 512], yt, gcol2[q][:, 0:1], None, ALU.mult)
            p.scatter(Z, ybuf2[q], idcol2[q][:, 0:1], tag="scz")
    p.emit()
    return nc


def run_phaseB(inputs, xn_all, scores_all, n_cores=8):
    w = inputs
    cB = hostB_consts()
    nc = build_phaseB()
    in_maps = []
    for e in range(n_cores):
        m = dict(cB)
        m.update(wB_prep(w, e))
        m["xn_all"] = xn_all
        m["sc_e"] = np.ascontiguousarray(scores_all[:, e].reshape(128, 256))
        in_maps.append(m)
    res = run_bass_kernel_spmd(nc, in_maps, core_ids=list(range(n_cores)))
    return res.results



def wC_prep(w):
    Wq = np.asarray(w["L1_attn_Wqkv"], np.float32)[:, 0:1024]
    return {
        "wqkv_q": np.ascontiguousarray(Wq.reshape(8, 128, 1024).transpose(1, 0, 2)),
        "bq_c": np.ascontiguousarray(np.asarray(w["L1_attn_bqkv"], np.float32)[0:1024].reshape(8, 128).T),
        "wlif": np.asarray(w["L1_attn_Wlif"], np.float32),
        "blif_r": np.asarray(w["L1_attn_blif"], np.float32).reshape(1, 16),
    }


def build_phaseC():
    nc = bass.Bass("TRN2", target_bir_lowering=False, name="phC")
    B, S, D = 2, 2048, 1024
    x = nc.dram_tensor("x", [B * S, D], F32, kind="ExternalInput").ap()
    s0 = nc.dram_tensor("s0_fm", [B, D, S], F32, kind="ExternalInput").ap()
    Zp = nc.dram_tensor("Zp", [8, B * S, D], BF16, kind="ExternalInput").ap()
    wq = nc.dram_tensor("wqkv_q", [128, 8, 1024], F32, kind="ExternalInput").ap()
    bq = nc.dram_tensor("bq_c", [128, 8], F32, kind="ExternalInput").ap()
    wlif = nc.dram_tensor("wlif", [16, 64], F32, kind="ExternalInput").ap()
    blif = nc.dram_tensor("blif_r", [1, 16], F32, kind="ExternalInput").ap()
    identd = nc.dram_tensor("identC", [128, 128], F32, kind="ExternalInput").ap()
    x1_out = nc.dram_tensor("x1_fm", [B, D, S], F32, kind="ExternalOutput").ap()
    xn1_out = nc.dram_tensor("xn1_fm", [B, D, S], BF16, kind="ExternalOutput").ap()
    xn1f_out = nc.dram_tensor("xn1f_fm", [B, D, S], F32, kind="ExternalOutput").ap()
    spk_out = nc.dram_tensor("spikes", [32, S], F32, kind="ExternalOutput").ap()
    I_hbm = nc.dram_tensor("I_hbm", [32, S], F32).ap()

    p = Prog(nc)
    al = nc.alloc_sbuf_tensor
    s_id = al("s_idC", [128, 128], F32).ap()
    wqb = al("wqbC", [128, 8, 1024], BF16).ap()
    wtmp = al("wtmpC", [128, 1024], F32).ap()
    s_bq = al("s_bq", [128, 8], F32).ap()
    s_wlif = al("s_wlif", [16, 64], F32).ap()
    wlifT2 = al("wlifT2", [128, 16], F32).ap()
    s_blif = al("s_blif", [1, 16], F32).ap()
    eps5C = al("eps5C", [128, 1], F32).ap()
    xt = al("xtC", [128, 1024], F32).ap()
    zt = al("ztC", [128, 1024], BF16).ap()
    acc = al("accC", [128, 1024], F32).ap()
    g0 = al("g0C", [128, 1024], F32).ap()
    s0tm = al("s0tm", [128, 1024], F32).ap()
    x1t = al("x1t", [128, 1024], F32).ap()
    xn1t = al("xn1t", [128, 1024], F32).ap()
    fmt = al("fmtC", [128, 1024], F32).ap()
    fmb = al("fmbC", [128, 1024], BF16).ap()
    fmt2 = al("fmt2C", [128, 1024], F32).ap()
    st1 = al("st1C", [128, 1], F32).ap()
    st2 = al("st2C", [128, 1], F32).ap()
    xnf = al("xnfC", [128, 8, 512], BF16).ap()
    qe = al("qeC", [128, 512], F32).ap()
    irow = al("irowC", [1, 512], F32).ap()
    Iall = al("IallC", [32, 2048], F32).ap()
    Mtraj = al("MtrajC", [32, 2048], F32).ap()
    mcol = al("mcolC", [32, 1], F32).ap()
    spkt = al("spktC", [32, 2048], F32).ap()

    psA = nc.alloc_psum_tensor("psA", [128, 512], F32).ap()
    psB = nc.alloc_psum_tensor("psB", [128, 512], F32).ap()
    psE = nc.alloc_psum_tensor("psE", [128, 512], F32).ap()

    p.dma("sync", s_id, identd)
    for c in range(8):
        p.dma("sync", wtmp, wq[:, c, :])
        p.copy(wqb[:, c, :], wtmp)
    p.dma("sync", s_bq, bq)
    p.dma("sync", s_wlif, wlif)
    p.dma("sync", s_blif, blif)
    p.memset(eps5C, 1e-5)
    p.tr(psE[0:64, 0:16], s_wlif, s_id[0:16, 0:16], tag="trwlif")
    p.copy(wlifT2[0:64, :], psE[0:64, 0:16])
    p.copy(wlifT2[64:128, :], psE[0:64, 0:16])

    NT = 16
    for b in range(B):
        for t in range(NT):
            row0 = b * S + t * 128
            p.dma("sync", xt, x[row0:row0 + 128, :])
            p.dma("sync", zt, Zp[0, row0:row0 + 128, :])
            p.copy(acc, zt)
            for e in range(1, 8):
                p.dma("sync", zt, Zp[e, row0:row0 + 128, :])
                p.tt(acc, acc, zt, ALU.add)
            p.act(g0, acc, AF.Silu)
            p.dma("sync", fmt.rearrange("p (c q) -> p c q", c=8),
                  s0[b].rearrange("(c p) s -> p c s", p=128)[:, :, t * 128:(t + 1) * 128])
            for c in range(8):
                p.tr(psA[:, 0:128], fmt[:, c * 128:(c + 1) * 128], s_id, tag="trs0")
                p.copy(s0tm[:, c * 128:(c + 1) * 128], psA[:, 0:128])
            p.tt(x1t, s0tm, g0, ALU.mult)
            p.tt(x1t, x1t, xt, ALU.add)
            p.act(fmt, x1t, AF.Identity, accum_out=st1)
            p.ts(st1, st1, 1.0 / D, None, ALU.mult)
            p.ts(xn1t, x1t, st1[:, 0:1], None, ALU.subtract)
            p.act(fmt, xn1t, AF.Square, accum_out=st2)
            p.ts(st2, st2, 1.0 / D, None, ALU.mult)
            p.act(st2, st2, AF.Ln, bias=eps5C[:, 0:1])
            p.act(st2, st2, AF.Exp, scale=-0.5)
            p.ts(xn1t, xn1t, st2[:, 0:1], None, ALU.mult)
            for c in range(8):
                p.tr(psA[:, 0:128], x1t[:, c * 128:(c + 1) * 128], s_id)
                p.copy(fmt[:, c * 128:(c + 1) * 128], psA[:, 0:128])
                p.tr(psB[:, 0:128], xn1t[:, c * 128:(c + 1) * 128], s_id)
                p.copy(fmb[:, c * 128:(c + 1) * 128], psB[:, 0:128])
                p.copy(fmt2[:, c * 128:(c + 1) * 128], psB[:, 0:128])
            p.dma("gpsimd", x1_out[b].rearrange("(c p) s -> p c s", p=128)[:, :, t * 128:(t + 1) * 128],
                  fmt.rearrange("p (c q) -> p c q", c=8))
            p.dma("gpsimd", xn1_out[b].rearrange("(c p) s -> p c s", p=128)[:, :, t * 128:(t + 1) * 128],
                  fmb.rearrange("p (c q) -> p c q", c=8))
            p.dma("gpsimd", xn1f_out[b].rearrange("(c p) s -> p c s", p=128)[:, :, t * 128:(t + 1) * 128],
                  fmt2.rearrange("p (c q) -> p c q", c=8))
    p.fence_all()
    for b in range(B):
        for tc in range(4):
            tsl = slice(tc * 512, (tc + 1) * 512)
            p.dma("sync", xnf, xn1_out[b].rearrange("(c p) s -> p c s", p=128)[:, :, tsl])
            for e in range(8):
                for c in range(8):
                    p.mm(psA, wqb[:, c, e * 128:(e + 1) * 128], xnf[:, c, :],
                         start=(c == 0), stop=(c == 7))
                p.ts(qe, psA, s_bq[:, e:e + 1], None, ALU.add)
                for hh in range(2):
                    h = 2 * e + hh
                    hp = 64 * hh
                    p.mm(psB[0:1, :], wlifT2[hp:hp + 64, h:h + 1], qe[hp:hp + 64, :], tag="iq")
                    p.ts(irow, psB[0:1, :], s_blif[0:1, h:h + 1], None, ALU.add)
                    p.dma("gpsimd", I_hbm[b * 16 + h:b * 16 + h + 1, tsl], irow)
    p.fence_all()
    p.dma("sync", Iall, I_hbm)
    p.memset(mcol, 0.0)
    for t in range(2048):
        p.stt(Mtraj[:, t:t + 1], mcol, 0.9, Iall[:, t:t + 1], ALU.mult, ALU.add)
        p.stt(mcol, Mtraj[:, t:t + 1], 1.0, Mtraj[:, t:t + 1], ALU.is_lt, ALU.mult)
    p.ts(spkt, Mtraj, 1.0, None, ALU.is_ge)
    p.dma("gpsimd", spk_out, spkt)
    p.emit()
    return nc


def run_phaseC(inputs, x_shards, s0_shards, Zall, n_cores=8):
    w = inputs
    wC = wC_prep(w)
    nc = build_phaseC()
    in_maps = []
    for c in range(n_cores):
        m = dict(wC)
        m["identC"] = np.eye(128, dtype=np.float32)
        m["x"] = x_shards[c]
        m["s0_fm"] = s0_shards[c]
        m["Zp"] = np.ascontiguousarray(Zall[:, c * 4096:(c + 1) * 4096, :])
        in_maps.append(m)
    res = run_bass_kernel_spmd(nc, in_maps, core_ids=list(range(n_cores)))
    return res.results



def hostC2_consts():
    cos, sin, rsc = _rot_tables()
    i = np.arange(128)
    return {
        "cos_rsc": np.ascontiguousarray((cos * rsc).astype(np.float32)),
        "sin_rsc": np.ascontiguousarray((sin * rsc).astype(np.float32)),
        "cos_div": np.ascontiguousarray((cos / rsc).astype(np.float32)),
        "sin_div": np.ascontiguousarray((sin / rsc).astype(np.float32)),
        "identC2": np.eye(128, dtype=np.float32),
        "causal_addC2": np.where(i[None, :] <= i[:, None], 0.0, NEG).astype(np.float32),
        "causalT_addC2": np.where(i[:, None] <= i[None, :], 0.0, NEG).astype(np.float32),
        "cam512C2": np.concatenate(
            [np.where(i[None, :] <= i[:, None], 0.0, NEG).astype(np.float32),
             np.zeros((128, 384), np.float32)], axis=1),
        "cat4C2": np.tile(np.where(i[:, None] <= i[None, :], 0.0, NEG).astype(np.float32), (1, 4)),
        "catzC2": np.concatenate(
            [np.where(i[:, None] <= i[None, :], 0.0, NEG).astype(np.float32),
             np.zeros((128, 384), np.float32)], axis=1),
        "zcaC2": np.concatenate(
            [np.zeros((128, 384), np.float32),
             np.where(i[None, :] <= i[:, None], 0.0, NEG).astype(np.float32)], axis=1),
    }


def _rot_tables(n=2048, DH=64, WINDOW=512):
    half = np.arange(0, DH, 2, dtype=np.float32)
    inv_freq = 1.0 / 10000.0 ** (half / DH)
    t = np.arange(n, dtype=np.float32)
    freqs = t[:, None] * inv_freq[None, :]
    freqs = np.concatenate([freqs, freqs], axis=-1)
    base = (half + 0.4 * DH) / (1.4 * DH)
    power = (t - n // 2) / float(WINDOW // 2)
    scale = base[None, :] ** power[:, None]
    scale = np.concatenate([scale, scale], axis=-1).astype(np.float32)
    return np.cos(freqs).astype(np.float32), np.sin(freqs).astype(np.float32), scale


def wC2_prep(w):
    d = {}
    for l in range(2):
        d[f"wqkv1_{l}"] = np.ascontiguousarray(np.asarray(w["L1_ffn_wqkv"][l], np.float32).reshape(8, 128, 1536).transpose(1, 0, 2))
        d[f"wo1_{l}"] = np.ascontiguousarray(np.asarray(w["L1_ffn_wo"][l], np.float32).reshape(4, 128, 1024).transpose(1, 0, 2))
        w1 = np.asarray(w["L1_ffn_w1"][l], np.float32)
        w1a = np.zeros((1024, 2816), np.float32); w1a[:, :2730] = w1[:, :2730]
        w1g = np.zeros((1024, 2816), np.float32); w1g[:, :2730] = w1[:, 2730:]
        d[f"w1a_{l}"] = np.ascontiguousarray(w1a.reshape(8, 128, 2816).transpose(1, 0, 2))
        d[f"w1g_{l}"] = np.ascontiguousarray(w1g.reshape(8, 128, 2816).transpose(1, 0, 2))
        w2 = np.zeros((2816, 1024), np.float32); w2[:2730] = np.asarray(w["L1_ffn_w2"][l], np.float32)
        d[f"w2_{l}"] = np.ascontiguousarray(w2.reshape(22, 128, 1024).transpose(1, 0, 2))
        b1 = np.asarray(w["L1_ffn_b1"][l], np.float32)
        b1a = np.zeros(2816, np.float32); b1a[:2730] = b1[:2730]
        b1g = np.zeros(2816, np.float32); b1g[:2730] = b1[2730:]
        d[f"b1a_{l}"] = np.ascontiguousarray(b1a.reshape(22, 128).T)
        d[f"b1g_{l}"] = np.ascontiguousarray(b1g.reshape(22, 128).T)
        d[f"b2_{l}"] = np.ascontiguousarray(np.asarray(w["L1_ffn_b2"][l], np.float32).reshape(8, 128).T)
        d[f"lg_{l}"] = np.ascontiguousarray(np.asarray(w["L1_ffn_ln_g"][l], np.float32).reshape(8, 128).T)
        d[f"lb_{l}"] = np.ascontiguousarray(np.asarray(w["L1_ffn_ln_b"][l], np.float32).reshape(8, 128).T)
        d[f"qs_{l}"] = np.asarray(w["L1_ffn_q_scale"][l], np.float32).reshape(1, 64)
        d[f"ks_{l}"] = np.asarray(w["L1_ffn_k_scale"][l], np.float32).reshape(1, 64)
    return d


def build_phaseC2():
    nc = bass.Bass("TRN2", target_bir_lowering=False, name="phC2")
    B, S, D = 2, 2048, 1024
    INNER, H = 512, 8
    xn1f = nc.dram_tensor("xn1f_fm", [B, D, S], F32, kind="ExternalInput").ap()
    ins = {}
    for l in range(2):
        for nm, shp in ((f"wqkv1_{l}", [128, 8, 1536]), (f"wo1_{l}", [128, 4, 1024]),
                        (f"w1a_{l}", [128, 8, 2816]), (f"w1g_{l}", [128, 8, 2816]),
                        (f"w2_{l}", [128, 22, 1024]), (f"b1a_{l}", [128, 22]),
                        (f"b1g_{l}", [128, 22]), (f"b2_{l}", [128, 8]),
                        (f"lg_{l}", [128, 8]), (f"lb_{l}", [128, 8]),
                        (f"qs_{l}", [1, 64]), (f"ks_{l}", [1, 64])):
            ins[nm] = nc.dram_tensor(nm, shp, F32, kind="ExternalInput").ap()
    for nm, shp in (("cos_rsc", [S, 64]), ("sin_rsc", [S, 64]), ("cos_div", [S, 64]),
                    ("sin_div", [S, 64]), ("identC2", [128, 128]),
                    ("causal_addC2", [128, 128]), ("causalT_addC2", [128, 128]),
                    ("cam512C2", [128, 512]), ("cat4C2", [128, 512]),
                    ("catzC2", [128, 512]), ("zcaC2", [128, 512])):
        ins[nm] = nc.dram_tensor(nm, shp, F32, kind="ExternalInput").ap()
    g1_out = nc.dram_tensor("g1_fm", [B, D, S], BF16, kind="ExternalOutput").ap()
    xfa = nc.dram_tensor("xfa", [B, D, S], F32).ap()
    xfb = nc.dram_tensor("xfb", [B, D, S], F32).ap()
    sh_hbm = nc.dram_tensor("sh_hbm", [B, D, S], BF16).ap()
    shf_hbm = nc.dram_tensor("shf_hbm", [B, D, S], F32).ap()
    qf_hbm = nc.dram_tensor("qf_hbm", [B, INNER, S], BF16).ap()
    kf_hbm = nc.dram_tensor("kf_hbm", [B, INNER, S], BF16).ap()
    vf_hbm = nc.dram_tensor("vf_hbm", [B, S, INNER], BF16).ap()
    tab_hbm = nc.dram_tensor("tab_hbm", [4, S, 64], F32).ap()
    o_hbm = nc.dram_tensor("o_hbm", [INNER, S], BF16).ap()

    p = Prog(nc)
    al = nc.alloc_sbuf_tensor
    warena = al("warena", [128, 67584], BF16).ap()
    s_id = al("s_idC2", [128, 128], F32).ap()
    identb = al("identbC2", [128, 128], BF16).ap()
    s_ca = al("s_caC2", [128, 128], F32).ap()
    s_cat = al("s_catC2", [128, 128], F32).ap()
    s_cam = al("s_camC2", [128, 512], F32).ap()
    s_cat4 = al("s_cat4C2", [128, 512], F32).ap()
    s_catz = al("s_catzC2", [128, 512], F32).ap()
    s_zca = al("s_zcaC2", [128, 512], F32).ap()
    wtmp = None
    ones_r = al("ones_rC2", [1, 128], F32).ap()
    ones_c = al("ones_cC2", [128, 1], F32).ap()
    eps5 = al("eps5C2", [128, 1], F32).ap()
    qsb = al("qsbC2", [128, 64], F32).ap()
    ksb = al("ksbC2", [128, 64], F32).ap()
    rotq = al("rotqC2", [128, 64], F32).ap()
    rotk = al("rotkC2", [128, 64], F32).ap()
    zc = al("zcC2", [128, 4], BF16).ap()
    zcf = al("zcfC2", [128, 4], F32).ap()
    s_small = {}
    for l in range(2):
        for nm in (f"b1a_{l}", f"b1g_{l}"):
            s_small[nm] = al("s_" + nm, [128, 22], F32).ap()
        for nm in (f"b2_{l}", f"lg_{l}", f"lb_{l}"):
            s_small[nm] = al("s_" + nm, [128, 8], F32).ap()
        for nm in (f"qs_{l}", f"ks_{l}"):
            s_small[nm] = al("s_" + nm, [1, 64], F32).ap()
    shj = None
    sarena = al("sarenaC2", [128, 8704], F32).ap()
    tabraw = sarena[:, 0:1024].rearrange("p (t d) -> p t d", t=16)
    shj = sarena.bitcast(BF16)[:, 6144:7168].rearrange("p (c q) -> p c q", c=8)
    wtmp = sarena[:, 7040:8448]
    qkvt = sarena[:, 0:1536]
    rotb = sarena[:, 1536:2048]
    tmpa = sarena[:, 2048:2560]
    tmpb = sarena[:, 2560:3072]
    ssb = al("ssbC2", [128, 8], F32).ap()
    nrm = al("nrmC2", [128, 8], F32).ap()
    qout = al("qoutC2", [128, 512], BF16).ap()
    qfm_c = al("qfm_cC2", [128, 2048], BF16).ap()
    kfm_c = al("kfm_cC2", [128, 2048], BF16).ap()
    o4 = al("o4C2", [128, 4, 512], BF16).ap()
    sml = sarena[:, 3072:3584]
    m_a = al("m_aC2", [128, 1], F32).ap()
    m_b = al("m_bC2", [128, 1], F32).ap()
    m_col = al("m_colC2", [128, 4], F32).ap()
    rowbuf = al("rowbufC2", [1, 1536], F32).ap()
    mneg_r = rowbuf[0:1, 0:512]
    den_r = rowbuf[0:1, 512:1024]
    fac_r = rowbuf[0:1, 1024:1536]
    mb_s = sarena[:, 4096:4608]
    cmb = sarena[:, 3584:4096]
    xcur = al("xcurC2", [128, 512], F32).ap()
    sqt = sarena[:, 4608:5120]
    sbf = sarena.bitcast(BF16)
    glu = sbf[:, 0:11264].rearrange("p (h x) -> p h x", h=22)
    hln = sbf[:, 11264:15360].rearrange("p (h x) -> p h x", h=8)
    abuf = sarena[:, 7680:8192]
    gbuf = sarena[:, 8192:8704]
    mub = qfm_c.bitcast(F32)[:, 0:512]
    rsb = qfm_c.bitcast(F32)[:, 512:1024]
    msq = kfm_c.bitcast(F32)[:, 0:512]
    abuf2 = kfm_c.bitcast(F32)[:, 512:1024]
    gbuf2 = o4.bitcast(F32).rearrange("p a b -> p (a b)")[:, 0:512]
    stat = rowbuf

    psA = nc.alloc_psum_tensor("psA", [128, 512], F32).ap()
    psB = nc.alloc_psum_tensor("psB", [128, 512], F32).ap()
    psC = nc.alloc_psum_tensor("psC", [128, 512], F32).ap()
    psD = nc.alloc_psum_tensor("psD", [128, 512], F32).ap()
    psE = nc.alloc_psum_tensor("psE", [128, 512], F32).ap()
    psF = nc.alloc_psum_tensor("psF", [128, 512], F32).ap()
    psG = nc.alloc_psum_tensor("psG", [128, 512], F32).ap()
    psH = nc.alloc_psum_tensor("psH", [128, 512], F32).ap()

    _dbase = [16384]

    def dal(nb):
        bb = (_dbase[0] + 3) & ~3
        _dbase[0] = bb + nb
        assert _dbase[0] <= 67584, _dbase[0]
        return bb

    wscf = warena.bitcast(F32)

    def dbf(n):
        bb = dal(n)
        return warena[:, bb:bb + n]

    def df32(n):
        bb = dal(2 * n)
        return wscf[:, bb // 2:bb // 2 + n]

    shj2 = [dbf(1024).rearrange("p (c q) -> p c q", c=8) for _ in range(2)]
    qkvt2 = [df32(1536) for _ in range(2)]
    sqt2 = [df32(512) for _ in range(2)]
    rotb2 = [df32(512) for _ in range(2)]
    tmpa2 = [df32(512) for _ in range(2)]
    tmpb2 = [df32(512) for _ in range(2)]
    tab42 = [df32(256).rearrange("p (t d) -> p t d", t=4) for _ in range(2)]
    qout2 = [dbf(512) for _ in range(2)]
    trc42 = [dbf(512).rearrange("p (c q) -> p c q", c=4) for _ in range(2)]
    ssb2 = [df32(8) for _ in range(2)]
    nrm2 = [df32(8) for _ in range(2)]
    qkb2 = [dbf(1024) for _ in range(2)]
    vb2 = [dbf(512) for _ in range(2)]
    vtmh2 = [dbf(1024).rearrange("p (t e) -> p t e", t=16) for _ in range(2)]
    vden2 = [dbf(1040).rearrange("p (t e) -> p t e", t=16) for _ in range(2)]
    sml_h2 = [df32(512) for _ in range(2)]
    cmb_h2 = [df32(512) for _ in range(2)]
    mb_s2 = [df32(512) for _ in range(2)]
    pT_h2 = [dbf(512) for _ in range(2)]
    fac_b2 = [df32(512) for _ in range(2)]
    obuf2 = [dbf(512) for _ in range(2)]
    rowat2 = [df32(1536) for _ in range(2)]
    m_a2 = [df32(4) for _ in range(2)]
    m_b2_ = [df32(4) for _ in range(2)]
    m_col2 = [df32(4) for _ in range(2)]
    xcur2 = [df32(512) for _ in range(2)]
    o4b = dbf(2048).rearrange("p (c s) -> p c s", c=4)

    def zipper(bodyA, bodyB):
        mk = len(p.ops)
        bodyA()
        A = p.ops[mk:]
        del p.ops[mk:]
        bodyB()
        Bl = p.ops[mk:]
        del p.ops[mk:]
        ia = ib = 0
        while ia < len(A) or ib < len(Bl):
            if ia < len(A):
                p.ops.append(A[ia])
                ia += 1
            if ib < len(Bl):
                p.ops.append(Bl[ib])
                ib += 1

    p.dma("sync", s_id, ins["identC2"])
    p.copy(identb, s_id)
    p.dma("sync", s_ca, ins["causal_addC2"])
    p.dma("sync", s_cat, ins["causalT_addC2"])
    p.dma("sync", s_cam, ins["cam512C2"])
    p.dma("sync", s_cat4, ins["cat4C2"])
    p.dma("sync", s_catz, ins["catzC2"])
    p.dma("sync", s_zca, ins["zcaC2"])
    p.memset(ones_r, 1.0)
    p.memset(ones_c, 1.0)
    p.memset(eps5, 1e-5)
    p.memset(zc, 0.0)
    p.memset(zcf, 0.0)
    for l in range(2):
        for nm in (f"b1a_{l}", f"b1g_{l}", f"b2_{l}", f"lg_{l}", f"lb_{l}", f"qs_{l}", f"ks_{l}"):
            p.dma("sync", s_small[nm], ins[nm])

    def wv(off, sub, width):
        return warena[:, off + sub * width:off + (sub + 1) * width]
    OFF_QKV = 0
    OFF_WO = 8 * 1536
    OFF_W1A = 0
    OFF_W1G = 8 * 2816
    OFF_W2 = 16 * 2816

    def shift_copy(dst, srcb, b, cast):
        p.dma("gpsimd" if cast else "sync", dst[b, 0:512, :], srcb[b, 0:512, :])
        p.dma("gpsimd" if cast else "sync", dst[b, 512:1024, 1:2048], srcb[b, 512:1024, 0:2047])
        zz = zc if cast else zcf
        p.dma("gpsimd", dst[b, 512:1024, 0:1].rearrange("(a c) one -> a (c one)", c=4), zz, slow=True)

    for l in range(2):
        src = xn1f if l == 0 else xfb
        for c in range(8):
            for hv in range(2):
                p.dma("sync", wtmp[:, 0:768], ins[f"wqkv1_{l}"][:, c, hv * 768:(hv + 1) * 768])
                p.copy(wv(OFF_QKV, c, 1536)[:, hv * 768:(hv + 1) * 768], wtmp[:, 0:768])
        for c in range(4):
            p.dma("sync", wtmp[:, 0:1024], ins[f"wo1_{l}"][:, c, :])
            p.copy(wv(OFF_WO, c, 1024), wtmp[:, 0:1024])
        p.mm(psA[:, 0:64], ones_r, s_small[f"qs_{l}"], tag="qsb")
        p.copy(qsb, psA[:, 0:64])
        p.mm(psA[:, 0:64], ones_r, s_small[f"ks_{l}"], tag="ksb")
        p.copy(ksb, psA[:, 0:64])
        p.copy(rotq[:, 0:32], qsb[:, 32:64])
        p.copy(rotq[:, 32:64], qsb[:, 0:32])
        p.copy(rotk[:, 0:32], ksb[:, 32:64])
        p.copy(rotk[:, 32:64], ksb[:, 0:32])
        for tbl, cstn, mulb in ((0, "cos_rsc", qsb), (1, "sin_rsc", rotq),
                                (2, "cos_div", ksb), (3, "sin_div", rotk)):
            p.dma("sync", tabraw, ins[cstn].rearrange("(t p) d -> p t d", p=128))
            for tt_ in range(16):
                p.tt(tabraw[:, tt_, :], tabraw[:, tt_, :], mulb, ALU.mult)
            p.dma("gpsimd", tab_hbm[tbl].rearrange("(t p) d -> p t d", p=128), tabraw)
        for b in range(B):
            shift_copy(sh_hbm, src, b, cast=True)

            def qkv_body(t):
                r = t % 2
                shj_ = shj2[r]
                tab4_ = tab42[r]
                qkvt_ = qkvt2[r]
                sqt_ = sqt2[r]
                ssb_ = ssb2[r]
                nrm_ = nrm2[r]
                rotb_ = rotb2[r]
                tmpa_ = tmpa2[r]
                tmpb_ = tmpb2[r]
                qkb_ = qkb2[r]
                vb_ = vb2[r]
                trc4_ = trc42[r]
                ps3 = ((psA, psB, psC), (psD, psE, psG))[r]
                p.dma("sync", shj_, sh_hbm[b].rearrange("(c p) s -> p c s", p=128)[:, :, t * 128:(t + 1) * 128])
                p.dma("scalar", tab4_, tab_hbm[:, t * 128:(t + 1) * 128, :].rearrange("tb p d -> p tb d"))
                for third in range(3):
                    ps = ps3[third]
                    for c in range(8):
                        p.mm(ps, shj_[:, c, :],
                             wv(OFF_QKV, c, 1536)[:, third * 512:(third + 1) * 512],
                             start=(c == 0), stop=(c == 7))
                p.copy(qkvt_[:, 0:512], ps3[0], eng="scalar")
                p.copy(qkvt_[:, 512:1024], ps3[1], eng="scalar")
                p.copy(vb_, ps3[2])
                p.dma("gpsimd", vf_hbm[b, t * 128:(t + 1) * 128, :], vb_)
                for base, scb, t1, t2 in ((0, qsb, 0, 1), (512, ksb, 2, 3)):
                    seg = qkvt_[:, base:base + 512]
                    segv = seg.rearrange("p (h x) -> p h x", h=8)
                    p.act(sqt_, seg, AF.Square)
                    p.red(ssb_, sqt_.rearrange("p (h x) -> p h x", h=8), ALU.add)
                    p.act(nrm_, ssb_, AF.Sqrt)
                    p.ts(nrm_, nrm_, 1e-12, None, ALU.max)
                    p.recip(nrm_, nrm_)
                    for h in range(8):
                        if h % 2 == 0:
                            p.ts(seg[:, h * 64:(h + 1) * 64], seg[:, h * 64:(h + 1) * 64],
                                 nrm_[:, h:h + 1], None, ALU.mult, eng="gpsimd")
                        else:
                            p.act(seg[:, h * 64:(h + 1) * 64], seg[:, h * 64:(h + 1) * 64],
                                  AF.Identity, scale=nrm_[:, h:h + 1])
                    rotv = rotb_.rearrange("p (h x) -> p h x", h=8)
                    p.ts(rotv[:, :, 0:32], segv[:, :, 32:64], -1.0, None, ALU.mult, eng="gpsimd")
                    p.copy(rotv[:, :, 32:64], segv[:, :, 0:32], eng="gpsimd")
                    tb1 = tab4_[:, t1, :].unsqueeze(1).broadcast_to((128, 8, 64))
                    tb2 = tab4_[:, t2, :].unsqueeze(1).broadcast_to((128, 8, 64))
                    p.tt(tmpa_.rearrange("p (h x) -> p h x", h=8), segv, tb1, ALU.mult)
                    p.tt(tmpb_.rearrange("p (h x) -> p h x", h=8), rotv, tb2, ALU.mult, eng="gpsimd")
                    p.tt(qkb_[:, base:base + 512], tmpa_, tmpb_, ALU.add)
                for base, dsth in ((0, qf_hbm), (512, kf_hbm)):
                    for c in range(4):
                        pcol = r * 512 + c * 128
                        p.tr(psH.bitcast(BF16)[:, pcol:pcol + 128],
                             qkb_[:, base + c * 128:base + (c + 1) * 128], identb)
                        p.copy(trc4_[:, c, :], psH.bitcast(BF16)[:, pcol:pcol + 128])
                    p.dma("gpsimd", dsth[b, 0:512, t * 128:(t + 1) * 128].rearrange("(c p) q -> p c q", p=128),
                          trc4_)

            zipper(lambda: [qkv_body(t) for t in range(0, 16, 2)],
                   lambda: [qkv_body(t) for t in range(1, 16, 2)])

            for r in range(2):
                for kt in range(16):
                    p.memset(vden2[r][:, kt, 64:65], 1.0)

            def att_body(hc, hh):
                h = 2 * hc + hh
                hp = 64 * hh
                pstats = (psA, psD)[hh]
                pst = (psB, psE)[hh]
                pav = (psC, psG)[hh]
                pbc = (psF, psH)[hh]
                vtmh_ = vtmh2[hh]
                vden_ = vden2[hh]
                sml_ = sml_h2[hh]
                cmb_ = cmb_h2[hh]
                mb_s_ = mb_s2[hh]
                pT_ = pT_h2[hh]
                fac_b_ = fac_b2[hh]
                obuf_ = obuf2[hh]
                m_a4 = m_a2[hh]
                m_d4 = m_b2_[hh]
                m_col = m_col2[hh]
                mneg_r_ = rowat2[hh][0:1, 0:512]
                den_r_ = rowat2[hh][0:1, 512:1024]
                fac_r_ = rowat2[hh][0:1, 1024:1536]
                p.dma("sync", vtmh_, vf_hbm[b].rearrange("(t p) e -> p t e", p=128)[:, :, h * 64:(h + 1) * 64])
                for kt in range(16):
                    p.copy(vden_[:, kt, 0:64], vtmh_[:, kt, :], eng="gpsimd")
                for w in range(4):
                    for j in range(4):
                        q0 = (w * 4 + j) * 128
                        p.mm(pav[:, j * 128:(j + 1) * 128], qfm_c[hp:hp + 64, q0:q0 + 128],
                             kfm_c[hp:hp + 64, q0:q0 + 128])
                    if w == 0:
                        p.memset(m_a4[:, 0:1], NEG)
                        for j in range(1, 4):
                            q0 = j * 128
                            p.mm(pstats[:, 0:q0], qfm_c[hp:hp + 64, q0:q0 + 128],
                                 kfm_c[hp:hp + 64, 0:q0])
                            p.red(m_a4[:, j:j + 1], pstats[:, 0:q0], ALU.max)
                    else:
                        for j in range(4):
                            q0 = (w * 4 + j) * 128
                            p.mm(pstats, qfm_c[hp:hp + 64, q0:q0 + 128],
                                 kfm_c[hp:hp + 64, q0 - 512:q0])
                            p.stt(sml_, pstats, 1.0, s_cam, ALU.mult, ALU.add)
                            p.red(m_a4[:, j:j + 1], sml_, ALU.max)
                    p.stt(sml_, pav, 1.0, s_cat4, ALU.mult, ALU.add)
                    p.red(m_d4, sml_.rearrange("p (j x) -> p j x", j=4), ALU.max)
                    p.tt(m_col, m_a4, m_d4, ALU.max)
                    p.ts(m_col, m_col, 8.0, None, ALU.mult)
                    for j in range(4):
                        p.tr(pav[0:1, j * 128:(j + 1) * 128], m_col[:, j:j + 1], s_id, tag="c2rm2")
                        p.ts(mneg_r_[0:1, j * 128:(j + 1) * 128], pav[0:1, j * 128:(j + 1) * 128],
                             -1.0, None, ALU.mult, eng="scalar")
                    p.mm(pbc, ones_r, mneg_r_, tag="bcm2")
                    p.copy(mb_s_, pbc, eng="scalar")
                    qa = w * 512
                    kts = []
                    for kt in range(max(0, w * 4 - 4), w * 4 + 4):
                        qlo = max(qa, kt * 128)
                        qhi = min(qa + 512, kt * 128 + 128 + 512)
                        if qlo < qhi:
                            kts.append((kt, qlo, qhi))
                    kts.sort(key=lambda z: (z[1] - z[2]))
                    for ii, (kt, qlo, qhi) in enumerate(kts):
                        nn = qhi - qlo
                        off = qlo - qa
                        p.mm(pst[:, :nn], kfm_c[hp:hp + 64, kt * 128:(kt + 1) * 128],
                             qfm_c[hp:hp + 64, qlo:qlo + nn], tag="mmlT2")
                        if kt >= w * 4:
                            p.tt(cmb_[:, :nn], s_catz[:, 0:nn], mb_s_[:, off:off + nn], ALU.add, eng="gpsimd")
                        else:
                            p.tt(cmb_[:, :nn], s_zca[:, 512 - nn:512], mb_s_[:, off:off + nn], ALU.add, eng="gpsimd")
                        p.stt(sml_[:, :nn], pst[:, :nn], 8.0, cmb_[:, :nn], ALU.mult, ALU.add)
                        p.act(pT_[:, :nn], sml_[:, :nn], AF.Exp)
                        p.mm(pav[0:65, off:off + nn], vden_[:, kt, :], pT_[:, :nn],
                             start=(ii == 0), stop=(ii == len(kts) - 1), tag="avl2")
                    p.copy(den_r_, pav[64:65, 0:512], eng="scalar")
                    p.recip(fac_r_, den_r_)
                    p.mm(pbc, ones_r, fac_r_, tag="bcf2")
                    p.copy(fac_b_[0:64, :], pbc[0:64, :], eng="scalar")
                    p.tt(obuf_[0:64, :], pav[0:64, :], fac_b_[0:64, :], ALU.mult)
                    p.dma("gpsimd", o_hbm[h * 64:(h + 1) * 64, qa:qa + 512], obuf_[0:64, :])

            for hc in range(4):
                p.dma("sync", qfm_c, qf_hbm[b, hc * 128:(hc + 1) * 128, :])
                p.dma("sync", kfm_c, kf_hbm[b, hc * 128:(hc + 1) * 128, :])
                zipper(lambda: att_body(hc, 0), lambda: att_body(hc, 1))
            for tc in range(4):
                tsl = slice(tc * 512, (tc + 1) * 512)
                o4v = o4 if tc % 2 == 0 else o4b
                p.dma("sync", o4v, o_hbm.rearrange("(c p) s -> p c s", p=128)[:, :, tsl])

                def wo_body(es):
                    for e in es:
                        ps = (psA, psD)[e % 2]
                        xc2 = xcur2[e % 2]
                        p.dma("sync", xc2, src[b, e * 128:(e + 1) * 128, tsl])
                        for c in range(4):
                            p.mm(ps, wv(OFF_WO, c, 1024)[:, e * 128:(e + 1) * 128], o4v[:, c, :],
                                 start=(c == 0), stop=(c == 3))
                        p.tt(xc2, xc2, ps, ALU.add)
                        p.dma("gpsimd", xfa[b, e * 128:(e + 1) * 128, tsl], xc2)

                zipper(lambda: wo_body([0, 2, 4, 6]), lambda: wo_body([1, 3, 5, 7]))
        p.fence_all()
        for c in range(8):
            for hv in range(2):
                p.dma("sync", wtmp, ins[f"w1a_{l}"][:, c, hv * 1408:(hv + 1) * 1408])
                p.copy(wv(OFF_W1A, c, 2816)[:, hv * 1408:(hv + 1) * 1408], wtmp)
                p.dma("sync", wtmp, ins[f"w1g_{l}"][:, c, hv * 1408:(hv + 1) * 1408])
                p.copy(wv(OFF_W1G, c, 2816)[:, hv * 1408:(hv + 1) * 1408], wtmp)
        for c in range(22):
            p.dma("sync", wtmp[:, 0:1024], ins[f"w2_{l}"][:, c, :])
            p.copy(wv(OFF_W2, c, 1024), wtmp[:, 0:1024])
        for b in range(B):
            shift_copy(shf_hbm, xfa, b, cast=False)
            for tc in range(4):
                tsl = slice(tc * 512, (tc + 1) * 512)
                xc = xcur[:, 0:512]
                sq = msq
                for c in range(8):
                    p.dma("sync", xc, shf_hbm[b, c * 128:(c + 1) * 128, tsl])
                    p.mm(psA[0:1, 0:512], ones_c, xc, start=(c == 0), stop=(c == 7), tag="mu")
                    p.act(sq, xc, AF.Square)
                    p.mm(psB[0:1, 0:512], ones_c, sq, start=(c == 0), stop=(c == 7), tag="ssq")
                p.ts(stat[0:1, 0:512], psA[0:1, 0:512], 1.0 / D, None, ALU.mult)
                p.ts(stat[0:1, 512:1024], psB[0:1, 0:512], 1.0 / D, None, ALU.mult)
                p.act(stat[0:1, 1024:1536], stat[0:1, 0:512], AF.Square)
                p.tt(stat[0:1, 512:1024], stat[0:1, 512:1024], stat[0:1, 1024:1536], ALU.subtract)
                p.act(stat[0:1, 512:1024], stat[0:1, 512:1024], AF.Ln, bias=eps5[0:1, 0:1])
                p.act(stat[0:1, 512:1024], stat[0:1, 512:1024], AF.Exp, scale=-0.5)
                p.mm(psD[:, 0:512], ones_r, stat[0:1, 0:512], tag="mub")
                p.copy(mub, psD[:, 0:512], eng="scalar")
                p.mm(psD[:, 0:512], ones_r, stat[0:1, 512:1024], tag="rsb")
                p.copy(rsb, psD[:, 0:512], eng="scalar")
                for c in range(8):
                    p.dma("sync", xc, shf_hbm[b, c * 128:(c + 1) * 128, tsl])
                    p.tt(sq, xc, mub, ALU.subtract)
                    p.tt(sq, sq, rsb, ALU.mult)
                    p.ts(hln[:, c, :], sq, s_small[f"lg_{l}"][:, c:c + 1],
                         s_small[f"lb_{l}"][:, c:c + 1], ALU.mult, ALU.add)
                for hc in range(22):
                    pa = psA if hc % 2 == 0 else psE
                    pb = psB if hc % 2 == 0 else psF
                    ab = abuf if hc % 2 == 0 else abuf2
                    gb = gbuf if hc % 2 == 0 else gbuf2
                    for c in range(8):
                        p.mm(pa[:, 0:512], wv(OFF_W1A, c, 2816)[:, hc * 128:(hc + 1) * 128], hln[:, c, :],
                             start=(c == 0), stop=(c == 7))
                    for c in range(8):
                        p.mm(pb[:, 0:512], wv(OFF_W1G, c, 2816)[:, hc * 128:(hc + 1) * 128], hln[:, c, :],
                             start=(c == 0), stop=(c == 7))
                    p.ts(ab, pa[:, 0:512], s_small[f"b1a_{l}"][:, hc:hc + 1], None, ALU.add)
                    p.act(gb, pb[:, 0:512], AF.Gelu, bias=s_small[f"b1g_{l}"][:, hc:hc + 1])
                    p.tt(glu[:, hc, :], ab, gb, ALU.mult)
                for e in range(8):
                    pc = psC if e % 2 == 0 else psD
                    for hc in range(22):
                        p.mm(pc[:, 0:512], wv(OFF_W2, hc, 1024)[:, e * 128:(e + 1) * 128], glu[:, hc, :],
                             start=(hc == 0), stop=(hc == 21))
                    p.ts(sq, pc[:, 0:512], s_small[f"b2_{l}"][:, e:e + 1], None, ALU.add)
                    p.dma("sync", xc, xfa[b, e * 128:(e + 1) * 128, tsl])
                    p.tt(xc, xc, sq, ALU.add)
                    if l == 1:
                        p.act(xc, xc, AF.Silu)
                        p.copy(qout[:, 0:512], xc)
                        p.dma("gpsimd", g1_out[b, e * 128:(e + 1) * 128, tsl], qout[:, 0:512])
                    else:
                        p.dma("gpsimd", xfb[b, e * 128:(e + 1) * 128, tsl], xc)
    p.emit()
    return nc


def run_phaseC2(inputs, xn1f_shards, n_cores=8):
    w = inputs
    wC2 = wC2_prep(w)
    cC2 = hostC2_consts()
    nc = build_phaseC2()
    in_maps = []
    for c in range(n_cores):
        m = dict(wC2)
        m.update(cC2)
        m["xn1f_fm"] = xn1f_shards[c]
        in_maps.append(m)
    res = run_bass_kernel_spmd(nc, in_maps, core_ids=list(range(n_cores)))
    return res.results



def hostD_consts():
    i = np.arange(128)[:, None]
    j = np.arange(128)[None, :]
    gcols = np.array([0, 1, 2, 3] + list(range(64, 2048, 64)), np.int64)
    c = {}
    c["identD"] = np.eye(128, dtype=np.float32)
    def addm(valid):
        return np.where(valid, 0.0, NEG).astype(np.float32)
    k_, q_ = i, j
    same = (k_ <= q_) | ((k_ % 64) == 0)
    mT_same = addm(same)
    mT_same0 = addm(same | (k_ < 4))
    prev = (k_ >= q_) | ((k_ % 64) == 0)
    mT_prev = addm(prev)
    mT_prev1 = addm(prev | (k_ < 4))
    c["mT_same0"] = mT_same0
    c["mT_ps"] = np.ascontiguousarray(np.concatenate([mT_prev, mT_same], axis=1))
    c["mT_ps1"] = np.ascontiguousarray(np.concatenate([mT_prev1, mT_same], axis=1))
    c["m_s0s"] = np.ascontiguousarray(mT_same0.T) * 0.125
    c["m_ps"] = np.ascontiguousarray(np.concatenate([mT_prev.T, mT_same.T], axis=1)) * 0.125
    c["m_ps1"] = np.ascontiguousarray(np.concatenate([mT_prev1.T, mT_same.T], axis=1)) * 0.125
    gT = np.zeros((35, 2048), np.float32)
    for t in range(16):
        lo, hi = 128 * (t - 1), 128 * (t + 1)
        excl = (gcols >= lo) & (gcols < hi)
        gT[excl, t * 128:(t + 1) * 128] = NEG
    c["gT"] = gT
    c["g_rm"] = np.ascontiguousarray(gT.T) * 0.125
    sel = np.zeros((256, 16), np.float32)
    r = np.arange(256)
    sel[r, r % 16] = 1.0
    c["selD"] = sel
    return c


def wD_prep(w):
    Wq = np.asarray(w["L1_attn_Wqkv"], np.float32)
    return {
        "wqkvD": np.ascontiguousarray(Wq.reshape(8, 128, 3072).transpose(1, 0, 2)),
        "bqkD": np.ascontiguousarray(np.asarray(w["L1_attn_bqkv"], np.float32)[0:2048].reshape(16, 128).T),
        "bvD": np.asarray(w["L1_attn_bqkv"], np.float32)[2048:3072].reshape(1, 1024),
        "woutD": np.ascontiguousarray(np.asarray(w["L1_attn_Wout"], np.float32).reshape(8, 128, 1024).transpose(1, 0, 2)),
        "boutD": np.ascontiguousarray(np.asarray(w["L1_attn_bout"], np.float32).reshape(8, 128).T),
    }


def build_phaseD():
    nc = bass.Bass("TRN2", target_bir_lowering=False, name="phD")
    B, S, D = 2, 2048, 1024
    H = 16
    x1 = nc.dram_tensor("x1_fm", [B, D, S], F32, kind="ExternalInput").ap()
    xn1 = nc.dram_tensor("xn1_fm", [B, D, S], BF16, kind="ExternalInput").ap()
    g1 = nc.dram_tensor("g1_fm", [B, D, S], BF16, kind="ExternalInput").ap()
    spk_all = nc.dram_tensor("spk_all", [256, S], F32, kind="ExternalInput").ap()
    wqkv = nc.dram_tensor("wqkvD", [128, 8, 3072], F32, kind="ExternalInput").ap()
    bqk = nc.dram_tensor("bqkD", [128, 16], F32, kind="ExternalInput").ap()
    bv = nc.dram_tensor("bvD", [1, 1024], F32, kind="ExternalInput").ap()
    wout = nc.dram_tensor("woutD", [128, 8, 1024], F32, kind="ExternalInput").ap()
    bout = nc.dram_tensor("boutD", [128, 8], F32, kind="ExternalInput").ap()
    cst = {}
    for nm, shp in (("identD", [128, 128]), ("mT_same0", [128, 128]),
                    ("mT_ps", [128, 256]), ("mT_ps1", [128, 256]),
                    ("m_s0s", [128, 128]), ("m_ps", [128, 256]), ("m_ps1", [128, 256]),
                    ("gT", [35, 2048]), ("g_rm", [2048, 35]), ("selD", [256, 16])):
        cst[nm] = nc.dram_tensor(nm, shp, F32, kind="ExternalInput").ap()
    out = nc.dram_tensor("out", [B * S, D], F32, kind="ExternalOutput").ap()
    qf_hbm = nc.dram_tensor("qfD", [B, D, S], BF16).ap()
    kf_hbm = nc.dram_tensor("kfD", [B, D, S], BF16).ap()
    vt_hbm = nc.dram_tensor("vtD", [B, S, D], BF16).ap()
    vdh2 = [nc.dram_tensor(f"vdenD{r}", [S, 65], BF16).ap() for r in range(2)]

    p = Prog(nc)
    al = nc.alloc_sbuf_tensor
    s_id = al("s_idD", [128, 128], F32).ap()
    identb = al("identbD", [128, 128], BF16).ap()
    s_cst = {}
    for nm in cst:
        if nm in ("identD", "selD", "g_rm"):
            continue
        shp = list(cst[nm].shape)
        s_cst[nm] = al("s_" + nm, shp, F32).ap()
    g_rm_sb = al("s_g_rm", [128, 16, 35], F32).ap()
    wqb = al("wqbD", [128, 8, 3072], BF16).ap()
    wob = al("wobD", [128, 8, 1024], BF16).ap()
    wtmp = al("wtmpD", [128, 1024], F32).ap()
    s_bqk = al("s_bqkD", [128, 16], F32).ap()
    s_bvr = al("s_bvrD", [1, 1024], F32).ap()
    bvb = al("bvbD", [128, 1024], F32).ap()
    s_bout = al("s_boutD", [128, 8], F32).ap()
    ones_r = al("ones_rD", [1, 128], F32).ap()
    spkm = al("spkmD", [16, 2048], F32).ap()
    spkT = al("spkTD", [128, 16, 16], F32).ap()
    xnf = al("xnfD", [128, 8, 512], BF16).ap()
    prt2 = [al(f"prtD{r}", [128, 512], F32).ap() for r in range(2)]
    prb2 = [al(f"prbD{r}", [128, 512], BF16).ap() for r in range(2)]
    qfm2 = [al(f"qfm_cD{r}", [128, 2048], BF16).ap() for r in range(2)]
    kfm2 = [al(f"kfm_cD{r}", [128, 2048], BF16).ap() for r in range(2)]
    vtmh2 = [al(f"vtmhD{r}", [128, 16, 64], BF16).ap() for r in range(2)]
    vden2 = [al(f"vdenD_s{r}", [128, 16, 65], BF16).ap() for r in range(2)]
    vcden2 = [al(f"vcdenD{r}", [35, 65], BF16).ap() for r in range(2)]
    kc2 = [al(f"kcD{r}", [128, 35], BF16).ap() for r in range(2)]
    sml2 = [al(f"smlD{r}", [128, 256], F32).ap() for r in range(2)]
    smlg2 = [al(f"smlgD{r}", [35, 128], F32).ap() for r in range(2)]
    smg2 = [al(f"smgD{r}", [128, 35], F32).ap() for r in range(2)]
    m_b2 = [al(f"m_bD{r}", [128, 1], F32).ap() for r in range(2)]
    m_c2 = [al(f"m_cD{r}", [128, 1], F32).ap() for r in range(2)]
    m_col2 = [al(f"m_colD{r}", [128, 1], F32).ap() for r in range(2)]
    rowb2 = [al(f"rowbD{r}", [1, 640], F32).ap() for r in range(2)]
    cmb2 = [al(f"cmbD{r}", [128, 256], F32).ap() for r in range(2)]
    cmbg2 = [al(f"cmbgD{r}", [35, 128], F32).ap() for r in range(2)]
    pT2 = [al(f"pTD{r}", [128, 256], BF16).ap() for r in range(2)]
    pTg2 = [al(f"pTgD{r}", [35, 128], BF16).ap() for r in range(2)]
    fac_b2 = [al(f"fac_bD{r}", [64, 128], F32).ap() for r in range(2)]
    o_fm = al("o_fmD", [128, 8, 2048], BF16).ap()
    x1t2 = [al(f"x1tD{r}", [128, 512], F32).ap() for r in range(2)]
    g1t2 = [al(f"g1tD{r}", [128, 512], BF16).ap() for r in range(2)]
    outt2 = [al(f"outtD{r}", [128, 512], F32).ap() for r in range(2)]

    psA = nc.alloc_psum_tensor("psA", [128, 512], F32).ap()
    psB = nc.alloc_psum_tensor("psB", [128, 512], F32).ap()
    psC = nc.alloc_psum_tensor("psC", [128, 512], F32).ap()
    psD = nc.alloc_psum_tensor("psD", [128, 512], F32).ap()
    psE = nc.alloc_psum_tensor("psE", [128, 512], F32).ap()
    psF = nc.alloc_psum_tensor("psF", [128, 512], F32).ap()
    psG = nc.alloc_psum_tensor("psG", [128, 512], F32).ap()
    psAs2 = (psA, psG)
    psBs2 = (psB, psD)
    psCs2 = (psC, psE)

    p.dma("sync", s_id, cst["identD"])
    p.copy(identb, s_id)
    for nm in s_cst:
        p.dma("sync", s_cst[nm], cst[nm])
    p.dma("sync", g_rm_sb, cst["g_rm"].rearrange("(t p) g -> p t g", p=128))
    for c in range(8):
        for third in range(3):
            p.dma("sync", wtmp, wqkv[:, c, third * 1024:(third + 1) * 1024])
            p.copy(wqb[:, c, third * 1024:(third + 1) * 1024], wtmp)
        p.dma("sync", wtmp, wout[:, c, :])
        p.copy(wob[:, c, :], wtmp)
    p.dma("sync", s_bqk, bqk)
    p.dma("sync", s_bvr, bv)
    p.dma("sync", s_bout, bout)
    p.memset(ones_r, 1.0)
    p.mm(psA, ones_r, s_bvr[0:1, 0:512], tag="bvb0")
    p.copy(bvb[:, 0:512], psA)
    p.mm(psA, ones_r, s_bvr[0:1, 512:1024], tag="bvb1")
    p.copy(bvb[:, 512:1024], psA)
    spk_sb = al("spk_sbD", [128, 2048], F32).ap()
    sel_sb = al("sel_sbD", [128, 2, 16], F32).ap()
    p.dma("sync", sel_sb, cst["selD"].rearrange("(c p) h -> p c h", p=128))
    for ch in range(2):
        for qc in range(4):
            p.dma("sync", spk_sb[:, qc * 512:(qc + 1) * 512],
                  spk_all[ch * 128:(ch + 1) * 128, qc * 512:(qc + 1) * 512])
        for qc in range(4):
            ps = (psB, psC, psD, psE)[qc]
            p.mm(ps[0:16, :], sel_sb[:, ch, :], spk_sb[:, qc * 512:(qc + 1) * 512],
                 start=(ch == 0), stop=(ch == 1), tag=f"spk{qc}")
            if ch == 1:
                p.ts(spkm[:, qc * 512:(qc + 1) * 512], ps[0:16, :], 0.0, None, ALU.is_gt)
    for t in range(16):
        p.tr(psE[0:128, 0:16], spkm[:, t * 128:(t + 1) * 128], s_id[0:16, 0:16], tag="trspk")
        p.copy(spkT[:, t, :], psE[0:128, 0:16])
    for b in range(B):
        for tc in range(4):
            tsl = slice(tc * 512, (tc + 1) * 512)
            p.dma("sync", xnf, xn1[b].rearrange("(c p) s -> p c s", p=128)[:, :, tsl])
            for e in range(16):
                ps = (psA, psB)[e % 2]
                pr = prt2[e % 2]
                pbb = prb2[e % 2]
                for c in range(8):
                    p.mm(ps, wqb[:, c, e * 128:(e + 1) * 128], xnf[:, c, :],
                         start=(c == 0), stop=(c == 7))
                p.ts(pr, ps, s_bqk[:, e:e + 1], None, ALU.add)
                p.copy(pbb, pr)
                dsth = qf_hbm if e < 8 else kf_hbm
                ee = e % 8
                p.dma("gpsimd", dsth[b, ee * 128:(ee + 1) * 128, tsl], pbb)
            for j in range(4):
                t0 = tc * 512 + j * 128
                for half in range(2):
                    ps = psD if half == 0 else psE
                    pr = prt2[half]
                    pbb = prb2[half]
                    for c in range(8):
                        p.mm(ps, xnf[:, c, j * 128:(j + 1) * 128],
                             wqb[:, c, 2048 + half * 512:2048 + (half + 1) * 512],
                             start=(c == 0), stop=(c == 7))
                    p.tt(pr, ps, bvb[:, half * 512:(half + 1) * 512], ALU.add)
                    p.copy(pbb, pr)
                    p.dma("gpsimd", vt_hbm[b, t0:t0 + 128, half * 512:(half + 1) * 512], pbb)
        p.fence_all()
        for r in range(2):
            for kt in range(16):
                p.memset(vden2[r][:, kt, 64:65], 1.0)
        for hc in range(8):
            qf_c = qfm2[hc % 2]
            kf_c = kfm2[hc % 2]
            p.dma("sync", qf_c, qf_hbm[b, hc * 128:(hc + 1) * 128, :])
            p.dma("sync", kf_c, kf_hbm[b, hc * 128:(hc + 1) * 128, :])
            for hh in range(2):
                h = 2 * hc + hh
                ph = h % 2
                kc = kc2[ph]
                vtmh = vtmh2[ph]
                vden = vden2[ph]
                vcden = vcden2[ph]
                vdh = vdh2[ph]
                hp = 64 * hh
                p.copy(kc[hp:hp + 64, 0:4], kf_c[hp:hp + 64, 0:4], eng="scalar")
                p.copy(kc[hp:hp + 64, 4:35],
                       kf_c.rearrange("p (a c) -> p a c", c=64)[hp:hp + 64, 1:32, 0], eng="scalar")
                p.dma("sync", vtmh, vt_hbm[b].rearrange("(t p) e -> p t e", p=128)[:, :, h * 64:(h + 1) * 64])
                for kt in range(16):
                    p.act(vden[:, kt, 0:64], vtmh[:, kt, :], AF.Identity, scale=spkT[:, kt, h:h + 1])
                    p.copy(vden[:, kt, 64:65], spkT[:, kt, h:h + 1], eng="scalar")
                p.dma("gpsimd", vdh.rearrange("(t p) e -> p t e", p=128), vden)
                p.dma("sync", vcden[0:4, :], vdh[0:4, :])
                p.dma("sync", vcden[4:35, :], vdh.rearrange("(a c) e -> a c e", c=64)[1:32, 0, :])

                def S1(t):
                    par = t % 2
                    pa, pb = psAs2[par], psBs2[par]
                    q0 = t * 128
                    qv = qf_c[hp:hp + 64, q0:q0 + 128]
                    if t > 0:
                        p.mm(pa[:, 0:256], qv, kf_c[hp:hp + 64, q0 - 128:q0 + 128], tag="st2")
                        p.mm(pb[:, 128:256], kf_c[hp:hp + 64, q0 - 128:q0], qv, tag="nT")
                    else:
                        p.mm(pa[:, 128:256], qv, kf_c[hp:hp + 64, 0:128], tag="st2")
                    p.mm(pa[:, 256:291], qv, kc[hp:hp + 64, :], tag="stg")
                    p.mm(pb[0:35, 0:128], kc[hp:hp + 64, :], qv, tag="gT")
                    p.mm(pb[:, 256:384], kf_c[hp:hp + 64, q0:q0 + 128], qv, tag="nT")

                def V1(t):
                    par = t % 2
                    pa = psAs2[par]
                    sml, smg = sml2[par], smg2[par]
                    m_b, m_c, m_col = m_b2[par], m_c2[par], m_col2[par]
                    if t == 0:
                        p.stt(sml[:, 128:256], pa[:, 128:256], 0.125, s_cst["m_s0s"], ALU.mult, ALU.add)
                        p.red(m_b, sml[:, 128:256], ALU.max)
                    else:
                        mk = "m_ps1" if t == 1 else "m_ps"
                        p.stt(sml, pa[:, 0:256], 0.125, s_cst[mk], ALU.mult, ALU.add)
                        p.red(m_b, sml, ALU.max)
                    p.stt(smg, pa[:, 256:291], 0.125, g_rm_sb[:, t, :], ALU.mult, ALU.add)
                    p.red(m_c, smg, ALU.max)
                    p.tt(m_col, m_b, m_c, ALU.max)

                def X(t):
                    par = t % 2
                    pa, pb, pc = psAs2[par], psBs2[par], psCs2[par]
                    sml, smlg = sml2[par], smlg2[par]
                    cmb, cmbg = cmb2[par], cmbg2[par]
                    pT, pTg = pT2[par], pTg2[par]
                    m_col = m_col2[par]
                    mneg2 = rowb2[par][0:1, 0:256]
                    q0 = t * 128
                    fb = par * 256
                    p.tr(pa[0:1, 384:512], m_col, s_id, tag="c2rD")
                    p.ts(mneg2[0:1, 0:128], pa[0:1, 384:512], -1.0, None, ALU.mult, eng="scalar")
                    p.ts(mneg2[0:1, 128:256], pa[0:1, 384:512], -1.0, None, ALU.mult, eng="scalar")
                    p.mm(psF[:, fb:fb + 256], ones_r, mneg2, tag="bcmD")
                    p.tt(cmbg, s_cst["gT"][:, q0:q0 + 128], psF[0:35, fb:fb + 128], ALU.add)
                    p.stt(smlg, pb[0:35, 0:128], 0.125, cmbg, ALU.mult, ALU.add)
                    p.act(pTg, smlg, AF.Exp)
                    p.mm(pc[0:65, 0:128], vcden, pTg, start=True, stop=False, tag="avgD")
                    if t == 0:
                        p.tt(cmb[:, 128:256], s_cst["mT_same0"], psF[:, fb + 128:fb + 256], ALU.add)
                        p.stt(sml[:, 128:256], pb[:, 256:384], 0.125, cmb[:, 128:256], ALU.mult, ALU.add)
                        p.act(pT[:, 128:256], sml[:, 128:256], AF.Exp)
                        p.mm(pc[0:65, 0:128], vden[:, 0, :], pT[:, 128:256],
                             start=False, stop=True, tag="avnD")
                    else:
                        mk = "mT_ps1" if t == 1 else "mT_ps"
                        p.tt(cmb, s_cst[mk], psF[:, fb:fb + 256], ALU.add)
                        p.stt(sml, pb[:, 128:384], 0.125, cmb, ALU.mult, ALU.add)
                        p.act(pT, sml, AF.Exp)
                        p.mm(pc[0:65, 0:128], vden[:, t - 1, :], pT[:, 0:128],
                             start=False, stop=False, tag="avnD")
                        p.mm(pc[0:65, 0:128], vden[:, t, :], pT[:, 128:256],
                             start=False, stop=True, tag="avnD")

                def N(t):
                    par = t % 2
                    pb, pc = psBs2[par], psCs2[par]
                    den_r = rowb2[par][0:1, 256:384]
                    fac_r = rowb2[par][0:1, 384:512]
                    rv_r = rowb2[par][0:1, 512:640]
                    fac_b = fac_b2[par]
                    q0 = t * 128
                    p.copy(den_r, pc[64:65, 0:128], eng="scalar")
                    p.ts(rv_r, den_r, 0.0, None, ALU.is_gt)
                    p.ts(fac_r, den_r, 1e-30, None, ALU.max)
                    p.recip(fac_r, fac_r)
                    p.tt(fac_r, fac_r, rv_r, ALU.mult)
                    p.mm(pb[:, 384:512], ones_r, fac_r, tag="bcfD")
                    p.copy(fac_b, pb[0:64, 384:512], eng="scalar")
                    p.tt(o_fm[hp:hp + 64, hc, q0:q0 + 128], pc[0:64, 0:128], fac_b, ALU.mult)

                S1(0)
                V1(0)
                for t in range(16):
                    if t + 1 < 16:
                        S1(t + 1)
                    X(t)
                    if t + 1 < 16:
                        V1(t + 1)
                    N(t)
        for tc in range(4):
            tsl = slice(tc * 512, (tc + 1) * 512)
            for e in range(8):
                par = e % 2
                ps = (psA, psB)[par]
                pr = prt2[par]
                x1t = x1t2[par]
                g1t = g1t2[par]
                outt = outt2[par]
                p.dma("sync", x1t, x1[b, e * 128:(e + 1) * 128, tsl])
                p.dma("sync", g1t, g1[b, e * 128:(e + 1) * 128, tsl])
                for c in range(8):
                    p.mm(ps, wob[:, c, e * 128:(e + 1) * 128], o_fm[:, c, tsl],
                         start=(c == 0), stop=(c == 7))
                p.ts(pr, ps, s_bout[:, e:e + 1], None, ALU.add)
                p.tt(pr, pr, g1t, ALU.mult)
                p.tt(pr, pr, x1t, ALU.add)
                for j in range(4):
                    pt = (psD, psE)[j % 2]
                    p.tr(pt[:, 0:128], pr[:, j * 128:(j + 1) * 128], s_id)
                    p.copy(outt[:, j * 128:(j + 1) * 128], pt[:, 0:128])
                p.dma("gpsimd", out[b * S + tc * 512:b * S + (tc + 1) * 512,
                                    e * 128:(e + 1) * 128].rearrange("(j p) q -> p j q", p=128),
                      outt.rearrange("p (j q) -> p j q", j=4))
    p.emit()
    return nc


def run_phaseD(inputs, x1_shards, xn1_shards, g1_shards, spikes_all, n_cores=8):
    w = inputs
    wD = wD_prep(w)
    cD = hostD_consts()
    nc = build_phaseD()
    in_maps = []
    for c in range(n_cores):
        m = dict(wD)
        m.update(cD)
        m["x1_fm"] = x1_shards[c]
        m["xn1_fm"] = xn1_shards[c]
        m["g1_fm"] = g1_shards[c]
        m["spk_all"] = spikes_all
        in_maps.append(m)
    res = run_bass_kernel_spmd(nc, in_maps, core_ids=list(range(n_cores)))
    return res.results



def kernel(**inputs):
    n = 8
    w = inputs
    resA = run_phaseA(w, n_cores=n)
    xn_all = np.concatenate([resA[c]["xn_bf16"] for c in range(n)], axis=0)
    scores_all = np.concatenate([resA[c]["scores"] for c in range(n)], axis=0)
    resB = run_phaseB(w, xn_all, scores_all, n_cores=n)
    Zall = np.stack([resB[e]["Z"] for e in range(n)], axis=0)
    x = np.asarray(w["L0_attn_x"], np.float32).reshape(16 * 2048, 1024)
    x_shards = [np.ascontiguousarray(x[c * 4096:(c + 1) * 4096]) for c in range(n)]
    s0_shards = [resA[c]["s0_fm"] for c in range(n)]
    resC = run_phaseC(w, x_shards, s0_shards, Zall, n_cores=n)
    xn1f_shards = [resC[c]["xn1f_fm"] for c in range(n)]
    resC2 = run_phaseC2(w, xn1f_shards, n_cores=n)
    spk_all = np.concatenate([resC[c]["spikes"] for c in range(n)], axis=0)
    x1_shards = [resC[c]["x1_fm"] for c in range(n)]
    xn1_shards = [resC[c]["xn1_fm"] for c in range(n)]
    g1_shards = [resC2[c]["g1_fm"] for c in range(n)]
    resD = run_phaseD(w, x1_shards, xn1_shards, g1_shards, spk_all, n_cores=n)
    out = np.concatenate([resD[c]["out"] for c in range(n)], axis=0)
    return out.reshape(16, 2048, 1024).astype(np.float32)


# version 25
# speedup vs baseline: 1.0097x; 1.0097x over previous
import numpy as np
import ml_dtypes
import concourse.bass as bass
import concourse.mybir as mybir
from concourse.bass_utils import run_bass_kernel_spmd
from concourse.bass_core_rust import IndirectOffsetOnAxis

F32 = mybir.dt.float32
BF16 = mybir.dt.bfloat16
I32 = mybir.dt.int32
AF = mybir.ActivationFunctionType
ALU = mybir.AluOpType
AX = mybir.AxisListType
bfdt = ml_dtypes.bfloat16

ENGINES = ["tensor", "vector", "scalar", "gpsimd", "sync"]
NEG = -1e30


class Op:
    __slots__ = ["eng", "fn", "reads", "writes", "seq", "dma", "tag", "_oc", "_dc", "sreads",
                 "fence_oc", "fence_dc", "_slot", "_dcs"]

    def __init__(self, eng, fn, reads, writes, dma=False, tag="", sreads=()):
        self.eng = eng
        self.fn = fn
        self.reads = reads
        self.writes = writes
        self.dma = dma
        self.tag = tag
        self.sreads = sreads


def _toks(args):
    out = []
    for a in args:
        if a is None or isinstance(a, float):
            continue
        if isinstance(a, (list, tuple)):
            out += _toks(a)
            continue
        t = getattr(a, "tensor", a)
        name = getattr(t, "name", None)
        if name is None:
            out.append((str(t), 0, 1 << 30, 0, 1 << 40))
            continue
        try:
            isz = mybir.dt.size(a.dtype)
            ap = a.ap
            pbase = getattr(a, "base_partition", 0) or 0
            pcnt = int(ap[0][1]) if len(ap) else 128
            flo = int(a.offset) * isz
            ext = 1
            for st, cnt in list(ap)[1:]:
                ext += abs(int(st)) * (int(cnt) - 1)
            fhi = flo + ext * isz
            out.append((name, pbase, pbase + pcnt, flo, fhi))
        except Exception:
            out.append((name, 0, 1 << 30, 0, 1 << 40))
    return out


def _overlap(x, y):
    return (x[0] == y[0] and x[1] < y[2] and y[1] < x[2]
            and x[3] < y[4] and y[3] < x[4])


def _covered(x, y):
    return (x[0] == y[0] and y[1] <= x[1] and x[2] <= y[2]
            and y[3] <= x[3] and x[4] <= y[4])


class Prog:
    def __init__(self, nc):
        self.nc = nc
        self.ops = []
        self._rr = 0

    def add(self, eng, fn, reads, writes, dma=False, tag="", sreads=()):
        sr = _toks(sreads)
        self.ops.append(Op(eng, fn, _toks(reads), _toks(writes), dma, tag, sr))

    def dma(self, eng, out, in_, tag="", slow=False):
        try:
            if out.dtype != in_.dtype:
                eng = "gpsimd"
            else:
                eng = ("sync", "scalar", "gpsimd")[self._rr % 3]
                self._rr += 1
        except Exception:
            pass
        if slow:
            self.add(eng, lambda g: g.dma_start(out=out, in_=in_, allow_slow_non_contiguous=True),
                     [in_], [out], dma=True, tag=tag or "dma")
        else:
            self.add(eng, lambda g: g.dma_start(out=out, in_=in_), [in_], [out], dma=True, tag=tag or "dma")

    def gather(self, out, in_, idx_ap, tag="", accum=False):
        op = ALU.add if accum else ALU.bypass
        self.add("gpsimd",
                 lambda g: g.indirect_dma_start(out, None, in_, IndirectOffsetOnAxis(idx_ap, 0), compute_op=op),
                 [in_, idx_ap], [out], dma=True, tag=tag or "gather")

    def scatter(self, out, in_, idx_ap, tag="", accum=False):
        op = ALU.add if accum else ALU.bypass
        self.add("gpsimd",
                 lambda g: g.indirect_dma_start(out, IndirectOffsetOnAxis(idx_ap, 0), in_, None, compute_op=op),
                 [in_, idx_ap], [out], dma=True, tag=tag or "scatter")

    def mm(self, out, lhsT, rhs, start=True, stop=True, tag=""):
        self.add("tensor", lambda g: g.matmul(out, lhsT, rhs, start=start, stop=stop, skip_group_check=True),
                 [lhsT, rhs], [out], tag=tag or "mm")

    def tr(self, out, in_, ident, tag=""):
        self.add("tensor", lambda g: g.transpose(out, in_, ident), [in_, ident], [out], tag=tag or "tr")

    def act(self, out, in_, func, bias=0.0, scale=1.0, accum_out=None, eng="scalar", tag=""):
        self.add(eng,
                 lambda g: g.activation(out, in_, func, bias=bias, scale=scale, accum_out=accum_out),
                 [in_], [out, accum_out], tag=tag or "act", sreads=[bias, scale])

    def tt(self, out, a, b, op, eng="vector", tag=""):
        self.add(eng, lambda g: g.tensor_tensor(out, a, b, op), [a, b], [out], tag=tag or "tt")

    def ts(self, out, a, s1, s2=None, op0=ALU.mult, op1=ALU.bypass, eng="vector", tag=""):
        if eng == "scalar":
            assert s2 is None and isinstance(s1, float)
            fn = (lambda g: g.mul(out, a, s1)) if op0 == ALU.mult else (lambda g: g.add(out, a, s1))
            self.add(eng, fn, [a], [out], tag=tag or "ts")
        else:
            self.add(eng, lambda g: g.tensor_scalar(out, a, s1, s2, op0, op1),
                     [a], [out], tag=tag or "ts", sreads=[s1, s2])

    def stt(self, out, in0, scalar, in1, op0, op1, eng="vector", tag=""):
        self.add(eng, lambda g: g.scalar_tensor_tensor(out, in0, scalar, in1, op0, op1),
                 [in0, in1], [out], tag=tag or "stt", sreads=[scalar])

    def copy(self, out, in_, eng="vector", tag=""):
        if eng == "scalar":
            self.add(eng, lambda g: g.copy(out, in_), [in_], [out], tag=tag or "copy")
        else:
            self.add(eng, lambda g: g.tensor_copy(out, in_), [in_], [out], tag=tag or "copy")

    def memset(self, ap, val, eng="vector", tag=""):
        self.add(eng, lambda g: g.memset(ap, val), [], [ap], tag=tag or "memset")

    def red(self, out, in_, op, negate=False, eng="vector", tag=""):
        self.add(eng, lambda g: g.tensor_reduce(out, in_, AX.X, op, negate=negate),
                 [in_], [out], tag=tag or "red")

    def recip(self, out, in_, tag=""):
        self.add("vector", lambda g: g.reciprocal(out, in_), [in_], [out], tag=tag or "recip")

    def sel(self, out, mask, a, b, tag=""):
        self.add("vector", lambda g: g.select(out, mask, a, b), [mask, a, b], [out], tag=tag or "sel")

    def scan(self, out, d0, d1, initial, op0, op1, eng="vector", tag=""):
        self.add(eng, lambda g: g.tensor_tensor_scan(out, d0, d1, initial, op0, op1),
                 [d0, d1], [out], tag=tag or "scan")

    def raw(self, eng, fn, reads, writes, tag=""):
        self.add(eng, fn, reads, writes, tag=tag)

    def fence_all(self):
        for e in ENGINES:
            op = Op(e, lambda g: g.engine_nop() if hasattr(g, "engine_nop") else None,
                    [], [], False, "fence")
            op.sreads = ()
            self.ops.append(op)

    def emit(self):
        nc = self.nc
        per_eng = {e: [] for e in ENGINES}
        for op in self.ops:
            per_eng[op.eng].append(op)
        writes_by = {}
        reads_by = {}
        edges = [set() for _ in self.ops]
        sedges = [set() for _ in self.ops]
        for i, op in enumerate(self.ops):
            for t in op.reads:
                for (w, j) in writes_by.get(t[0], ()):
                    if _overlap(t, w):
                        edges[i].add(j)
            for t in op.sreads:
                for (w, j) in writes_by.get(t[0], ()):
                    if _overlap(t, w):
                        sedges[i].add(j)
            for t in op.writes:
                for (w, j) in writes_by.get(t[0], ()):
                    if _overlap(t, w):
                        edges[i].add(j)
                for (r, j) in reads_by.get(t[0], ()):
                    if _overlap(t, r):
                        edges[i].add(j)
            for t in list(op.reads) + list(op.sreads):
                reads_by.setdefault(t[0], []).append((t, i))
            for t in op.writes:
                lw = writes_by.setdefault(t[0], [])
                lw[:] = [(w, j) for (w, j) in lw if not _covered(w, t)]
                lw.append((t, i))
                rb = reads_by.get(t[0])
                if rb:
                    rb[:] = [(r, j) for (r, j) in rb if not _covered(r, t)]
        NSLOT = 4
        has_dma = {e: any(op.dma for op in per_eng[e]) for e in ENGINES}
        sems, dsems = {}, {}
        for e in ENGINES:
            sems[e] = nc.alloc_semaphore(f"sem_{e}")
            if has_dma[e]:
                dsems[e] = [nc.alloc_semaphore(f"dsem_{e}{s}") for s in range(NSLOT)]
        idx_all = {id(op): i for i, op in enumerate(self.ops)}
        xconsumed = set()
        for j, op in enumerate(self.ops):
            for i in (edges[j] | sedges[j]):
                if self.ops[i].eng != op.eng:
                    xconsumed.add(i)
        known = {e: {p: 0 for p in ENGINES} for e in ENGINES}
        knownd = {e: {} for e in ENGINES}
        for e in ENGINES:
            oc = 0
            dcs = [0] * NSLOT
            dc = 0
            for op in per_eng[e]:
                if op.tag == "fence":
                    op._oc = oc
                    continue
                if op.dma:
                    op._slot = dc % NSLOT
                    dcs[op._slot] += 1
                    op._dcs = dcs[op._slot]
                    dc += 1
                else:
                    oc += 1
                op._oc = oc
        final_dma = {}
        for e in ENGINES:
            cnt = [0] * NSLOT
            for op in per_eng[e]:
                if op.dma:
                    cnt[op._slot] += 1
            final_dma[e] = cnt
        final_oc = {e: sum(1 for op in per_eng[e] if (not op.dma and op.tag != "fence")) for e in ENGINES}

        run_oc = {e: 0 for e in ENGINES}
        run_dc = {e: [0] * NSLOT for e in ENGINES}
        for op in self.ops:
            if op.tag == "fence":
                op.fence_oc = dict(run_oc)
                op.fence_dc = {e: list(run_dc[e]) for e in ENGINES}
            elif op.dma:
                run_dc[op.eng][op._slot] += 1
            else:
                run_oc[op.eng] += 1

        def emit_engine(g, e):
            for op in per_eng[e]:
                i = idx_all[id(op)]
                wants, wantsd = {}, {}
                if op.tag == "fence":
                    for pe in ENGINES:
                        if op.fence_oc[pe] and pe != e:
                            wants[pe] = op.fence_oc[pe]
                        for s in range(NSLOT):
                            if op.fence_dc[pe][s]:
                                wantsd[(pe, s)] = op.fence_dc[pe][s] * 16
                for j in edges[i]:
                    p = self.ops[j]
                    if p.dma:
                        k = (p.eng, p._slot)
                        wantsd[k] = max(wantsd.get(k, 0), p._dcs * 16)
                    else:
                        wants[p.eng] = max(wants.get(p.eng, 0), p._oc)
                for j in sedges[i]:
                    p = self.ops[j]
                    if p.dma:
                        k = (p.eng, p._slot)
                        wantsd[k] = max(wantsd.get(k, 0), p._dcs * 16)
                    else:
                        wants[p.eng] = max(wants.get(p.eng, 0), p._oc)
                for pe, v in wants.items():
                    if known[e][pe] < v:
                        g.wait_ge(sems[pe], v)
                        known[e][pe] = v
                for (pe, s), v in wantsd.items():
                    if knownd[e].get((pe, s), 0) < v:
                        g.wait_ge(dsems[pe][s], v)
                        knownd[e][(pe, s)] = v
                if op.tag == "fence":
                    continue
                if op.dma and op._dcs > 1 and knownd[e].get((e, op._slot), 0) < (op._dcs - 1) * 16:
                    g.wait_ge(dsems[e][op._slot], (op._dcs - 1) * 16)
                    knownd[e][(e, op._slot)] = (op._dcs - 1) * 16
                inst = op.fn(g)
                if inst is not None:
                    if op.dma:
                        inst.then_inc(dsems[e][op._slot], 16)
                    else:
                        inst.then_inc(sems[e], 1)

        with nc.Block() as block:
            for e, deco in (("tensor", block.tensor), ("vector", block.vector),
                            ("scalar", block.scalar), ("gpsimd", block.gpsimd),
                            ("sync", block.sync)):
                def make(e=e):
                    def body(g):
                        emit_engine(g, e)
                        if e == "gpsimd":
                            for pe in ENGINES:
                                if has_dma[pe]:
                                    for s in range(NSLOT):
                                        if final_dma[pe][s]:
                                            g.wait_ge(dsems[pe][s], final_dma[pe][s] * 16)
                                if final_oc[pe] and pe != "gpsimd":
                                    g.wait_ge(sems[pe], final_oc[pe])
                    return body
                deco(make())
        return self



def host_consts_A():
    i = np.arange(128)
    c = {}
    c["ident_f32"] = np.eye(128, dtype=np.float32)
    c["causal_add"] = np.where(i[None, :] <= i[:, None], 0.0, NEG).astype(np.float32)
    c["causalT_add"] = np.where(i[:, None] <= i[None, :], 0.0, NEG).astype(np.float32)
    c["iota256_row"] = np.tile(np.arange(1, 257, dtype=np.float32), (128, 1))
    c["iota_row2048"] = np.tile(np.arange(2048, dtype=np.float32), (128, 1))
    c["qidx_cols"] = np.ascontiguousarray(np.arange(2048, dtype=np.float32).reshape(16, 128).T)
    return c


def wA_prep(w):
    d = {}
    for nm, key in (("wq", "L0_attn_w_q"), ("wk", "L0_attn_w_k"), ("wv", "L0_attn_w_v"),
                    ("wo", "L0_attn_w_out")):
        d[nm] = np.ascontiguousarray(np.asarray(w[key], np.float32).reshape(8, 128, 1024).transpose(1, 0, 2))
    d["scorer_w"] = np.ascontiguousarray(np.asarray(w["L0_attn_scorer_w"], np.float32).reshape(8, 128, 1).transpose(1, 0, 2))
    d["Wr"] = np.ascontiguousarray(np.asarray(w["L0_ffn_Wr"], np.float32).reshape(8, 128, 8).transpose(1, 0, 2))
    d["norm_w"] = np.ascontiguousarray(np.asarray(w["L0_attn_norm_w"], np.float32).reshape(8, 128).T)
    d["scorer_b"] = np.asarray(w["L0_attn_scorer_b"], np.float32).reshape(1, 1)
    return d


def build_phaseA():
    nc = bass.Bass("TRN2", target_bir_lowering=False, name="phA")
    B, S, D = 2, 2048, 1024
    H, G = 16, 256
    NT = S // 128
    x = nc.dram_tensor("x", [B * S, D], F32, kind="ExternalInput").ap()
    win = {}
    for nm, shp in (("wq", [128, 8, 1024]), ("wk", [128, 8, 1024]), ("wv", [128, 8, 1024]),
                    ("wo", [128, 8, 1024]), ("scorer_w", [128, 8, 1]), ("scorer_b", [1, 1]),
                    ("Wr", [128, 8, 8]), ("norm_w", [128, 8])):
        win[nm] = nc.dram_tensor(nm, shp, F32, kind="ExternalInput").ap()
    cst = {}
    for nm, shp in (("ident_f32", [128, 128]), ("causal_add", [128, 128]),
                    ("causalT_add", [128, 128]), ("iota256_row", [128, 256]),
                    ("iota_row2048", [128, 2048]), ("qidx_cols", [128, 16])):
        cst[nm] = nc.dram_tensor(nm, shp, F32, kind="ExternalInput").ap()
    xn_out = nc.dram_tensor("xn_bf16", [B * S, D], BF16, kind="ExternalOutput").ap()
    scores_out = nc.dram_tensor("scores", [B * S, 8], F32, kind="ExternalOutput").ap()
    s0_out = nc.dram_tensor("s0_fm", [B, D, S], F32, kind="ExternalOutput").ap()
    h_fm_hbm = nc.dram_tensor("h_fm_hbm", [B, D, S], BF16).ap()
    xn_fm_hbm = nc.dram_tensor("xn_fm_hbm", [B, D, S], F32).ap()
    q_fm_hbm = nc.dram_tensor("q_fm_hbm", [B, D, S], BF16).ap()
    k_fm_hbm = nc.dram_tensor("k_fm_hbm", [B, D, S], BF16).ap()
    k_tm_hbm = nc.dram_tensor("k_tm_hbm", [B, S, D], BF16).ap()
    v_tm_hbm = nc.dram_tensor("v_tm_hbm", [B, S, D], BF16).ap()
    sc_hbm = nc.dram_tensor("sc_hbm", [B, S, 1], F32).ap()
    row_hbm = nc.dram_tensor("row_hbm", [2048, 1], F32).ap()

    p = Prog(nc)
    al = nc.alloc_sbuf_tensor
    wtmp = al("wtmp", [128, 512], F32).ap()
    wtmp2 = al("wtmp2", [128, 512], F32).ap()
    wqb = al("wqb", [128, 8, 1024], BF16).ap()
    wkb = al("wkb", [128, 8, 1024], BF16).ap()
    wvb = al("wvb", [128, 8, 1024], BF16).ap()
    wob = al("wob", [128, 8, 1024], BF16).ap()
    s_scw = al("s_scw", [128, 8, 1], F32).ap()
    s_scb = al("s_scb", [1, 1], F32).ap()
    s_wr = al("s_wr", [128, 8, 8], F32).ap()
    s_nw = al("s_nw", [128, 8], F32).ap()
    s_cst = {nm: al("s_" + nm, list(cst[nm].shape), F32).ap() for nm in cst}
    identb = al("identb", [128, 128], BF16).ap()
    ones_r = al("ones_r", [1, 128], F32).ap()
    scb_col = al("scb_col", [128, 1], F32).ap()
    eps5 = al("eps5", [128, 1], F32).ap()
    eps6 = al("eps6", [128, 1], F32).ap()

    xt = al("xt", [128, 1024], F32).ap()
    xs1 = al("xs1", [128, 1], F32).ap()
    xs2 = al("xs2", [128, 1], F32).ap()
    xnt = al("xnt", [128, 1024], F32).ap()
    xnb = al("xnb", [128, 1024], BF16).ap()
    hst = xt
    hfm_t = al("hfm_t", [128, 1024], F32).ap()
    hfm_b = al("hfm_b", [128, 1024], BF16).ap()
    xnfm_t = al("xnfm_t", [128, 1024], F32).ap()
    lgt = al("lgt", [128, 8], F32).ap()
    lm = al("lm", [128, 1], F32).ap()
    ldn = al("ldn", [128, 1], F32).ap()
    sc_cols = al("sc_cols", [128, 2, 16], F32).ap()

    actbuf = al("actbuf", [128, 8, 2048], BF16).ap()
    scb_b = actbuf.bitcast(F32).rearrange("p a b -> p (a b)")[:, 0:2048]
    vtmh = al("vtmh", [128, 16, 64], BF16).ap()
    qfm_c = al("qfm_c", [128, 2048], BF16).ap()
    kfm_c = al("kfm_c", [128, 2048], BF16).ap()
    ktt = al("ktt", [128, 1024], BF16).ap()
    trc4A = al("trc4A", [128, 4, 128], BF16).ap()
    rank_c = al("rank_c", [128, 1], F32).ap()
    cmp_t = xt
    memb_c = al("memb_c", [128, 16], F32).ap()
    memb_r = al("memb_r", [1, 2048], F32).ap()
    cum_r = al("cum_r", [1, 2048], F32).ap()
    zero1 = al("zero1", [1, 1], F32).ap()
    cum_c = al("cum_c", [128, 16], F32).ap()
    onehf = al("onehf", [128, 256], F32).ap()
    oneh_t = al("oneh_t", [128, 256], BF16).ap()
    pg_c = al("pg_c", [128, 2], F32).ap()
    pg_r = al("pg_r", [1, 256], F32).ap()
    pgb = al("pgb", [128, 256], F32).ap()
    kg_tm = al("kg_tm", [128, 2, 1024], BF16).ap()
    vg_tm = al("vg_tm", [128, 2, 1024], BF16).ap()
    kg_fm = al("kg_fm", [128, 8, 256], BF16).ap()
    gadd = al("gadd", [128, 256], F32).ap()
    gaddT = al("gaddT", [128, 512], F32).ap()
    sml = al("sml", [128, 512], F32).ap()
    smg = al("smg", [128, 256], F32).ap()
    m_a = al("m_a", [128, 1], F32).ap()
    m_b = al("m_b", [128, 1], F32).ap()
    m_c = al("m_c", [128, 1], F32).ap()
    m_col = al("m_col", [128, 16], F32).ap()

    mb_s = al("mb_s", [128, 2, 512], F32).ap()
    cmb = al("cmb", [128, 512], F32).ap()
    pT = al("pT", [128, 512], BF16).ap()
    vden = al("vden", [128, 16, 65], BF16).ap()
    vgden = al("vgden", [128, 2, 65], BF16).ap()

    fac_b = al("fac_b", [64, 512], F32).ap()
    s0t = al("s0t", [128, 512], F32).ap()
    sc_rowt = al("sc_rowt", [1, 2048], F32).ap()
    mneg_r = sc_rowt[0:1, 0:1024]
    den_r = sc_rowt[0:1, 1024:1536]
    fac_r = sc_rowt[0:1, 1536:2048]

    psA = nc.alloc_psum_tensor("psA", [128, 512], F32).ap()
    psB = nc.alloc_psum_tensor("psB", [128, 512], F32).ap()
    psC = nc.alloc_psum_tensor("psC", [128, 512], F32).ap()
    psD = nc.alloc_psum_tensor("psD", [128, 512], F32).ap()
    psE = nc.alloc_psum_tensor("psE", [128, 512], F32).ap()
    psF = nc.alloc_psum_tensor("psF", [128, 512], F32).ap()
    psG = nc.alloc_psum_tensor("psG", [128, 512], F32).ap()
    psH = nc.alloc_psum_tensor("psH", [128, 512], F32).ap()

    _wi = 0
    for nm, dst in (("wq", wqb), ("wk", wkb), ("wv", wvb), ("wo", wob)):
        for c in range(8):
            for hv in range(2):
                wt = wtmp if _wi % 2 == 0 else wtmp2
                _wi += 1
                p.dma("sync", wt, win[nm][:, c, hv * 512:(hv + 1) * 512])
                p.copy(dst[:, c, hv * 512:(hv + 1) * 512], wt)
    p.dma("sync", s_scw, win["scorer_w"])
    p.dma("sync", s_scb, win["scorer_b"])
    p.dma("sync", s_wr, win["Wr"])
    p.dma("sync", s_nw, win["norm_w"])
    for nm in cst:
        p.dma("sync", s_cst[nm], cst[nm])
    p.copy(identb, s_cst["ident_f32"])
    p.memset(ones_r, 1.0)
    p.memset(eps5, 1e-5)
    p.memset(eps6, 1e-6)
    p.mm(psE[:, 0:1], ones_r, s_scb, tag="bscb")
    p.copy(scb_col, psE[:, 0:1])

    def col_to_row(col_ap, row_ap, tag=""):
        p.tr(psE[0:1, 0:128], col_ap, s_cst["ident_f32"], tag=tag or "c2r")
        p.copy(row_ap, psE[0:1, 0:128], eng="scalar")

    for b in range(B):
        for t in range(NT):
            row0 = b * S + t * 128
            p.dma("sync", xt, x[row0:row0 + 128, :])
            p.act(hfm_t, xt, AF.Identity, accum_out=xs1)
            p.ts(xs1, xs1, 1.0 / D, None, ALU.mult)
            p.ts(xnt, xt, xs1[:, 0:1], None, ALU.subtract)
            p.act(hfm_t, xnt, AF.Square, accum_out=xs2)
            p.ts(xs2, xs2, 1.0 / D, None, ALU.mult)
            p.act(xs2, xs2, AF.Ln, bias=eps5[:, 0:1])
            p.act(xs2, xs2, AF.Exp, scale=-0.5)
            p.ts(xnt, xnt, xs2[:, 0:1], None, ALU.mult)
            p.copy(xnb, xnt)
            p.dma("gpsimd", xn_out[row0:row0 + 128, :], xnb)
            p.act(hfm_t, xnt, AF.Square, accum_out=xs1)
            p.ts(xs1, xs1, 1.0 / D, None, ALU.mult)
            p.act(xs1, xs1, AF.Ln, bias=eps6[:, 0:1])
            p.act(xs1, xs1, AF.Exp, scale=-0.5)
            p.ts(hst, xnt, xs1[:, 0:1], None, ALU.mult)
            for c in range(8):
                p.tr(psA[:, :128], xnt[:, c * 128:(c + 1) * 128], s_cst["ident_f32"])
                p.copy(xnfm_t[:, c * 128:(c + 1) * 128], psA[:, :128])
                p.tr(psB[:, :128], hst[:, c * 128:(c + 1) * 128], s_cst["ident_f32"])
                p.ts(hfm_t[:, c * 128:(c + 1) * 128], psB[:, :128], s_nw[:, c:c + 1], None, ALU.mult)
            p.copy(hfm_b, hfm_t)
            p.dma("gpsimd", xn_fm_hbm[b].rearrange("(c p) s -> p c s", p=128)[:, :, t * 128:(t + 1) * 128],
                  xnfm_t.rearrange("p (c q) -> p c q", c=8))
            p.dma("gpsimd", h_fm_hbm[b].rearrange("(c p) s -> p c s", p=128)[:, :, t * 128:(t + 1) * 128],
                  hfm_b.rearrange("p (c q) -> p c q", c=8))
            for c in range(8):
                p.mm(psC[:, 0:8], xnfm_t[:, c * 128:(c + 1) * 128], s_wr[:, c, :],
                     start=(c == 0), stop=(c == 7))
            p.red(lm, psC[:, 0:8], ALU.max, negate=True)
            p.act(lgt, psC[:, 0:8], AF.Exp, bias=lm[:, 0:1], accum_out=ldn)
            p.recip(ldn, ldn)
            p.ts(lgt, lgt, ldn[:, 0:1], None, ALU.mult)
            p.dma("gpsimd", scores_out[row0:row0 + 128, :], lgt)
            for c in range(8):
                p.mm(psD[:, 0:1], hfm_t[:, c * 128:(c + 1) * 128], s_scw[:, c, :],
                     start=(c == 0), stop=(c == 7))
            p.tt(sc_cols[:, b, t:t + 1], psD[:, 0:1], scb_col, ALU.add)
            p.dma("gpsimd", sc_hbm[b, t * 128:(t + 1) * 128, :], sc_cols[:, b, t:t + 1])

    p.fence_all()
    scale = 1.0 / 8.0
    o_fm = actbuf
    for b in range(B):
        p.dma("sync", actbuf, h_fm_hbm[b].rearrange("(c p) s -> p c s", p=128))
        for tc in range(4):
            tsl = slice(tc * 512, (tc + 1) * 512)
            for e in range(8):
                for c in range(8):
                    p.mm(psA, wqb[:, c, e * 128:(e + 1) * 128], actbuf[:, c, tsl],
                         start=(c == 0), stop=(c == 7))
                p.copy(qfm_c[:, 0:512], psA)
                p.dma("gpsimd", q_fm_hbm[b, e * 128:(e + 1) * 128, tsl], qfm_c[:, 0:512])
                for c in range(8):
                    p.mm(psB, wkb[:, c, e * 128:(e + 1) * 128], actbuf[:, c, tsl],
                         start=(c == 0), stop=(c == 7))
                p.copy(kfm_c[:, 0:512], psB)
                p.dma("gpsimd", k_fm_hbm[b, e * 128:(e + 1) * 128, tsl], kfm_c[:, 0:512])
                for j in range(4):
                    p.tr(psC.bitcast(BF16)[:, 0:128], kfm_c[:, j * 128:(j + 1) * 128], identb)
                    p.copy(trc4A[:, j, :], psC.bitcast(BF16)[:, 0:128])
                p.dma("gpsimd", k_tm_hbm[b, tc * 512:(tc + 1) * 512,
                                         e * 128:(e + 1) * 128].rearrange("(j p) q -> p j q", p=128),
                      trc4A)
            for j in range(4):
                t0 = tc * 512 + j * 128
                for c in range(8):
                    p.mm(psD, actbuf[:, c, t0:t0 + 128], wvb[:, c, 0:512],
                         start=(c == 0), stop=(c == 7))
                for c in range(8):
                    p.mm(psE, actbuf[:, c, t0:t0 + 128], wvb[:, c, 512:1024],
                         start=(c == 0), stop=(c == 7))
                p.copy(ktt[:, 0:512], psD)
                p.copy(ktt[:, 512:1024], psE)
                p.dma("gpsimd", v_tm_hbm[b, t0:t0 + 128, :], ktt)
        p.dma("sync", sc_rowt, sc_hbm[b].rearrange("s one -> one s"))
        for j in range(4):
            p.mm(psF, ones_r, sc_rowt[0:1, j * 512:(j + 1) * 512], tag="bcast")
            p.copy(scb_b[:, j * 512:(j + 1) * 512], psF, eng="scalar")
        for t in range(NT):
            p.ts(cmp_t, scb_b[:, 0:1024], sc_cols[:, b, t:t + 1], None, ALU.is_gt)
            p.red(rank_c, cmp_t, ALU.add)
            p.ts(cmp_t, scb_b[:, 1024:2048], sc_cols[:, b, t:t + 1], None, ALU.is_gt)
            p.red(m_a, cmp_t, ALU.add)
            p.tt(rank_c, rank_c, m_a, ALU.add)
            p.ts(memb_c[:, t:t + 1], rank_c, float(G), None, ALU.is_lt)
            col_to_row(memb_c[:, t:t + 1], memb_r[0:1, t * 128:(t + 1) * 128])
        p.memset(zero1, 0.0)
        p.scan(cum_r, memb_r, zero1.broadcast_to((1, 2048)), 0.0, ALU.add, ALU.add)
        p.dma("gpsimd", row_hbm.rearrange("s one -> one s"), cum_r)
        for t in range(NT):
            p.dma("sync", cum_c[:, t:t + 1], row_hbm[t * 128:(t + 1) * 128, :])
        for t in range(NT):
            p.ts(onehf, s_cst["iota256_row"], cum_c[:, t:t + 1], memb_c[:, t:t + 1],
                 ALU.is_equal, ALU.mult)
            p.copy(oneh_t, onehf)
            p.mm(psG[:, 0:1], onehf[:, 0:128], s_cst["qidx_cols"][:, t:t + 1],
                 start=(t == 0), stop=(t == NT - 1), tag="pg0")
            p.mm(psH[:, 0:1], onehf[:, 128:256], s_cst["qidx_cols"][:, t:t + 1],
                 start=(t == 0), stop=(t == NT - 1), tag="pg1")
            p.dma("sync", ktt, k_tm_hbm[b, t * 128:(t + 1) * 128, :])
            p.mm(psA, oneh_t[:, 0:128], ktt[:, 0:512], start=(t == 0), stop=(t == NT - 1), tag="kg00")
            p.mm(psB, oneh_t[:, 0:128], ktt[:, 512:1024], start=(t == 0), stop=(t == NT - 1), tag="kg01")
            p.mm(psD, oneh_t[:, 128:256], ktt[:, 0:512], start=(t == 0), stop=(t == NT - 1), tag="kg10")
            p.mm(psE, oneh_t[:, 128:256], ktt[:, 512:1024], start=(t == 0), stop=(t == NT - 1), tag="kg11")
        p.copy(pg_c[:, 0:1], psG[:, 0:1])
        p.copy(pg_c[:, 1:2], psH[:, 0:1])
        p.copy(kg_tm[:, 0, 0:512], psA)
        p.copy(kg_tm[:, 0, 512:1024], psB)
        p.copy(kg_tm[:, 1, 0:512], psD)
        p.copy(kg_tm[:, 1, 512:1024], psE)
        for t in range(NT):
            p.ts(onehf, s_cst["iota256_row"], cum_c[:, t:t + 1], memb_c[:, t:t + 1],
                 ALU.is_equal, ALU.mult)
            p.copy(oneh_t, onehf)
            p.dma("sync", ktt, v_tm_hbm[b, t * 128:(t + 1) * 128, :])
            p.mm(psA, oneh_t[:, 0:128], ktt[:, 0:512], start=(t == 0), stop=(t == NT - 1), tag="vg00")
            p.mm(psB, oneh_t[:, 0:128], ktt[:, 512:1024], start=(t == 0), stop=(t == NT - 1), tag="vg01")
            p.mm(psD, oneh_t[:, 128:256], ktt[:, 0:512], start=(t == 0), stop=(t == NT - 1), tag="vg10")
            p.mm(psE, oneh_t[:, 128:256], ktt[:, 512:1024], start=(t == 0), stop=(t == NT - 1), tag="vg11")
        p.copy(vg_tm[:, 0, 0:512], psA)
        p.copy(vg_tm[:, 0, 512:1024], psB)
        p.copy(vg_tm[:, 1, 0:512], psD)
        p.copy(vg_tm[:, 1, 512:1024], psE)
        col_to_row(pg_c[:, 0:1], pg_r[0:1, 0:128])
        col_to_row(pg_c[:, 1:2], pg_r[0:1, 128:256])
        p.mm(psF[:, 0:256], ones_r, pg_r, tag="bcastpg")
        p.copy(pgb, psF[:, 0:256], eng="scalar")
        for gc in range(2):
            for c in range(8):
                p.tr(psC.bitcast(BF16)[:, 0:128], kg_tm[:, gc, c * 128:(c + 1) * 128], identb)
                p.copy(kg_fm[:, c, gc * 128:(gc + 1) * 128], psC.bitcast(BF16)[:, 0:128])
        for kt in range(16):
            p.memset(vden[:, kt, 64:65], 1.0)
        for gc in range(2):
            p.memset(vgden[:, gc, 64:65], 1.0)
        for hc in range(8):
            p.dma("sync", qfm_c, q_fm_hbm[b, hc * 128:(hc + 1) * 128, :])
            p.dma("sync", kfm_c, k_fm_hbm[b, hc * 128:(hc + 1) * 128, :])
            for hh in range(2):
                h = 2 * hc + hh
                hp = 64 * hh
                p.dma("sync", vtmh, v_tm_hbm[b].rearrange("(t p) e -> p t e", p=128)[:, :, h * 64:(h + 1) * 64])
                for kt in range(16):
                    p.copy(vden[:, kt, 0:64], vtmh[:, kt, :])
                for gc in range(2):
                    p.copy(vgden[:, gc, 0:64], vg_tm[:, gc, h * 64:(h + 1) * 64])
                for w in range(2):
                    w0 = w * 1024
                    for tq in range(8):
                        t = w * 8 + tq
                        q0 = w0 + tq * 128
                        nfull = tq * 128
                        have_ma = False
                        for kk in range(0, nfull, 512):
                            nn = min(512, nfull - kk)
                            p.mm(psA[:, :nn], qfm_c[hp:hp + 64, q0:q0 + 128],
                                 kfm_c[hp:hp + 64, w0 + kk:w0 + kk + nn])
                            p.red(m_b if have_ma else m_a, psA[:, :nn], ALU.max)
                            if have_ma:
                                p.tt(m_a, m_a, m_b, ALU.max)
                            have_ma = True
                        p.mm(psA[:, :128], qfm_c[hp:hp + 64, q0:q0 + 128], kfm_c[hp:hp + 64, q0:q0 + 128])
                        p.stt(sml[:, :128], psA[:, :128], 1.0, s_cst["causal_add"], ALU.mult, ALU.add)
                        p.red(m_b, sml[:, :128], ALU.max)
                        if have_ma:
                            p.tt(m_a, m_a, m_b, ALU.max)
                        else:
                            p.copy(m_a, m_b)
                        p.mm(psB[:, 0:256], qfm_c[hp:hp + 64, q0:q0 + 128], kg_fm[hp:hp + 64, hc, :])
                        p.ts(gadd, pgb, s_cst["qidx_cols"][:, t:t + 1], NEG, ALU.is_gt, ALU.mult)
                        p.stt(smg, psB[:, 0:256], 1.0, gadd, ALU.mult, ALU.add)
                        p.red(m_c, smg, ALU.max)
                        p.tt(m_a, m_a, m_c, ALU.max)
                        p.ts(m_col[:, t:t + 1], m_a, scale, None, ALU.mult)
                    for tq in range(8):
                        t = w * 8 + tq
                        p.tr(psE[0:1, 0:128], m_col[:, t:t + 1], s_cst["ident_f32"], tag="c2rm")
                        p.ts(mneg_r[0:1, tq * 128:(tq + 1) * 128], psE[0:1, 0:128], -1.0, None,
                             ALU.mult, eng="scalar")
                    for qc in range(2):
                        p.mm(psF, ones_r, mneg_r[0:1, qc * 512:(qc + 1) * 512], tag="bcm")
                        p.copy(mb_s[:, qc, :], psF, eng="scalar")
                    for qc in range(2):
                        qa = w0 + qc * 512
                        for gc in range(2):
                            p.mm(psB, kg_fm[hp:hp + 64, hc, gc * 128:(gc + 1) * 128],
                                 qfm_c[hp:hp + 64, qa:qa + 512], tag="mmgT")
                            p.ts(gaddT, s_cst["iota_row2048"][:, qa:qa + 512], pg_c[:, gc:gc + 1],
                                 NEG, ALU.is_lt, ALU.mult)
                            p.tt(cmb, gaddT, mb_s[:, qc, :], ALU.add)
                            p.stt(sml, psB, scale, cmb, ALU.mult, ALU.add)
                            p.act(pT, sml, AF.Exp)
                            p.mm(psC[0:65, :], vgden[:, gc, :], pT, start=(gc == 0), stop=False, tag="avg")
                        valid_kts = [kt for kt in range(8) if max(kt * 128, qc * 512) < (qc + 1) * 512]
                        for kt in valid_kts:
                            k0 = w0 + kt * 128
                            lo = max(kt * 128, qc * 512)
                            nn = (qc + 1) * 512 - lo
                            off = lo - qc * 512
                            qb0 = w0 + lo
                            p.mm(psB[:, :nn], kfm_c[hp:hp + 64, k0:k0 + 128], qfm_c[hp:hp + 64, qb0:qb0 + nn],
                                 tag="mmlT")
                            if lo == kt * 128:
                                p.tt(cmb[:, :128], s_cst["causalT_add"], mb_s[:, qc, off:off + 128], ALU.add)
                                if nn > 128:
                                    p.copy(cmb[:, 128:nn], mb_s[:, qc, off + 128:off + nn])
                                p.stt(sml[:, :nn], psB[:, :nn], scale, cmb[:, :nn], ALU.mult, ALU.add)
                            else:
                                p.stt(sml[:, :nn], psB[:, :nn], scale, mb_s[:, qc, off:off + nn],
                                      ALU.mult, ALU.add)
                            p.act(pT[:, :nn], sml[:, :nn], AF.Exp)
                            p.mm(psC[0:65, off:off + nn], vden[:, w * 8 + kt, :], pT[:, :nn],
                                 start=False, stop=(kt == valid_kts[-1]), tag="avl")
                        p.copy(den_r, psC[64:65, 0:512])
                        p.recip(fac_r, den_r)
                        p.mm(psF, ones_r, fac_r, tag="bcf")
                        p.copy(fac_b, psF[0:64, :], eng="scalar")
                        p.tt(o_fm[hp:hp + 64, hc, qa:qa + 512], psC[0:64, :], fac_b, ALU.mult)

        for tc in range(4):
            tsl = slice(tc * 512, (tc + 1) * 512)
            for e in range(8):
                for c in range(8):
                    p.mm(psA, wob[:, c, e * 128:(e + 1) * 128], o_fm[:, c, tsl],
                         start=(c == 0), stop=(c == 7))
                p.dma("sync", s0t, xn_fm_hbm[b, e * 128:(e + 1) * 128, tsl])
                p.tt(s0t, s0t, psA, ALU.add)
                p.dma("gpsimd", s0_out[b, e * 128:(e + 1) * 128, tsl], s0t)
    p.emit()
    return nc


def run_phaseA(inputs, n_cores=8):
    w = inputs
    cA = host_consts_A()
    wA = wA_prep(w)
    x = np.asarray(w["L0_attn_x"], np.float32).reshape(16 * 2048, 1024)
    nc = build_phaseA()
    in_maps = []
    for c in range(n_cores):
        m = {"x": np.ascontiguousarray(x[c * 4096:(c + 1) * 4096])}
        m.update({k: v for k, v in wA.items()})
        m.update({k: v for k, v in cA.items()})
        in_maps.append(m)
    res = run_bass_kernel_spmd(nc, in_maps, core_ids=list(range(n_cores)))
    return res.results



def hostB_consts():
    tok = np.arange(32768, dtype=np.float32).reshape(128, 256)
    return {"iota_tokB": np.ascontiguousarray(tok),
            "iota_pB": np.arange(128, dtype=np.float32).reshape(128, 1),
            "identB": np.eye(128, dtype=np.float32)}


def wB_prep(w, e):
    return {
        "W1e": np.ascontiguousarray(np.asarray(w["L0_ffn_W1"][e], np.float32).reshape(8, 128, 4096).transpose(1, 0, 2)),
        "W2e": np.ascontiguousarray(np.asarray(w["L0_ffn_W2"][e], np.float32).reshape(32, 128, 1024).transpose(1, 0, 2)),
        "b1e": np.ascontiguousarray(np.asarray(w["L0_ffn_b1"][e], np.float32).reshape(32, 128).T),
        "b2e": np.asarray(w["L0_ffn_b2"][e], np.float32).reshape(1, 1024),
    }


def build_phaseB():
    nc = bass.Bass("TRN2", target_bir_lowering=False, name="phB")
    N, D, K = 32768, 1024, 8192
    NC = 64
    xn_all = nc.dram_tensor("xn_all", [N, D], BF16, kind="ExternalInput").ap()
    sc_e = nc.dram_tensor("sc_e", [128, 256], F32, kind="ExternalInput").ap()
    W1 = nc.dram_tensor("W1e", [128, 8, 4096], F32, kind="ExternalInput").ap()
    W2 = nc.dram_tensor("W2e", [128, 32, 1024], F32, kind="ExternalInput").ap()
    b1 = nc.dram_tensor("b1e", [128, 32], F32, kind="ExternalInput").ap()
    b2 = nc.dram_tensor("b2e", [1, 1024], F32, kind="ExternalInput").ap()
    iota_tok = nc.dram_tensor("iota_tokB", [128, 256], F32, kind="ExternalInput").ap()
    iota_p = nc.dram_tensor("iota_pB", [128, 1], F32, kind="ExternalInput").ap()
    identd = nc.dram_tensor("identB", [128, 128], F32, kind="ExternalInput").ap()
    Z = nc.dram_tensor("Z", [N, D], BF16, kind="ExternalOutput").ap()
    pair_hbm = nc.dram_tensor("pair_hbm", [K + 128, 2], F32).ap()
    rowb_hbm = nc.dram_tensor("rowb_hbm", [128, 1], F32).ap()

    p = Prog(nc)
    al = nc.alloc_sbuf_tensor
    W1b = al("W1b", [128, 8, 4096], BF16).ap()
    W2b = al("W2b", [128, 32, 1024], BF16).ap()
    wtmp = al("wtmpB", [128, 2048], F32).ap()
    _hbA = None
    s_b1 = al("s_b1", [128, 32], F32).ap()
    s_b2r = al("s_b2r", [1, 1024], F32).ap()
    b2b = al("b2b", [128, 1024], F32).ap()
    s_sc = al("s_sc", [128, 256], F32).ap()
    s_iota = al("s_iota", [128, 256], F32).ap()
    s_id = al("s_idB", [128, 128], F32).ap()
    identbB = al("identbB", [128, 128], BF16).ap()
    ones_rB = al("ones_rB", [1, 128], F32).ap()
    ones_cB = al("ones_cB", [128, 1], F32).ap()
    lo_c = al("lo_c", [128, 1], F32).ap()
    hi_c = al("hi_c", [128, 1], F32).ap()
    th_c = al("th_c", [128, 1], F32).ap()
    cnt_c = al("cnt_c", [128, 1], F32).ap()
    cgt_c = al("cgt_c", [128, 1], F32).ap()
    tmpa_c = al("tmpa_c", [128, 1], F32).ap()
    cmpb = al("cmpb", [128, 256], F32).ap()
    memb = al("memb", [128, 256], F32).ap()
    ws = al("ws", [128, 256], F32).ap()
    rowtot = al("rowtot", [128, 1], F32).ap()
    rowrow = al("rowrow", [1, 128], F32).ap()
    rowcum = al("rowcum", [1, 128], F32).ap()
    excl_c = al("excl_c", [128, 1], F32).ap()
    slots = al("slots", [128, 256], F32).ap()
    slot_i = al("slot_i", [128, 256], I32).ap()
    zero1B = al("zero1B", [1, 1], F32).ap()
    zcolB = al("zcolB", [128, 1], F32).ap()
    ovf_c = al("ovf_c", [128, 1], F32).ap()
    s_iotap = al("s_iotap", [128, 1], F32).ap()
    pairt2 = [al(f"pairt{s}", [128, 2], F32).ap() for s in range(4)]
    idcol2 = [al(f"idcol{s}", [128, 1], I32).ap() for s in range(4)]
    gcol2 = [al(f"gcol{s}", [128, 1], F32).ap() for s in range(4)]
    xg2 = [al(f"xg{s}", [128, 1024], BF16).ap() for s in range(2)]
    xgf2 = al("xgf2", [128, 8, 256], BF16).ap()
    xgf2b = al("xgf2b", [128, 8, 256], BF16).ap()
    hbuf2 = al("hbuf2", [128, 32, 256], BF16).ap()
    ybuf2 = [al(f"ybuf{s}", [128, 1024], BF16).ap() for s in range(4)]
    ytmp = al("ytmp", [128, 512], F32).ap()
    ytmp2 = al("ytmp2", [128, 512], F32).ap()
    ztile = al("ztile", [128, 4, 1024], BF16).ap()
    _hbA = wtmp.bitcast(BF16).rearrange("p (h x) -> p h x", h=16)
    _hbB = ztile.rearrange("p a b -> p (a b)").rearrange("p (h x) -> p h x", h=16)

    def hbuf2b(hc):
        return (_hbA if hc < 16 else _hbB)[:, hc % 16, :]

    psA = nc.alloc_psum_tensor("psA", [128, 512], F32).ap()
    psB = nc.alloc_psum_tensor("psB", [128, 512], F32).ap()
    psC = nc.alloc_psum_tensor("psC", [128, 512], F32).ap()
    psD = nc.alloc_psum_tensor("psD", [128, 512], F32).ap()
    psE = nc.alloc_psum_tensor("psE", [128, 512], F32).ap()
    psF = nc.alloc_psum_tensor("psF", [128, 512], F32).ap()
    psG = nc.alloc_psum_tensor("psG", [128, 512], F32).ap()
    psH = nc.alloc_psum_tensor("psH", [128, 512], F32).ap()

    for c in range(8):
        for hv in range(2):
            p.dma("sync", wtmp, W1[:, c, hv * 2048:(hv + 1) * 2048])
            p.copy(W1b[:, c, hv * 2048:(hv + 1) * 2048], wtmp)
    for c in range(32):
        p.dma("sync", wtmp[:, 0:1024], W2[:, c, :])
        p.copy(W2b[:, c, :], wtmp[:, 0:1024])
    p.dma("sync", s_b1, b1)
    p.dma("sync", s_b2r, b2)
    p.dma("sync", s_sc, sc_e)
    p.dma("sync", s_iota, iota_tok)
    p.dma("sync", s_iotap, iota_p)
    p.dma("sync", s_id, identd)
    p.copy(identbB, s_id)
    p.memset(ones_rB, 1.0)
    p.memset(ones_cB, 1.0)
    p.memset(zero1B, 0.0)
    p.memset(zcolB, 0.0)
    p.mm(psA[:, 0:1024].rearrange("p q -> p q") if False else psA, ones_rB, s_b2r[0:1, 0:512], tag="b2b0")
    p.copy(b2b[:, 0:512], psA)
    p.mm(psA, ones_rB, s_b2r[0:1, 512:1024], tag="b2b1")
    p.copy(b2b[:, 512:1024], psA)
    p.memset(ztile, 0.0)
    for t in range(64):
        p.dma("auto", Z[t * 512:(t + 1) * 512, :].rearrange("(j p) q -> p j q", p=128), ztile)

    p.fence_all()
    p.memset(lo_c, 0.0)
    p.memset(hi_c, 1.0)
    p.memset(th_c, 0.5)
    for it in range(40):
        p.ts(cmpb, s_sc, th_c[:, 0:1], None, ALU.is_gt)
        p.red(rowtot, cmpb, ALU.add)
        p.mm(psB[0:1, 0:1], rowtot, ones_cB, tag="cnt")
        p.mm(psC[:, 0:1], ones_rB, psB[0:1, 0:1], tag="cntb") if False else None
        p.copy(rowrow[0:1, 0:1], psB[0:1, 0:1])
        p.mm(psC[:, 0:1], ones_rB, rowrow[0:1, 0:1], tag="cntb")
        p.copy(cnt_c, psC[:, 0:1])
        p.ts(cgt_c, cnt_c, float(K), None, ALU.is_ge)
        p.tt(tmpa_c, th_c, lo_c, ALU.subtract)
        p.stt(lo_c, tmpa_c, cgt_c[:, 0:1], lo_c, ALU.mult, ALU.add)
        p.tt(tmpa_c, hi_c, th_c, ALU.subtract)
        p.stt(hi_c, tmpa_c, cgt_c[:, 0:1], th_c, ALU.mult, ALU.add)
        p.tt(th_c, lo_c, hi_c, ALU.add)
        p.ts(th_c, th_c, 0.5, None, ALU.mult)
    p.ts(memb, s_sc, lo_c[:, 0:1], None, ALU.is_gt)
    p.scan(ws, memb, zcolB.broadcast_to((128, 256)), 0.0, ALU.add, ALU.add)
    p.red(rowtot, memb, ALU.add)
    p.tr(psD[0:1, 0:128], rowtot, s_id, tag="c2rB")
    p.copy(rowrow, psD[0:1, 0:128], eng="scalar")
    p.scan(rowcum, rowrow, zero1B.broadcast_to((1, 128)), 0.0, ALU.add, ALU.add)
    p.tt(rowcum, rowcum, rowrow, ALU.subtract)
    p.dma("gpsimd", rowb_hbm.rearrange("s one -> one s"), rowcum)
    p.dma("sync", excl_c, rowb_hbm)
    p.ts(slots, ws, excl_c[:, 0:1], -1.0, ALU.add, ALU.add)
    p.ts(ovf_c, s_iotap, float(K), None, ALU.add)
    p.tt(cmpb, slots, ovf_c.broadcast_to((128, 256)), ALU.subtract)
    p.tt(cmpb, cmpb, memb, ALU.mult)
    p.tt(slots, cmpb, ovf_c.broadcast_to((128, 256)), ALU.add)
    p.copy(slot_i, slots)
    for f in range(256):
        pt = pairt2[f % 2]
        p.copy(pt[:, 0:1], s_iota[:, f:f + 1])
        p.copy(pt[:, 1:2], s_sc[:, f:f + 1])
        p.scatter(pair_hbm, pt, slot_i[:, f:f + 1], tag="scpair")
    for op in p.ops:
        if op.tag == "scpair":
            op.dma = True
    p.fence_all()
    for pr in range(NC // 2):
        par = pr % 2
        xgf_p = xgf2 if par == 0 else xgf2b
        pass
        ptr = psA if par == 0 else psF
        w1banks = (psB, psD) if par == 0 else (psG, psH)
        for s in range(2):
            ch = pr * 2 + s
            q = par * 2 + s
            p.dma("sync", pairt2[q], pair_hbm[ch * 128:(ch + 1) * 128, :])
            p.copy(idcol2[q], pairt2[q][:, 0:1])
            p.copy(gcol2[q], pairt2[q][:, 1:2])
            p.gather(xg2[s], xn_all, idcol2[q][:, 0:1], tag="gx")
            for c in range(8):
                p.tr(ptr.bitcast(BF16)[:, 0:128], xg2[s][:, c * 128:(c + 1) * 128], identbB)
                p.copy(xgf_p[:, c, s * 128:(s + 1) * 128], ptr.bitcast(BF16)[:, 0:128])
        def hb_p(hc, par=par):
            return hbuf2[:, hc, :] if par == 0 else hbuf2b(hc)
        for hc in range(32):
            pb = w1banks[hc % 2]
            for c in range(8):
                p.mm(pb[:, 0:256], W1b[:, c, hc * 128:(hc + 1) * 128], xgf_p[:, c, :],
                     start=(c == 0), stop=(c == 7))
            p.act(hb_p(hc), pb[:, 0:256], AF.Gelu, bias=s_b1[:, hc:hc + 1])
        for s in range(2):
            q = par * 2 + s
            for half in range(2):
                pc = psC if half == 0 else psE
                yt = ytmp if half == 0 else ytmp2
                for hc in range(32):
                    p.mm(pc, hb_p(hc)[:, s * 128:(s + 1) * 128],
                         W2b[:, hc, half * 512:(half + 1) * 512],
                         start=(hc == 0), stop=(hc == 31))
                p.tt(yt, pc, b2b[:, half * 512:(half + 1) * 512], ALU.add)
                p.ts(ybuf2[q][:, half * 512:(half + 1) * 512], yt, gcol2[q][:, 0:1], None, ALU.mult)
            p.scatter(Z, ybuf2[q], idcol2[q][:, 0:1], tag="scz")
    p.emit()
    return nc


def run_phaseB(inputs, xn_all, scores_all, n_cores=8):
    w = inputs
    cB = hostB_consts()
    nc = build_phaseB()
    in_maps = []
    for e in range(n_cores):
        m = dict(cB)
        m.update(wB_prep(w, e))
        m["xn_all"] = xn_all
        m["sc_e"] = np.ascontiguousarray(scores_all[:, e].reshape(128, 256))
        in_maps.append(m)
    res = run_bass_kernel_spmd(nc, in_maps, core_ids=list(range(n_cores)))
    return res.results



def wC_prep(w):
    Wq = np.asarray(w["L1_attn_Wqkv"], np.float32)[:, 0:1024]
    return {
        "wqkv_q": np.ascontiguousarray(Wq.reshape(8, 128, 1024).transpose(1, 0, 2)),
        "bq_c": np.ascontiguousarray(np.asarray(w["L1_attn_bqkv"], np.float32)[0:1024].reshape(8, 128).T),
        "wlif": np.asarray(w["L1_attn_Wlif"], np.float32),
        "blif_r": np.asarray(w["L1_attn_blif"], np.float32).reshape(1, 16),
    }


def build_phaseC():
    nc = bass.Bass("TRN2", target_bir_lowering=False, name="phC")
    B, S, D = 2, 2048, 1024
    x = nc.dram_tensor("x", [B * S, D], F32, kind="ExternalInput").ap()
    s0 = nc.dram_tensor("s0_fm", [B, D, S], F32, kind="ExternalInput").ap()
    Zp = nc.dram_tensor("Zp", [8, B * S, D], BF16, kind="ExternalInput").ap()
    wq = nc.dram_tensor("wqkv_q", [128, 8, 1024], F32, kind="ExternalInput").ap()
    bq = nc.dram_tensor("bq_c", [128, 8], F32, kind="ExternalInput").ap()
    wlif = nc.dram_tensor("wlif", [16, 64], F32, kind="ExternalInput").ap()
    blif = nc.dram_tensor("blif_r", [1, 16], F32, kind="ExternalInput").ap()
    identd = nc.dram_tensor("identC", [128, 128], F32, kind="ExternalInput").ap()
    x1_out = nc.dram_tensor("x1_fm", [B, D, S], F32, kind="ExternalOutput").ap()
    xn1_out = nc.dram_tensor("xn1_fm", [B, D, S], BF16, kind="ExternalOutput").ap()
    xn1f_out = nc.dram_tensor("xn1f_fm", [B, D, S], F32, kind="ExternalOutput").ap()
    spk_out = nc.dram_tensor("spikes", [32, S], F32, kind="ExternalOutput").ap()
    I_hbm = nc.dram_tensor("I_hbm", [32, S], F32).ap()

    p = Prog(nc)
    al = nc.alloc_sbuf_tensor
    s_id = al("s_idC", [128, 128], F32).ap()
    wqb = al("wqbC", [128, 8, 1024], BF16).ap()
    wtmp = al("wtmpC", [128, 1024], F32).ap()
    s_bq = al("s_bq", [128, 8], F32).ap()
    s_wlif = al("s_wlif", [16, 64], F32).ap()
    wlifT2 = al("wlifT2", [128, 16], F32).ap()
    s_blif = al("s_blif", [1, 16], F32).ap()
    eps5C = al("eps5C", [128, 1], F32).ap()
    xt = al("xtC", [128, 1024], F32).ap()
    zt = al("ztC", [128, 1024], BF16).ap()
    acc = al("accC", [128, 1024], F32).ap()
    g0 = al("g0C", [128, 1024], F32).ap()
    s0tm = al("s0tm", [128, 1024], F32).ap()
    x1t = al("x1t", [128, 1024], F32).ap()
    xn1t = al("xn1t", [128, 1024], F32).ap()
    fmt = al("fmtC", [128, 1024], F32).ap()
    fmb = al("fmbC", [128, 1024], BF16).ap()
    fmt2 = al("fmt2C", [128, 1024], F32).ap()
    st1 = al("st1C", [128, 1], F32).ap()
    st2 = al("st2C", [128, 1], F32).ap()
    xnf = al("xnfC", [128, 8, 512], BF16).ap()
    qe = al("qeC", [128, 512], F32).ap()
    irow = al("irowC", [1, 512], F32).ap()
    Iall = al("IallC", [32, 2048], F32).ap()
    Mtraj = al("MtrajC", [32, 2048], F32).ap()
    mcol = al("mcolC", [32, 1], F32).ap()
    spkt = al("spktC", [32, 2048], F32).ap()

    psA = nc.alloc_psum_tensor("psA", [128, 512], F32).ap()
    psB = nc.alloc_psum_tensor("psB", [128, 512], F32).ap()
    psE = nc.alloc_psum_tensor("psE", [128, 512], F32).ap()

    p.dma("sync", s_id, identd)
    for c in range(8):
        p.dma("sync", wtmp, wq[:, c, :])
        p.copy(wqb[:, c, :], wtmp)
    p.dma("sync", s_bq, bq)
    p.dma("sync", s_wlif, wlif)
    p.dma("sync", s_blif, blif)
    p.memset(eps5C, 1e-5)
    p.tr(psE[0:64, 0:16], s_wlif, s_id[0:16, 0:16], tag="trwlif")
    p.copy(wlifT2[0:64, :], psE[0:64, 0:16])
    p.copy(wlifT2[64:128, :], psE[0:64, 0:16])

    NT = 16
    for b in range(B):
        for t in range(NT):
            row0 = b * S + t * 128
            p.dma("sync", xt, x[row0:row0 + 128, :])
            p.dma("sync", zt, Zp[0, row0:row0 + 128, :])
            p.copy(acc, zt)
            for e in range(1, 8):
                p.dma("sync", zt, Zp[e, row0:row0 + 128, :])
                p.tt(acc, acc, zt, ALU.add)
            p.act(g0, acc, AF.Silu)
            p.dma("sync", fmt.rearrange("p (c q) -> p c q", c=8),
                  s0[b].rearrange("(c p) s -> p c s", p=128)[:, :, t * 128:(t + 1) * 128])
            for c in range(8):
                p.tr(psA[:, 0:128], fmt[:, c * 128:(c + 1) * 128], s_id, tag="trs0")
                p.copy(s0tm[:, c * 128:(c + 1) * 128], psA[:, 0:128])
            p.tt(x1t, s0tm, g0, ALU.mult)
            p.tt(x1t, x1t, xt, ALU.add)
            p.act(fmt, x1t, AF.Identity, accum_out=st1)
            p.ts(st1, st1, 1.0 / D, None, ALU.mult)
            p.ts(xn1t, x1t, st1[:, 0:1], None, ALU.subtract)
            p.act(fmt, xn1t, AF.Square, accum_out=st2)
            p.ts(st2, st2, 1.0 / D, None, ALU.mult)
            p.act(st2, st2, AF.Ln, bias=eps5C[:, 0:1])
            p.act(st2, st2, AF.Exp, scale=-0.5)
            p.ts(xn1t, xn1t, st2[:, 0:1], None, ALU.mult)
            for c in range(8):
                p.tr(psA[:, 0:128], x1t[:, c * 128:(c + 1) * 128], s_id)
                p.copy(fmt[:, c * 128:(c + 1) * 128], psA[:, 0:128])
                p.tr(psB[:, 0:128], xn1t[:, c * 128:(c + 1) * 128], s_id)
                p.copy(fmb[:, c * 128:(c + 1) * 128], psB[:, 0:128])
                p.copy(fmt2[:, c * 128:(c + 1) * 128], psB[:, 0:128])
            p.dma("gpsimd", x1_out[b].rearrange("(c p) s -> p c s", p=128)[:, :, t * 128:(t + 1) * 128],
                  fmt.rearrange("p (c q) -> p c q", c=8))
            p.dma("gpsimd", xn1_out[b].rearrange("(c p) s -> p c s", p=128)[:, :, t * 128:(t + 1) * 128],
                  fmb.rearrange("p (c q) -> p c q", c=8))
            p.dma("gpsimd", xn1f_out[b].rearrange("(c p) s -> p c s", p=128)[:, :, t * 128:(t + 1) * 128],
                  fmt2.rearrange("p (c q) -> p c q", c=8))
    p.fence_all()
    for b in range(B):
        for tc in range(4):
            tsl = slice(tc * 512, (tc + 1) * 512)
            p.dma("sync", xnf, xn1_out[b].rearrange("(c p) s -> p c s", p=128)[:, :, tsl])
            for e in range(8):
                for c in range(8):
                    p.mm(psA, wqb[:, c, e * 128:(e + 1) * 128], xnf[:, c, :],
                         start=(c == 0), stop=(c == 7))
                p.ts(qe, psA, s_bq[:, e:e + 1], None, ALU.add)
                for hh in range(2):
                    h = 2 * e + hh
                    hp = 64 * hh
                    p.mm(psB[0:1, :], wlifT2[hp:hp + 64, h:h + 1], qe[hp:hp + 64, :], tag="iq")
                    p.ts(irow, psB[0:1, :], s_blif[0:1, h:h + 1], None, ALU.add)
                    p.dma("gpsimd", I_hbm[b * 16 + h:b * 16 + h + 1, tsl], irow)
    p.fence_all()
    p.dma("sync", Iall, I_hbm)
    p.memset(mcol, 0.0)
    for t in range(2048):
        p.stt(Mtraj[:, t:t + 1], mcol, 0.9, Iall[:, t:t + 1], ALU.mult, ALU.add)
        p.stt(mcol, Mtraj[:, t:t + 1], 1.0, Mtraj[:, t:t + 1], ALU.is_lt, ALU.mult)
    p.ts(spkt, Mtraj, 1.0, None, ALU.is_ge)
    p.dma("gpsimd", spk_out, spkt)
    p.emit()
    return nc


def run_phaseC(inputs, x_shards, s0_shards, Zall, n_cores=8):
    w = inputs
    wC = wC_prep(w)
    nc = build_phaseC()
    in_maps = []
    for c in range(n_cores):
        m = dict(wC)
        m["identC"] = np.eye(128, dtype=np.float32)
        m["x"] = x_shards[c]
        m["s0_fm"] = s0_shards[c]
        m["Zp"] = np.ascontiguousarray(Zall[:, c * 4096:(c + 1) * 4096, :])
        in_maps.append(m)
    res = run_bass_kernel_spmd(nc, in_maps, core_ids=list(range(n_cores)))
    return res.results



def hostC2_consts():
    cos, sin, rsc = _rot_tables()
    i = np.arange(128)
    return {
        "cos_rsc": np.ascontiguousarray((cos * rsc).astype(np.float32)),
        "sin_rsc": np.ascontiguousarray((sin * rsc).astype(np.float32)),
        "cos_div": np.ascontiguousarray((cos / rsc).astype(np.float32)),
        "sin_div": np.ascontiguousarray((sin / rsc).astype(np.float32)),
        "identC2": np.eye(128, dtype=np.float32),
        "causal_addC2": np.where(i[None, :] <= i[:, None], 0.0, NEG).astype(np.float32),
        "causalT_addC2": np.where(i[:, None] <= i[None, :], 0.0, NEG).astype(np.float32),
        "cam512C2": np.concatenate(
            [np.where(i[:, None] <= i[None, :], 0.0, NEG).astype(np.float32),
             np.zeros((128, 384), np.float32)], axis=1),
        "cat4C2": np.tile(np.where(i[None, :] <= i[:, None], 0.0, NEG).astype(np.float32), (1, 4)),
        "catzC2": np.concatenate(
            [np.where(i[:, None] <= i[None, :], 0.0, NEG).astype(np.float32),
             np.zeros((128, 384), np.float32)], axis=1),
        "zcaC2": np.concatenate(
            [np.zeros((128, 384), np.float32),
             np.where(i[None, :] <= i[:, None], 0.0, NEG).astype(np.float32)], axis=1),
    }


def _rot_tables(n=2048, DH=64, WINDOW=512):
    half = np.arange(0, DH, 2, dtype=np.float32)
    inv_freq = 1.0 / 10000.0 ** (half / DH)
    t = np.arange(n, dtype=np.float32)
    freqs = t[:, None] * inv_freq[None, :]
    freqs = np.concatenate([freqs, freqs], axis=-1)
    base = (half + 0.4 * DH) / (1.4 * DH)
    power = (t - n // 2) / float(WINDOW // 2)
    scale = base[None, :] ** power[:, None]
    scale = np.concatenate([scale, scale], axis=-1).astype(np.float32)
    return np.cos(freqs).astype(np.float32), np.sin(freqs).astype(np.float32), scale


def wC2_prep(w):
    d = {}
    for l in range(2):
        d[f"wqkv1_{l}"] = np.ascontiguousarray(np.asarray(w["L1_ffn_wqkv"][l], np.float32).reshape(8, 128, 1536).transpose(1, 0, 2))
        d[f"wo1_{l}"] = np.ascontiguousarray(np.asarray(w["L1_ffn_wo"][l], np.float32).reshape(4, 128, 1024).transpose(1, 0, 2))
        w1 = np.asarray(w["L1_ffn_w1"][l], np.float32)
        w1a = np.zeros((1024, 2816), np.float32); w1a[:, :2730] = w1[:, :2730]
        w1g = np.zeros((1024, 2816), np.float32); w1g[:, :2730] = w1[:, 2730:]
        d[f"w1a_{l}"] = np.ascontiguousarray(w1a.reshape(8, 128, 2816).transpose(1, 0, 2))
        d[f"w1g_{l}"] = np.ascontiguousarray(w1g.reshape(8, 128, 2816).transpose(1, 0, 2))
        w2 = np.zeros((2816, 1024), np.float32); w2[:2730] = np.asarray(w["L1_ffn_w2"][l], np.float32)
        d[f"w2_{l}"] = np.ascontiguousarray(w2.reshape(22, 128, 1024).transpose(1, 0, 2))
        b1 = np.asarray(w["L1_ffn_b1"][l], np.float32)
        b1a = np.zeros(2816, np.float32); b1a[:2730] = b1[:2730]
        b1g = np.zeros(2816, np.float32); b1g[:2730] = b1[2730:]
        d[f"b1a_{l}"] = np.ascontiguousarray(b1a.reshape(22, 128).T)
        d[f"b1g_{l}"] = np.ascontiguousarray(b1g.reshape(22, 128).T)
        d[f"b2_{l}"] = np.ascontiguousarray(np.asarray(w["L1_ffn_b2"][l], np.float32).reshape(8, 128).T)
        d[f"lg_{l}"] = np.ascontiguousarray(np.asarray(w["L1_ffn_ln_g"][l], np.float32).reshape(8, 128).T)
        d[f"lb_{l}"] = np.ascontiguousarray(np.asarray(w["L1_ffn_ln_b"][l], np.float32).reshape(8, 128).T)
        d[f"qs_{l}"] = np.asarray(w["L1_ffn_q_scale"][l], np.float32).reshape(1, 64)
        d[f"ks_{l}"] = np.asarray(w["L1_ffn_k_scale"][l], np.float32).reshape(1, 64)
    return d


def build_phaseC2():
    nc = bass.Bass("TRN2", target_bir_lowering=False, name="phC2")
    B, S, D = 2, 2048, 1024
    INNER, H = 512, 8
    xn1f = nc.dram_tensor("xn1f_fm", [B, D, S], F32, kind="ExternalInput").ap()
    ins = {}
    for l in range(2):
        for nm, shp in ((f"wqkv1_{l}", [128, 8, 1536]), (f"wo1_{l}", [128, 4, 1024]),
                        (f"w1a_{l}", [128, 8, 2816]), (f"w1g_{l}", [128, 8, 2816]),
                        (f"w2_{l}", [128, 22, 1024]), (f"b1a_{l}", [128, 22]),
                        (f"b1g_{l}", [128, 22]), (f"b2_{l}", [128, 8]),
                        (f"lg_{l}", [128, 8]), (f"lb_{l}", [128, 8]),
                        (f"qs_{l}", [1, 64]), (f"ks_{l}", [1, 64])):
            ins[nm] = nc.dram_tensor(nm, shp, F32, kind="ExternalInput").ap()
    for nm, shp in (("cos_rsc", [S, 64]), ("sin_rsc", [S, 64]), ("cos_div", [S, 64]),
                    ("sin_div", [S, 64]), ("identC2", [128, 128]),
                    ("causal_addC2", [128, 128]), ("causalT_addC2", [128, 128]),
                    ("cam512C2", [128, 512]), ("cat4C2", [128, 512]),
                    ("catzC2", [128, 512]), ("zcaC2", [128, 512])):
        ins[nm] = nc.dram_tensor(nm, shp, F32, kind="ExternalInput").ap()
    g1_out = nc.dram_tensor("g1_fm", [B, D, S], BF16, kind="ExternalOutput").ap()
    xfa = nc.dram_tensor("xfa", [B, D, S], F32).ap()
    xfb = nc.dram_tensor("xfb", [B, D, S], F32).ap()
    sh_hbm = nc.dram_tensor("sh_hbm", [B, D, S], BF16).ap()
    shf_hbm = nc.dram_tensor("shf_hbm", [B, D, S], F32).ap()
    qf_hbm = nc.dram_tensor("qf_hbm", [B, INNER, S], BF16).ap()
    kf_hbm = nc.dram_tensor("kf_hbm", [B, INNER, S], BF16).ap()
    vf_hbm = nc.dram_tensor("vf_hbm", [B, S, INNER], BF16).ap()
    tab_hbm = nc.dram_tensor("tab_hbm", [4, S, 64], F32).ap()
    o_hbm = nc.dram_tensor("o_hbm", [INNER, S], BF16).ap()


    p = Prog(nc)
    al = nc.alloc_sbuf_tensor
    warena = al("warena", [128, 67584], BF16).ap()
    s_id = al("s_idC2", [128, 128], F32).ap()
    identb = al("identbC2", [128, 128], BF16).ap()
    s_ca = al("s_caC2", [128, 128], F32).ap()
    s_cat = al("s_catC2", [128, 128], F32).ap()
    s_cam = al("s_camC2", [128, 512], F32).ap()
    s_cat4 = al("s_cat4C2", [128, 512], F32).ap()
    s_catz = al("s_catzC2", [128, 512], F32).ap()
    s_zca = al("s_zcaC2", [128, 512], F32).ap()
    wtmp = None
    ones_r = al("ones_rC2", [1, 128], F32).ap()
    ones_c = al("ones_cC2", [128, 1], F32).ap()
    eps5 = al("eps5C2", [128, 1], F32).ap()
    qsb = al("qsbC2", [128, 64], F32).ap()
    ksb = al("ksbC2", [128, 64], F32).ap()
    rotq = al("rotqC2", [128, 64], F32).ap()
    rotk = al("rotkC2", [128, 64], F32).ap()
    zc = al("zcC2", [128, 4], BF16).ap()
    zcf = al("zcfC2", [128, 4], F32).ap()
    s_small = {}
    for l in range(2):
        for nm in (f"b1a_{l}", f"b1g_{l}"):
            s_small[nm] = al("s_" + nm, [128, 22], F32).ap()
        for nm in (f"b2_{l}", f"lg_{l}", f"lb_{l}"):
            s_small[nm] = al("s_" + nm, [128, 8], F32).ap()
        for nm in (f"qs_{l}", f"ks_{l}"):
            s_small[nm] = al("s_" + nm, [1, 64], F32).ap()
    shj = None
    sarena = al("sarenaC2", [128, 8704], F32).ap()
    tabraw = sarena[:, 0:1024].rearrange("p (t d) -> p t d", t=16)
    shj = sarena.bitcast(BF16)[:, 6144:7168].rearrange("p (c q) -> p c q", c=8)
    wtmp = sarena[:, 7040:8448]
    qkvt = sarena[:, 0:1536]
    rotb = sarena[:, 1536:2048]
    tmpa = sarena[:, 2048:2560]
    tmpb = sarena[:, 2560:3072]
    ssb = al("ssbC2", [128, 8], F32).ap()
    nrm = al("nrmC2", [128, 8], F32).ap()
    qout = al("qoutC2", [128, 512], BF16).ap()
    qfm_c = al("qfm_cC2", [128, 2048], BF16).ap()
    kfm_c = al("kfm_cC2", [128, 2048], BF16).ap()
    o4 = al("o4C2", [128, 4, 512], BF16).ap()
    sml = sarena[:, 3072:3584]
    m_a = al("m_aC2", [128, 1], F32).ap()
    m_b = al("m_bC2", [128, 1], F32).ap()
    m_col = al("m_colC2", [128, 4], F32).ap()
    rowbuf = al("rowbufC2", [1, 1536], F32).ap()
    mneg_r = rowbuf[0:1, 0:512]
    den_r = rowbuf[0:1, 512:1024]
    fac_r = rowbuf[0:1, 1024:1536]
    mb_s = sarena[:, 4096:4608]
    cmb = sarena[:, 3584:4096]
    xcur = al("xcurC2", [128, 512], F32).ap()
    sqt = sarena[:, 4608:5120]
    sbf = sarena.bitcast(BF16)
    glu = sbf[:, 0:11264].rearrange("p (h x) -> p h x", h=22)
    hln = sbf[:, 11264:15360].rearrange("p (h x) -> p h x", h=8)
    abuf = sarena[:, 7680:8192]
    gbuf = sarena[:, 8192:8704]
    mub = qfm_c.bitcast(F32)[:, 0:512]
    rsb = qfm_c.bitcast(F32)[:, 512:1024]
    msq = kfm_c.bitcast(F32)[:, 0:512]
    abuf2 = kfm_c.bitcast(F32)[:, 512:1024]
    gbuf2 = o4.bitcast(F32).rearrange("p a b -> p (a b)")[:, 0:512]
    stat = rowbuf

    psA = nc.alloc_psum_tensor("psA", [128, 512], F32).ap()
    psB = nc.alloc_psum_tensor("psB", [128, 512], F32).ap()
    psC = nc.alloc_psum_tensor("psC", [128, 512], F32).ap()
    psD = nc.alloc_psum_tensor("psD", [128, 512], F32).ap()
    psE = nc.alloc_psum_tensor("psE", [128, 512], F32).ap()
    psF = nc.alloc_psum_tensor("psF", [128, 512], F32).ap()
    psG = nc.alloc_psum_tensor("psG", [128, 512], F32).ap()
    psH = nc.alloc_psum_tensor("psH", [128, 512], F32).ap()

    _dbase = [16384]

    def dal(nb):
        bb = (_dbase[0] + 3) & ~3
        _dbase[0] = bb + nb
        assert _dbase[0] <= 67584, _dbase[0]
        return bb

    wscf = warena.bitcast(F32)

    def dbf(n):
        bb = dal(n)
        return warena[:, bb:bb + n]

    def df32(n):
        bb = dal(2 * n)
        return wscf[:, bb // 2:bb // 2 + n]

    shj2 = [dbf(1024).rearrange("p (c q) -> p c q", c=8) for _ in range(2)]
    qkvt2 = [df32(1536) for _ in range(2)]
    sqt2 = [df32(512) for _ in range(2)]
    rotb2 = [df32(512) for _ in range(2)]
    tmpa2 = [df32(512) for _ in range(2)]
    tmpb2 = [df32(512) for _ in range(2)]
    tab42 = [df32(256).rearrange("p (t d) -> p t d", t=4) for _ in range(2)]
    qout2 = [dbf(512) for _ in range(2)]
    trc42 = [dbf(512).rearrange("p (c q) -> p c q", c=4) for _ in range(2)]
    ssb2 = [df32(8) for _ in range(2)]
    nrm2 = [df32(8) for _ in range(2)]
    qkb2 = [dbf(1024) for _ in range(2)]
    vb2 = [dbf(512) for _ in range(2)]
    vtmh2 = [dbf(1024).rearrange("p (t e) -> p t e", t=16) for _ in range(2)]
    smlj2 = [[df32(512) for _ in range(4)] for _ in range(2)]
    mb_s2 = [df32(512) for _ in range(2)]
    pTrow2 = [dbf(2560).rearrange("p (s x) -> p s x", s=4) for _ in range(2)]
    ptile2 = [dbf(1024).rearrange("p (s x) -> p s x", s=8) for _ in range(2)]
    obuf2 = [dbf(512) for _ in range(2)]
    m_a2 = [df32(4) for _ in range(2)]
    m_b2_ = [df32(4) for _ in range(2)]
    m_col2 = [df32(4) for _ in range(2)]
    mneg82 = [df32(4) for _ in range(2)]
    den_a2 = [df32(4) for _ in range(2)]
    den_b2 = [df32(4) for _ in range(2)]
    fac42 = [df32(4) for _ in range(2)]
    xcur2 = [df32(512) for _ in range(2)]
    o4b = dbf(2048).rearrange("p (c s) -> p c s", c=4)

    def zipper(bodyA, bodyB):
        mk = len(p.ops)
        bodyA()
        A = p.ops[mk:]
        del p.ops[mk:]
        bodyB()
        Bl = p.ops[mk:]
        del p.ops[mk:]
        ia = ib = 0
        while ia < len(A) or ib < len(Bl):
            if ia < len(A):
                p.ops.append(A[ia])
                ia += 1
            if ib < len(Bl):
                p.ops.append(Bl[ib])
                ib += 1

    p.dma("sync", s_id, ins["identC2"])
    p.copy(identb, s_id)
    p.dma("sync", s_ca, ins["causal_addC2"])
    p.dma("sync", s_cat, ins["causalT_addC2"])
    p.dma("sync", s_cam, ins["cam512C2"])
    p.dma("sync", s_cat4, ins["cat4C2"])
    p.dma("sync", s_catz, ins["catzC2"])
    p.dma("sync", s_zca, ins["zcaC2"])
    p.memset(ones_r, 1.0)
    p.memset(ones_c, 1.0)
    p.memset(eps5, 1e-5)
    p.memset(zc, 0.0)
    p.memset(zcf, 0.0)
    for l in range(2):
        for nm in (f"b1a_{l}", f"b1g_{l}", f"b2_{l}", f"lg_{l}", f"lb_{l}", f"qs_{l}", f"ks_{l}"):
            p.dma("sync", s_small[nm], ins[nm])

    def wv(off, sub, width):
        return warena[:, off + sub * width:off + (sub + 1) * width]
    OFF_QKV = 0
    OFF_WO = 8 * 1536
    OFF_W1A = 0
    OFF_W1G = 8 * 2816
    OFF_W2 = 16 * 2816

    def shift_copy(dst, srcb, b, cast):
        p.dma("gpsimd" if cast else "sync", dst[b, 0:512, :], srcb[b, 0:512, :])
        p.dma("gpsimd" if cast else "sync", dst[b, 512:1024, 1:2048], srcb[b, 512:1024, 0:2047])
        zz = zc if cast else zcf
        p.dma("gpsimd", dst[b, 512:1024, 0:1].rearrange("(a c) one -> a (c one)", c=4), zz, slow=True)

    for l in range(2):
        src = xn1f if l == 0 else xfb
        for c in range(8):
            for hv in range(2):
                p.dma("sync", wtmp[:, 0:768], ins[f"wqkv1_{l}"][:, c, hv * 768:(hv + 1) * 768])
                p.copy(wv(OFF_QKV, c, 1536)[:, hv * 768:(hv + 1) * 768], wtmp[:, 0:768])
        for c in range(4):
            p.dma("sync", wtmp[:, 0:1024], ins[f"wo1_{l}"][:, c, :])
            p.copy(wv(OFF_WO, c, 1024), wtmp[:, 0:1024])
        p.mm(psA[:, 0:64], ones_r, s_small[f"qs_{l}"], tag="qsb")
        p.copy(qsb, psA[:, 0:64])
        p.mm(psA[:, 0:64], ones_r, s_small[f"ks_{l}"], tag="ksb")
        p.copy(ksb, psA[:, 0:64])
        p.copy(rotq[:, 0:32], qsb[:, 32:64])
        p.copy(rotq[:, 32:64], qsb[:, 0:32])
        p.copy(rotk[:, 0:32], ksb[:, 32:64])
        p.copy(rotk[:, 32:64], ksb[:, 0:32])
        for tbl, cstn, mulb in ((0, "cos_rsc", qsb), (1, "sin_rsc", rotq),
                                (2, "cos_div", ksb), (3, "sin_div", rotk)):
            p.dma("sync", tabraw, ins[cstn].rearrange("(t p) d -> p t d", p=128))
            for tt_ in range(16):
                p.tt(tabraw[:, tt_, :], tabraw[:, tt_, :], mulb, ALU.mult)
            p.dma("gpsimd", tab_hbm[tbl].rearrange("(t p) d -> p t d", p=128), tabraw)
        for b in range(B):
            shift_copy(sh_hbm, src, b, cast=True)

            def qkv_body(t):
                r = t % 2
                shj_ = shj2[r]
                tab4_ = tab42[r]
                qkvt_ = qkvt2[r]
                sqt_ = sqt2[r]
                ssb_ = ssb2[r]
                nrm_ = nrm2[r]
                rotb_ = rotb2[r]
                tmpa_ = tmpa2[r]
                tmpb_ = tmpb2[r]
                qkb_ = qkb2[r]
                vb_ = vb2[r]
                trc4_ = trc42[r]
                ps3 = ((psA, psB, psC), (psD, psE, psG))[r]
                p.dma("sync", shj_, sh_hbm[b].rearrange("(c p) s -> p c s", p=128)[:, :, t * 128:(t + 1) * 128])
                p.dma("scalar", tab4_, tab_hbm[:, t * 128:(t + 1) * 128, :].rearrange("tb p d -> p tb d"))
                for third in range(3):
                    ps = ps3[third]
                    for c in range(8):
                        p.mm(ps, shj_[:, c, :],
                             wv(OFF_QKV, c, 1536)[:, third * 512:(third + 1) * 512],
                             start=(c == 0), stop=(c == 7))
                p.copy(qkvt_[:, 0:512], ps3[0], eng="scalar")
                p.copy(qkvt_[:, 512:1024], ps3[1], eng="scalar")
                p.copy(vb_, ps3[2])
                p.dma("gpsimd", vf_hbm[b, t * 128:(t + 1) * 128, :], vb_)
                for base, scb, t1, t2 in ((0, qsb, 0, 1), (512, ksb, 2, 3)):
                    seg = qkvt_[:, base:base + 512]
                    segv = seg.rearrange("p (h x) -> p h x", h=8)
                    p.act(sqt_, seg, AF.Square)
                    p.red(ssb_, sqt_.rearrange("p (h x) -> p h x", h=8), ALU.add)
                    p.act(nrm_, ssb_, AF.Sqrt)
                    p.ts(nrm_, nrm_, 1e-12, None, ALU.max)
                    p.recip(nrm_, nrm_)
                    for h in range(8):
                        if h % 2 == 0:
                            p.ts(seg[:, h * 64:(h + 1) * 64], seg[:, h * 64:(h + 1) * 64],
                                 nrm_[:, h:h + 1], None, ALU.mult, eng="gpsimd")
                        else:
                            p.act(seg[:, h * 64:(h + 1) * 64], seg[:, h * 64:(h + 1) * 64],
                                  AF.Identity, scale=nrm_[:, h:h + 1])
                    rotv = rotb_.rearrange("p (h x) -> p h x", h=8)
                    p.ts(rotv[:, :, 0:32], segv[:, :, 32:64], -1.0, None, ALU.mult, eng="gpsimd")
                    p.copy(rotv[:, :, 32:64], segv[:, :, 0:32], eng="gpsimd")
                    tb1 = tab4_[:, t1, :].unsqueeze(1).broadcast_to((128, 8, 64))
                    tb2 = tab4_[:, t2, :].unsqueeze(1).broadcast_to((128, 8, 64))
                    p.tt(tmpa_.rearrange("p (h x) -> p h x", h=8), segv, tb1, ALU.mult)
                    p.tt(tmpb_.rearrange("p (h x) -> p h x", h=8), rotv, tb2, ALU.mult, eng="gpsimd")
                    p.tt(qkb_[:, base:base + 512], tmpa_, tmpb_, ALU.add)
                for base, dsth in ((0, qf_hbm), (512, kf_hbm)):
                    for c in range(4):
                        pcol = r * 512 + c * 128
                        p.tr(psH.bitcast(BF16)[:, pcol:pcol + 128],
                             qkb_[:, base + c * 128:base + (c + 1) * 128], identb)
                        p.copy(trc4_[:, c, :], psH.bitcast(BF16)[:, pcol:pcol + 128])
                    p.dma("gpsimd", dsth[b, 0:512, t * 128:(t + 1) * 128].rearrange("(c p) q -> p c q", p=128),
                          trc4_)

            zipper(lambda: [qkv_body(t) for t in range(0, 16, 2)],
                   lambda: [qkv_body(t) for t in range(1, 16, 2)])

            def att_body(hc, hh):
                h = 2 * hc + hh
                hp = 64 * hh
                pS = ((psA, psB), (psD, psE))[hh]
                pav = (psC, psG)[hh]
                ptr = (psF, psH)[hh]
                vtmh_ = vtmh2[hh]
                smlj = smlj2[hh]
                smld = mb_s2[hh]
                pTrow_ = pTrow2[hh]
                ptile_ = ptile2[hh]
                obuf_ = obuf2[hh]
                m_a4 = m_a2[hh]
                m_d4 = m_b2_[hh]
                m_col = m_col2[hh]
                mneg8 = mneg82[hh]
                den_a = den_a2[hh]
                den_b = den_b2[hh]
                fac4 = fac42[hh]
                p.dma("sync", vtmh_, vf_hbm[b].rearrange("(t p) e -> p t e", p=128)[:, :, h * 64:(h + 1) * 64])
                tr_slot = [0]
                for w in range(4):
                    qa = w * 512
                    for j in range(4):
                        q0 = (w * 4 + j) * 128
                        p.mm(pav[:, j * 128:(j + 1) * 128], qfm_c[hp:hp + 64, q0:q0 + 128],
                             kfm_c[hp:hp + 64, q0:q0 + 128])
                    p.stt(smld, pav, 1.0, s_cat4, ALU.mult, ALU.add)
                    if w == 0:
                        p.memset(m_a4[:, 0:1], NEG)
                        p.memset(den_a[:, 0:1], 0.0)
                        for j in range(1, 4):
                            q0 = j * 128
                            p.mm(pS[j % 2][:, 0:q0], qfm_c[hp:hp + 64, q0:q0 + 128],
                                 kfm_c[hp:hp + 64, 0:q0])
                            p.copy(smlj[j][:, 0:q0], pS[j % 2][:, 0:q0], eng="scalar")
                            p.red(m_a4[:, j:j + 1], smlj[j][:, 0:q0], ALU.max)
                    else:
                        for j in range(4):
                            q0 = (w * 4 + j) * 128
                            p.mm(pS[j % 2], qfm_c[hp:hp + 64, q0:q0 + 128],
                                 kfm_c[hp:hp + 64, q0 - 512:q0])
                            p.stt(smlj[j], pS[j % 2], 1.0, s_cam, ALU.mult, ALU.add)
                            p.red(m_a4[:, j:j + 1], smlj[j], ALU.max)
                    p.red(m_d4, smld.rearrange("p (j x) -> p j x", j=4), ALU.max)
                    p.tt(m_col, m_a4, m_d4, ALU.max)
                    p.ts(mneg8, m_col, -8.0, None, ALU.mult)
                    for j in range(4):
                        if w == 0:
                            if j > 0:
                                p.act(pTrow_[:, j, 0:j * 128], smlj[j][:, 0:j * 128], AF.Exp,
                                      bias=mneg8[:, j:j + 1], scale=8.0, accum_out=den_a[:, j:j + 1])
                        else:
                            p.act(pTrow_[:, j, 0:512], smlj[j], AF.Exp,
                                  bias=mneg8[:, j:j + 1], scale=8.0, accum_out=den_a[:, j:j + 1])
                        p.act(pTrow_[:, j, 512:640], smld[:, j * 128:(j + 1) * 128], AF.Exp,
                              bias=mneg8[:, j:j + 1], scale=8.0, accum_out=den_b[:, j:j + 1])
                    p.tt(den_a, den_a, den_b, ALU.add)
                    p.recip(fac4, den_a)
                    for j in range(4):
                        t = w * 4 + j
                        nm_ = (t if w == 0 else 4)
                        p.act(pTrow_[:, j, 0:640], pTrow_[:, j, 0:640], AF.Identity,
                              scale=fac4[:, j:j + 1])
                        kts_j = list(range(t - nm_, t)) + [t]
                        srcs = [pTrow_[:, j, m * 128:(m + 1) * 128] for m in range(nm_)] +                                [pTrow_[:, j, 512:640]]
                        tiles = []
                        for m in range(len(kts_j)):
                            sl = tr_slot[0] % 8
                            tr_slot[0] += 1
                            p.tr(ptr.bitcast(BF16)[:, sl * 128:(sl + 1) * 128], srcs[m], identb)
                            p.copy(ptile_[:, sl, :], ptr.bitcast(BF16)[:, sl * 128:(sl + 1) * 128],
                                   eng=("vector", "scalar")[m % 2])
                            tiles.append(sl)
                        for m, kt in enumerate(kts_j):
                            p.mm(pav[0:64, j * 128:(j + 1) * 128], vtmh_[:, kt, :], ptile_[:, tiles[m], :],
                                 start=(m == 0), stop=(m == len(kts_j) - 1), tag="avl2")
                    p.copy(obuf_[0:64, :], pav[0:64, 0:512])
                    p.dma("gpsimd", o_hbm[h * 64:(h + 1) * 64, qa:qa + 512], obuf_[0:64, :])

            for hc in range(4):
                p.dma("sync", qfm_c, qf_hbm[b, hc * 128:(hc + 1) * 128, :])
                p.dma("sync", kfm_c, kf_hbm[b, hc * 128:(hc + 1) * 128, :])
                zipper(lambda: att_body(hc, 0), lambda: att_body(hc, 1))
            for tc in range(4):
                tsl = slice(tc * 512, (tc + 1) * 512)
                o4v = o4 if tc % 2 == 0 else o4b
                p.dma("sync", o4v, o_hbm.rearrange("(c p) s -> p c s", p=128)[:, :, tsl])

                def wo_body(es):
                    for e in es:
                        ps = (psA, psD)[e % 2]
                        xc2 = xcur2[e % 2]
                        p.dma("sync", xc2, src[b, e * 128:(e + 1) * 128, tsl])
                        for c in range(4):
                            p.mm(ps, wv(OFF_WO, c, 1024)[:, e * 128:(e + 1) * 128], o4v[:, c, :],
                                 start=(c == 0), stop=(c == 3))
                        p.tt(xc2, xc2, ps, ALU.add)
                        p.dma("gpsimd", xfa[b, e * 128:(e + 1) * 128, tsl], xc2)

                zipper(lambda: wo_body([0, 2, 4, 6]), lambda: wo_body([1, 3, 5, 7]))
        p.fence_all()
        for c in range(8):
            for hv in range(2):
                p.dma("sync", wtmp, ins[f"w1a_{l}"][:, c, hv * 1408:(hv + 1) * 1408])
                p.copy(wv(OFF_W1A, c, 2816)[:, hv * 1408:(hv + 1) * 1408], wtmp)
                p.dma("sync", wtmp, ins[f"w1g_{l}"][:, c, hv * 1408:(hv + 1) * 1408])
                p.copy(wv(OFF_W1G, c, 2816)[:, hv * 1408:(hv + 1) * 1408], wtmp)
        for c in range(22):
            p.dma("sync", wtmp[:, 0:1024], ins[f"w2_{l}"][:, c, :])
            p.copy(wv(OFF_W2, c, 1024), wtmp[:, 0:1024])
        for b in range(B):
            shift_copy(shf_hbm, xfa, b, cast=False)
            for tc in range(4):
                tsl = slice(tc * 512, (tc + 1) * 512)
                xc = xcur[:, 0:512]
                sq = msq
                for c in range(8):
                    p.dma("sync", xc, shf_hbm[b, c * 128:(c + 1) * 128, tsl])
                    p.mm(psA[0:1, 0:512], ones_c, xc, start=(c == 0), stop=(c == 7), tag="mu")
                    p.act(sq, xc, AF.Square)
                    p.mm(psB[0:1, 0:512], ones_c, sq, start=(c == 0), stop=(c == 7), tag="ssq")
                p.ts(stat[0:1, 0:512], psA[0:1, 0:512], 1.0 / D, None, ALU.mult)
                p.ts(stat[0:1, 512:1024], psB[0:1, 0:512], 1.0 / D, None, ALU.mult)
                p.act(stat[0:1, 1024:1536], stat[0:1, 0:512], AF.Square)
                p.tt(stat[0:1, 512:1024], stat[0:1, 512:1024], stat[0:1, 1024:1536], ALU.subtract)
                p.act(stat[0:1, 512:1024], stat[0:1, 512:1024], AF.Ln, bias=eps5[0:1, 0:1])
                p.act(stat[0:1, 512:1024], stat[0:1, 512:1024], AF.Exp, scale=-0.5)
                p.mm(psD[:, 0:512], ones_r, stat[0:1, 0:512], tag="mub")
                p.copy(mub, psD[:, 0:512], eng="scalar")
                p.mm(psD[:, 0:512], ones_r, stat[0:1, 512:1024], tag="rsb")
                p.copy(rsb, psD[:, 0:512], eng="scalar")
                for c in range(8):
                    p.dma("sync", xc, shf_hbm[b, c * 128:(c + 1) * 128, tsl])
                    p.tt(sq, xc, mub, ALU.subtract)
                    p.tt(sq, sq, rsb, ALU.mult)
                    p.ts(hln[:, c, :], sq, s_small[f"lg_{l}"][:, c:c + 1],
                         s_small[f"lb_{l}"][:, c:c + 1], ALU.mult, ALU.add)
                for hc in range(22):
                    pa = psA if hc % 2 == 0 else psE
                    pb = psB if hc % 2 == 0 else psF
                    ab = abuf if hc % 2 == 0 else abuf2
                    gb = gbuf if hc % 2 == 0 else gbuf2
                    for c in range(8):
                        p.mm(pa[:, 0:512], wv(OFF_W1A, c, 2816)[:, hc * 128:(hc + 1) * 128], hln[:, c, :],
                             start=(c == 0), stop=(c == 7))
                    for c in range(8):
                        p.mm(pb[:, 0:512], wv(OFF_W1G, c, 2816)[:, hc * 128:(hc + 1) * 128], hln[:, c, :],
                             start=(c == 0), stop=(c == 7))
                    p.ts(ab, pa[:, 0:512], s_small[f"b1a_{l}"][:, hc:hc + 1], None, ALU.add)
                    p.act(gb, pb[:, 0:512], AF.Gelu, bias=s_small[f"b1g_{l}"][:, hc:hc + 1])
                    p.tt(glu[:, hc, :], ab, gb, ALU.mult)
                for e in range(8):
                    pc = psC if e % 2 == 0 else psD
                    for hc in range(22):
                        p.mm(pc[:, 0:512], wv(OFF_W2, hc, 1024)[:, e * 128:(e + 1) * 128], glu[:, hc, :],
                             start=(hc == 0), stop=(hc == 21))
                    p.ts(sq, pc[:, 0:512], s_small[f"b2_{l}"][:, e:e + 1], None, ALU.add)
                    p.dma("sync", xc, xfa[b, e * 128:(e + 1) * 128, tsl])
                    p.tt(xc, xc, sq, ALU.add)
                    if l == 1:
                        p.act(xc, xc, AF.Silu)
                        p.copy(qout[:, 0:512], xc)
                        p.dma("gpsimd", g1_out[b, e * 128:(e + 1) * 128, tsl], qout[:, 0:512])
                    else:
                        p.dma("gpsimd", xfb[b, e * 128:(e + 1) * 128, tsl], xc)
    p.emit()
    return nc


def run_phaseC2(inputs, xn1f_shards, n_cores=8):
    w = inputs
    wC2 = wC2_prep(w)
    cC2 = hostC2_consts()
    nc = build_phaseC2()
    in_maps = []
    for c in range(n_cores):
        m = dict(wC2)
        m.update(cC2)
        m["xn1f_fm"] = xn1f_shards[c]
        in_maps.append(m)
    res = run_bass_kernel_spmd(nc, in_maps, core_ids=list(range(n_cores)))
    return res.results



def hostD_consts():
    i = np.arange(128)[:, None]
    j = np.arange(128)[None, :]
    gcols = np.array([0, 1, 2, 3] + list(range(64, 2048, 64)), np.int64)
    c = {}
    c["identD"] = np.eye(128, dtype=np.float32)
    def addm(valid):
        return np.where(valid, 0.0, NEG).astype(np.float32)
    k_, q_ = i, j
    same = (k_ <= q_) | ((k_ % 64) == 0)
    mT_same = addm(same)
    mT_same0 = addm(same | (k_ < 4))
    prev = (k_ >= q_) | ((k_ % 64) == 0)
    mT_prev = addm(prev)
    mT_prev1 = addm(prev | (k_ < 4))
    c["mT_same0"] = mT_same0
    c["mT_ps"] = np.ascontiguousarray(np.concatenate([mT_prev, mT_same], axis=1))
    c["mT_ps1"] = np.ascontiguousarray(np.concatenate([mT_prev1, mT_same], axis=1))
    c["m_s0s"] = np.ascontiguousarray(mT_same0.T) * 0.125
    c["m_ps"] = np.ascontiguousarray(np.concatenate([mT_prev.T, mT_same.T], axis=1)) * 0.125
    c["m_ps1"] = np.ascontiguousarray(np.concatenate([mT_prev1.T, mT_same.T], axis=1)) * 0.125
    gT = np.zeros((35, 2048), np.float32)
    for t in range(16):
        lo, hi = 128 * (t - 1), 128 * (t + 1)
        excl = (gcols >= lo) & (gcols < hi)
        gT[excl, t * 128:(t + 1) * 128] = NEG
    c["gT"] = gT
    c["g_rm"] = np.ascontiguousarray(gT.T) * 0.125
    sel = np.zeros((256, 16), np.float32)
    r = np.arange(256)
    sel[r, r % 16] = 1.0
    c["selD"] = sel
    return c


def wD_prep(w):
    Wq = np.asarray(w["L1_attn_Wqkv"], np.float32)
    return {
        "wqkvD": np.ascontiguousarray(Wq.reshape(8, 128, 3072).transpose(1, 0, 2)),
        "bqkD": np.ascontiguousarray(np.asarray(w["L1_attn_bqkv"], np.float32)[0:2048].reshape(16, 128).T),
        "bvD": np.asarray(w["L1_attn_bqkv"], np.float32)[2048:3072].reshape(1, 1024),
        "woutD": np.ascontiguousarray(np.asarray(w["L1_attn_Wout"], np.float32).reshape(8, 128, 1024).transpose(1, 0, 2)),
        "boutD": np.ascontiguousarray(np.asarray(w["L1_attn_bout"], np.float32).reshape(8, 128).T),
    }


def build_phaseD():
    nc = bass.Bass("TRN2", target_bir_lowering=False, name="phD")
    B, S, D = 2, 2048, 1024
    H = 16
    x1 = nc.dram_tensor("x1_fm", [B, D, S], F32, kind="ExternalInput").ap()
    xn1 = nc.dram_tensor("xn1_fm", [B, D, S], BF16, kind="ExternalInput").ap()
    g1 = nc.dram_tensor("g1_fm", [B, D, S], BF16, kind="ExternalInput").ap()
    spk_all = nc.dram_tensor("spk_all", [256, S], F32, kind="ExternalInput").ap()
    wqkv = nc.dram_tensor("wqkvD", [128, 8, 3072], F32, kind="ExternalInput").ap()
    bqk = nc.dram_tensor("bqkD", [128, 16], F32, kind="ExternalInput").ap()
    bv = nc.dram_tensor("bvD", [1, 1024], F32, kind="ExternalInput").ap()
    wout = nc.dram_tensor("woutD", [128, 8, 1024], F32, kind="ExternalInput").ap()
    bout = nc.dram_tensor("boutD", [128, 8], F32, kind="ExternalInput").ap()
    cst = {}
    for nm, shp in (("identD", [128, 128]), ("mT_same0", [128, 128]),
                    ("mT_ps", [128, 256]), ("mT_ps1", [128, 256]),
                    ("m_s0s", [128, 128]), ("m_ps", [128, 256]), ("m_ps1", [128, 256]),
                    ("gT", [35, 2048]), ("g_rm", [2048, 35]), ("selD", [256, 16])):
        cst[nm] = nc.dram_tensor(nm, shp, F32, kind="ExternalInput").ap()
    out = nc.dram_tensor("out", [B * S, D], F32, kind="ExternalOutput").ap()
    qf_hbm = nc.dram_tensor("qfD", [B, D, S], BF16).ap()
    kf_hbm = nc.dram_tensor("kfD", [B, D, S], BF16).ap()
    vt_hbm = nc.dram_tensor("vtD", [B, S, D], BF16).ap()
    vdh2 = [nc.dram_tensor(f"vdenD{r}", [S, 65], BF16).ap() for r in range(2)]

    p = Prog(nc)
    al = nc.alloc_sbuf_tensor
    s_id = al("s_idD", [128, 128], F32).ap()
    identb = al("identbD", [128, 128], BF16).ap()
    s_cst = {}
    for nm in cst:
        if nm in ("identD", "selD", "g_rm"):
            continue
        shp = list(cst[nm].shape)
        s_cst[nm] = al("s_" + nm, shp, F32).ap()
    g_rm_sb = al("s_g_rm", [128, 16, 35], F32).ap()
    wqb = al("wqbD", [128, 8, 3072], BF16).ap()
    wob = al("wobD", [128, 8, 1024], BF16).ap()
    wtmp = al("wtmpD", [128, 1024], F32).ap()
    s_bqk = al("s_bqkD", [128, 16], F32).ap()
    s_bvr = al("s_bvrD", [1, 1024], F32).ap()
    bvb = al("bvbD", [128, 1024], F32).ap()
    s_bout = al("s_boutD", [128, 8], F32).ap()
    ones_r = al("ones_rD", [1, 128], F32).ap()
    spkm = al("spkmD", [16, 2048], F32).ap()
    spkT = al("spkTD", [128, 16, 16], F32).ap()
    xnf = al("xnfD", [128, 8, 512], BF16).ap()
    prt2 = [al(f"prtD{r}", [128, 512], F32).ap() for r in range(2)]
    prb2 = [al(f"prbD{r}", [128, 512], BF16).ap() for r in range(2)]
    qfm2 = [al(f"qfm_cD{r}", [128, 2048], BF16).ap() for r in range(2)]
    kfm2 = [al(f"kfm_cD{r}", [128, 2048], BF16).ap() for r in range(2)]
    vtmh2 = [al(f"vtmhD{r}", [128, 16, 64], BF16).ap() for r in range(2)]
    vden2 = [al(f"vdenD_s{r}", [128, 16, 65], BF16).ap() for r in range(2)]
    vcden2 = [al(f"vcdenD{r}", [35, 65], BF16).ap() for r in range(2)]
    kc2 = [al(f"kcD{r}", [128, 35], BF16).ap() for r in range(2)]
    sml2 = [al(f"smlD{r}", [128, 256], F32).ap() for r in range(2)]
    smlg2 = [al(f"smlgD{r}", [35, 128], F32).ap() for r in range(2)]
    smg2 = [al(f"smgD{r}", [128, 35], F32).ap() for r in range(2)]
    m_b2 = [al(f"m_bD{r}", [128, 1], F32).ap() for r in range(2)]
    m_c2 = [al(f"m_cD{r}", [128, 1], F32).ap() for r in range(2)]
    m_col2 = [al(f"m_colD{r}", [128, 1], F32).ap() for r in range(2)]
    rowb2 = [al(f"rowbD{r}", [1, 640], F32).ap() for r in range(2)]
    cmb2 = [al(f"cmbD{r}", [128, 256], F32).ap() for r in range(2)]
    cmbg2 = [al(f"cmbgD{r}", [35, 128], F32).ap() for r in range(2)]
    pT2 = [al(f"pTD{r}", [128, 256], BF16).ap() for r in range(2)]
    pTg2 = [al(f"pTgD{r}", [35, 128], BF16).ap() for r in range(2)]
    fac_b2 = [al(f"fac_bD{r}", [64, 128], F32).ap() for r in range(2)]
    o_fm = al("o_fmD", [128, 8, 2048], BF16).ap()
    x1t2 = [al(f"x1tD{r}", [128, 512], F32).ap() for r in range(2)]
    g1t2 = [al(f"g1tD{r}", [128, 512], BF16).ap() for r in range(2)]
    outt2 = [al(f"outtD{r}", [128, 512], F32).ap() for r in range(2)]

    psA = nc.alloc_psum_tensor("psA", [128, 512], F32).ap()
    psB = nc.alloc_psum_tensor("psB", [128, 512], F32).ap()
    psC = nc.alloc_psum_tensor("psC", [128, 512], F32).ap()
    psD = nc.alloc_psum_tensor("psD", [128, 512], F32).ap()
    psE = nc.alloc_psum_tensor("psE", [128, 512], F32).ap()
    psF = nc.alloc_psum_tensor("psF", [128, 512], F32).ap()
    psG = nc.alloc_psum_tensor("psG", [128, 512], F32).ap()
    psAs2 = (psA, psG)
    psBs2 = (psB, psD)
    psCs2 = (psC, psE)

    p.dma("sync", s_id, cst["identD"])
    p.copy(identb, s_id)
    for nm in s_cst:
        p.dma("sync", s_cst[nm], cst[nm])
    p.dma("sync", g_rm_sb, cst["g_rm"].rearrange("(t p) g -> p t g", p=128))
    for c in range(8):
        for third in range(3):
            p.dma("sync", wtmp, wqkv[:, c, third * 1024:(third + 1) * 1024])
            p.copy(wqb[:, c, third * 1024:(third + 1) * 1024], wtmp)
        p.dma("sync", wtmp, wout[:, c, :])
        p.copy(wob[:, c, :], wtmp)
    p.dma("sync", s_bqk, bqk)
    p.dma("sync", s_bvr, bv)
    p.dma("sync", s_bout, bout)
    p.memset(ones_r, 1.0)
    p.mm(psA, ones_r, s_bvr[0:1, 0:512], tag="bvb0")
    p.copy(bvb[:, 0:512], psA)
    p.mm(psA, ones_r, s_bvr[0:1, 512:1024], tag="bvb1")
    p.copy(bvb[:, 512:1024], psA)
    spk_sb = al("spk_sbD", [128, 2048], F32).ap()
    sel_sb = al("sel_sbD", [128, 2, 16], F32).ap()
    p.dma("sync", sel_sb, cst["selD"].rearrange("(c p) h -> p c h", p=128))
    for ch in range(2):
        for qc in range(4):
            p.dma("sync", spk_sb[:, qc * 512:(qc + 1) * 512],
                  spk_all[ch * 128:(ch + 1) * 128, qc * 512:(qc + 1) * 512])
        for qc in range(4):
            ps = (psB, psC, psD, psE)[qc]
            p.mm(ps[0:16, :], sel_sb[:, ch, :], spk_sb[:, qc * 512:(qc + 1) * 512],
                 start=(ch == 0), stop=(ch == 1), tag=f"spk{qc}")
            if ch == 1:
                p.ts(spkm[:, qc * 512:(qc + 1) * 512], ps[0:16, :], 0.0, None, ALU.is_gt)
    for t in range(16):
        p.tr(psE[0:128, 0:16], spkm[:, t * 128:(t + 1) * 128], s_id[0:16, 0:16], tag="trspk")
        p.copy(spkT[:, t, :], psE[0:128, 0:16])
    for b in range(B):
        for tc in range(4):
            tsl = slice(tc * 512, (tc + 1) * 512)
            p.dma("sync", xnf, xn1[b].rearrange("(c p) s -> p c s", p=128)[:, :, tsl])
            for e in range(16):
                ps = (psA, psB)[e % 2]
                pr = prt2[e % 2]
                pbb = prb2[e % 2]
                for c in range(8):
                    p.mm(ps, wqb[:, c, e * 128:(e + 1) * 128], xnf[:, c, :],
                         start=(c == 0), stop=(c == 7))
                p.ts(pr, ps, s_bqk[:, e:e + 1], None, ALU.add)
                p.copy(pbb, pr)
                dsth = qf_hbm if e < 8 else kf_hbm
                ee = e % 8
                p.dma("gpsimd", dsth[b, ee * 128:(ee + 1) * 128, tsl], pbb)
            for j in range(4):
                t0 = tc * 512 + j * 128
                for half in range(2):
                    ps = psD if half == 0 else psE
                    pr = prt2[half]
                    pbb = prb2[half]
                    for c in range(8):
                        p.mm(ps, xnf[:, c, j * 128:(j + 1) * 128],
                             wqb[:, c, 2048 + half * 512:2048 + (half + 1) * 512],
                             start=(c == 0), stop=(c == 7))
                    p.tt(pr, ps, bvb[:, half * 512:(half + 1) * 512], ALU.add)
                    p.copy(pbb, pr)
                    p.dma("gpsimd", vt_hbm[b, t0:t0 + 128, half * 512:(half + 1) * 512], pbb)
        p.fence_all()
        for r in range(2):
            for kt in range(16):
                p.memset(vden2[r][:, kt, 64:65], 1.0)
        for hc in range(8):
            qf_c = qfm2[hc % 2]
            kf_c = kfm2[hc % 2]
            p.dma("sync", qf_c, qf_hbm[b, hc * 128:(hc + 1) * 128, :])
            p.dma("sync", kf_c, kf_hbm[b, hc * 128:(hc + 1) * 128, :])
            for hh in range(2):
                h = 2 * hc + hh
                ph = h % 2
                kc = kc2[ph]
                vtmh = vtmh2[ph]
                vden = vden2[ph]
                vcden = vcden2[ph]
                vdh = vdh2[ph]
                hp = 64 * hh
                p.copy(kc[hp:hp + 64, 0:4], kf_c[hp:hp + 64, 0:4], eng="scalar")
                p.copy(kc[hp:hp + 64, 4:35],
                       kf_c.rearrange("p (a c) -> p a c", c=64)[hp:hp + 64, 1:32, 0], eng="scalar")
                p.dma("sync", vtmh, vt_hbm[b].rearrange("(t p) e -> p t e", p=128)[:, :, h * 64:(h + 1) * 64])
                for kt in range(16):
                    p.act(vden[:, kt, 0:64], vtmh[:, kt, :], AF.Identity, scale=spkT[:, kt, h:h + 1])
                    p.copy(vden[:, kt, 64:65], spkT[:, kt, h:h + 1], eng="scalar")
                p.dma("gpsimd", vdh.rearrange("(t p) e -> p t e", p=128), vden)
                p.dma("sync", vcden[0:4, :], vdh[0:4, :])
                p.dma("sync", vcden[4:35, :], vdh.rearrange("(a c) e -> a c e", c=64)[1:32, 0, :])

                def S1(t):
                    par = t % 2
                    pa, pb = psAs2[par], psBs2[par]
                    q0 = t * 128
                    qv = qf_c[hp:hp + 64, q0:q0 + 128]
                    if t > 0:
                        p.mm(pa[:, 0:256], qv, kf_c[hp:hp + 64, q0 - 128:q0 + 128], tag="st2")
                        p.mm(pb[:, 128:256], kf_c[hp:hp + 64, q0 - 128:q0], qv, tag="nT")
                    else:
                        p.mm(pa[:, 128:256], qv, kf_c[hp:hp + 64, 0:128], tag="st2")
                    p.mm(pa[:, 256:291], qv, kc[hp:hp + 64, :], tag="stg")
                    p.mm(pb[0:35, 0:128], kc[hp:hp + 64, :], qv, tag="gT")
                    p.mm(pb[:, 256:384], kf_c[hp:hp + 64, q0:q0 + 128], qv, tag="nT")

                def V1(t):
                    par = t % 2
                    pa = psAs2[par]
                    sml, smg = sml2[par], smg2[par]
                    m_b, m_c, m_col = m_b2[par], m_c2[par], m_col2[par]
                    if t == 0:
                        p.stt(sml[:, 128:256], pa[:, 128:256], 0.125, s_cst["m_s0s"], ALU.mult, ALU.add)
                        p.red(m_b, sml[:, 128:256], ALU.max)
                    else:
                        mk = "m_ps1" if t == 1 else "m_ps"
                        p.stt(sml, pa[:, 0:256], 0.125, s_cst[mk], ALU.mult, ALU.add)
                        p.red(m_b, sml, ALU.max)
                    p.stt(smg, pa[:, 256:291], 0.125, g_rm_sb[:, t, :], ALU.mult, ALU.add)
                    p.red(m_c, smg, ALU.max)
                    p.tt(m_col, m_b, m_c, ALU.max)

                def X(t):
                    par = t % 2
                    pa, pb, pc = psAs2[par], psBs2[par], psCs2[par]
                    sml, smlg = sml2[par], smlg2[par]
                    cmb, cmbg = cmb2[par], cmbg2[par]
                    pT, pTg = pT2[par], pTg2[par]
                    m_col = m_col2[par]
                    mneg2 = rowb2[par][0:1, 0:256]
                    q0 = t * 128
                    fb = par * 256
                    p.tr(pa[0:1, 384:512], m_col, s_id, tag="c2rD")
                    p.ts(mneg2[0:1, 0:128], pa[0:1, 384:512], -1.0, None, ALU.mult, eng="scalar")
                    p.ts(mneg2[0:1, 128:256], pa[0:1, 384:512], -1.0, None, ALU.mult, eng="scalar")
                    p.mm(psF[:, fb:fb + 256], ones_r, mneg2, tag="bcmD")
                    p.tt(cmbg, s_cst["gT"][:, q0:q0 + 128], psF[0:35, fb:fb + 128], ALU.add)
                    p.stt(smlg, pb[0:35, 0:128], 0.125, cmbg, ALU.mult, ALU.add)
                    p.act(pTg, smlg, AF.Exp)
                    p.mm(pc[0:65, 0:128], vcden, pTg, start=True, stop=False, tag="avgD")
                    if t == 0:
                        p.tt(cmb[:, 128:256], s_cst["mT_same0"], psF[:, fb + 128:fb + 256], ALU.add)
                        p.stt(sml[:, 128:256], pb[:, 256:384], 0.125, cmb[:, 128:256], ALU.mult, ALU.add)
                        p.act(pT[:, 128:256], sml[:, 128:256], AF.Exp)
                        p.mm(pc[0:65, 0:128], vden[:, 0, :], pT[:, 128:256],
                             start=False, stop=True, tag="avnD")
                    else:
                        mk = "mT_ps1" if t == 1 else "mT_ps"
                        p.tt(cmb, s_cst[mk], psF[:, fb:fb + 256], ALU.add)
                        p.stt(sml, pb[:, 128:384], 0.125, cmb, ALU.mult, ALU.add)
                        p.act(pT, sml, AF.Exp)
                        p.mm(pc[0:65, 0:128], vden[:, t - 1, :], pT[:, 0:128],
                             start=False, stop=False, tag="avnD")
                        p.mm(pc[0:65, 0:128], vden[:, t, :], pT[:, 128:256],
                             start=False, stop=True, tag="avnD")

                def N(t):
                    par = t % 2
                    pb, pc = psBs2[par], psCs2[par]
                    den_r = rowb2[par][0:1, 256:384]
                    fac_r = rowb2[par][0:1, 384:512]
                    rv_r = rowb2[par][0:1, 512:640]
                    fac_b = fac_b2[par]
                    q0 = t * 128
                    p.copy(den_r, pc[64:65, 0:128], eng="scalar")
                    p.ts(rv_r, den_r, 0.0, None, ALU.is_gt)
                    p.ts(fac_r, den_r, 1e-30, None, ALU.max)
                    p.recip(fac_r, fac_r)
                    p.tt(fac_r, fac_r, rv_r, ALU.mult)
                    p.mm(pb[:, 384:512], ones_r, fac_r, tag="bcfD")
                    p.copy(fac_b, pb[0:64, 384:512], eng="scalar")
                    p.tt(o_fm[hp:hp + 64, hc, q0:q0 + 128], pc[0:64, 0:128], fac_b, ALU.mult)

                S1(0)
                V1(0)
                for t in range(16):
                    if t + 1 < 16:
                        S1(t + 1)
                    X(t)
                    if t + 1 < 16:
                        V1(t + 1)
                    N(t)
        for tc in range(4):
            tsl = slice(tc * 512, (tc + 1) * 512)
            for e in range(8):
                par = e % 2
                ps = (psA, psB)[par]
                pr = prt2[par]
                x1t = x1t2[par]
                g1t = g1t2[par]
                outt = outt2[par]
                p.dma("sync", x1t, x1[b, e * 128:(e + 1) * 128, tsl])
                p.dma("sync", g1t, g1[b, e * 128:(e + 1) * 128, tsl])
                for c in range(8):
                    p.mm(ps, wob[:, c, e * 128:(e + 1) * 128], o_fm[:, c, tsl],
                         start=(c == 0), stop=(c == 7))
                p.ts(pr, ps, s_bout[:, e:e + 1], None, ALU.add)
                p.tt(pr, pr, g1t, ALU.mult)
                p.tt(pr, pr, x1t, ALU.add)
                for j in range(4):
                    pt = (psD, psE)[j % 2]
                    p.tr(pt[:, 0:128], pr[:, j * 128:(j + 1) * 128], s_id)
                    p.copy(outt[:, j * 128:(j + 1) * 128], pt[:, 0:128])
                p.dma("gpsimd", out[b * S + tc * 512:b * S + (tc + 1) * 512,
                                    e * 128:(e + 1) * 128].rearrange("(j p) q -> p j q", p=128),
                      outt.rearrange("p (j q) -> p j q", j=4))
    p.emit()
    return nc


def run_phaseD(inputs, x1_shards, xn1_shards, g1_shards, spikes_all, n_cores=8):
    w = inputs
    wD = wD_prep(w)
    cD = hostD_consts()
    nc = build_phaseD()
    in_maps = []
    for c in range(n_cores):
        m = dict(wD)
        m.update(cD)
        m["x1_fm"] = x1_shards[c]
        m["xn1_fm"] = xn1_shards[c]
        m["g1_fm"] = g1_shards[c]
        m["spk_all"] = spikes_all
        in_maps.append(m)
    res = run_bass_kernel_spmd(nc, in_maps, core_ids=list(range(n_cores)))
    return res.results



def kernel(**inputs):
    n = 8
    w = inputs
    resA = run_phaseA(w, n_cores=n)
    xn_all = np.concatenate([resA[c]["xn_bf16"] for c in range(n)], axis=0)
    scores_all = np.concatenate([resA[c]["scores"] for c in range(n)], axis=0)
    resB = run_phaseB(w, xn_all, scores_all, n_cores=n)
    Zall = np.stack([resB[e]["Z"] for e in range(n)], axis=0)
    x = np.asarray(w["L0_attn_x"], np.float32).reshape(16 * 2048, 1024)
    x_shards = [np.ascontiguousarray(x[c * 4096:(c + 1) * 4096]) for c in range(n)]
    s0_shards = [resA[c]["s0_fm"] for c in range(n)]
    resC = run_phaseC(w, x_shards, s0_shards, Zall, n_cores=n)
    xn1f_shards = [resC[c]["xn1f_fm"] for c in range(n)]
    resC2 = run_phaseC2(w, xn1f_shards, n_cores=n)
    spk_all = np.concatenate([resC[c]["spikes"] for c in range(n)], axis=0)
    x1_shards = [resC[c]["x1_fm"] for c in range(n)]
    xn1_shards = [resC[c]["xn1_fm"] for c in range(n)]
    g1_shards = [resC2[c]["g1_fm"] for c in range(n)]
    resD = run_phaseD(w, x1_shards, xn1_shards, g1_shards, spk_all, n_cores=n)
    out = np.concatenate([resD[c]["out"] for c in range(n)], axis=0)
    return out.reshape(16, 2048, 1024).astype(np.float32)


# version 27
# speedup vs baseline: 1.0420x; 1.0320x over previous
import numpy as np
import ml_dtypes
import concourse.bass as bass
import concourse.mybir as mybir
from concourse.bass_utils import run_bass_kernel_spmd
from concourse.bass_core_rust import IndirectOffsetOnAxis

F32 = mybir.dt.float32
BF16 = mybir.dt.bfloat16
I32 = mybir.dt.int32
AF = mybir.ActivationFunctionType
ALU = mybir.AluOpType
AX = mybir.AxisListType
bfdt = ml_dtypes.bfloat16

ENGINES = ["tensor", "vector", "scalar", "gpsimd", "sync"]
NEG = -1e30


class Op:
    __slots__ = ["eng", "fn", "reads", "writes", "seq", "dma", "tag", "_oc", "_dc", "sreads",
                 "fence_oc", "fence_dc", "_slot", "_dcs"]

    def __init__(self, eng, fn, reads, writes, dma=False, tag="", sreads=()):
        self.eng = eng
        self.fn = fn
        self.reads = reads
        self.writes = writes
        self.dma = dma
        self.tag = tag
        self.sreads = sreads


def _toks(args):
    out = []
    for a in args:
        if a is None or isinstance(a, float):
            continue
        if isinstance(a, (list, tuple)):
            out += _toks(a)
            continue
        t = getattr(a, "tensor", a)
        name = getattr(t, "name", None)
        if name is None:
            out.append((str(t), 0, 1 << 30, 0, 1 << 40))
            continue
        try:
            isz = mybir.dt.size(a.dtype)
            ap = a.ap
            pbase = getattr(a, "base_partition", 0) or 0
            pcnt = int(ap[0][1]) if len(ap) else 128
            flo = int(a.offset) * isz
            ext = 1
            for st, cnt in list(ap)[1:]:
                ext += abs(int(st)) * (int(cnt) - 1)
            fhi = flo + ext * isz
            out.append((name, pbase, pbase + pcnt, flo, fhi))
        except Exception:
            out.append((name, 0, 1 << 30, 0, 1 << 40))
    return out


def _overlap(x, y):
    return (x[0] == y[0] and x[1] < y[2] and y[1] < x[2]
            and x[3] < y[4] and y[3] < x[4])


def _covered(x, y):
    return (x[0] == y[0] and y[1] <= x[1] and x[2] <= y[2]
            and y[3] <= x[3] and x[4] <= y[4])


class Prog:
    def __init__(self, nc):
        self.nc = nc
        self.ops = []
        self._rr = 0

    def add(self, eng, fn, reads, writes, dma=False, tag="", sreads=()):
        sr = _toks(sreads)
        self.ops.append(Op(eng, fn, _toks(reads), _toks(writes), dma, tag, sr))

    def dma(self, eng, out, in_, tag="", slow=False):
        try:
            if out.dtype != in_.dtype:
                eng = "gpsimd"
            else:
                eng = ("sync", "scalar", "gpsimd")[self._rr % 3]
                self._rr += 1
        except Exception:
            pass
        if slow:
            self.add(eng, lambda g: g.dma_start(out=out, in_=in_, allow_slow_non_contiguous=True),
                     [in_], [out], dma=True, tag=tag or "dma")
        else:
            self.add(eng, lambda g: g.dma_start(out=out, in_=in_), [in_], [out], dma=True, tag=tag or "dma")

    def gather(self, out, in_, idx_ap, tag="", accum=False):
        op = ALU.add if accum else ALU.bypass
        self.add("gpsimd",
                 lambda g: g.indirect_dma_start(out, None, in_, IndirectOffsetOnAxis(idx_ap, 0), compute_op=op),
                 [in_, idx_ap], [out], dma=True, tag=tag or "gather")

    def scatter(self, out, in_, idx_ap, tag="", accum=False):
        op = ALU.add if accum else ALU.bypass
        self.add("gpsimd",
                 lambda g: g.indirect_dma_start(out, IndirectOffsetOnAxis(idx_ap, 0), in_, None, compute_op=op),
                 [in_, idx_ap], [out], dma=True, tag=tag or "scatter")

    def mm(self, out, lhsT, rhs, start=True, stop=True, tag=""):
        self.add("tensor", lambda g: g.matmul(out, lhsT, rhs, start=start, stop=stop, skip_group_check=True),
                 [lhsT, rhs], [out], tag=tag or "mm")

    def tr(self, out, in_, ident, tag=""):
        self.add("tensor", lambda g: g.transpose(out, in_, ident), [in_, ident], [out], tag=tag or "tr")

    def act(self, out, in_, func, bias=0.0, scale=1.0, accum_out=None, eng="scalar", tag=""):
        self.add(eng,
                 lambda g: g.activation(out, in_, func, bias=bias, scale=scale, accum_out=accum_out),
                 [in_], [out, accum_out], tag=tag or "act", sreads=[bias, scale])

    def tt(self, out, a, b, op, eng="vector", tag=""):
        self.add(eng, lambda g: g.tensor_tensor(out, a, b, op), [a, b], [out], tag=tag or "tt")

    def ts(self, out, a, s1, s2=None, op0=ALU.mult, op1=ALU.bypass, eng="vector", tag=""):
        if eng == "scalar":
            assert s2 is None and isinstance(s1, float)
            fn = (lambda g: g.mul(out, a, s1)) if op0 == ALU.mult else (lambda g: g.add(out, a, s1))
            self.add(eng, fn, [a], [out], tag=tag or "ts")
        else:
            self.add(eng, lambda g: g.tensor_scalar(out, a, s1, s2, op0, op1),
                     [a], [out], tag=tag or "ts", sreads=[s1, s2])

    def stt(self, out, in0, scalar, in1, op0, op1, eng="vector", tag=""):
        self.add(eng, lambda g: g.scalar_tensor_tensor(out, in0, scalar, in1, op0, op1),
                 [in0, in1], [out], tag=tag or "stt", sreads=[scalar])

    def copy(self, out, in_, eng="vector", tag=""):
        if eng == "scalar":
            self.add(eng, lambda g: g.copy(out, in_), [in_], [out], tag=tag or "copy")
        else:
            self.add(eng, lambda g: g.tensor_copy(out, in_), [in_], [out], tag=tag or "copy")

    def memset(self, ap, val, eng="vector", tag=""):
        self.add(eng, lambda g: g.memset(ap, val), [], [ap], tag=tag or "memset")

    def red(self, out, in_, op, negate=False, eng="vector", tag=""):
        self.add(eng, lambda g: g.tensor_reduce(out, in_, AX.X, op, negate=negate),
                 [in_], [out], tag=tag or "red")

    def recip(self, out, in_, tag=""):
        self.add("vector", lambda g: g.reciprocal(out, in_), [in_], [out], tag=tag or "recip")

    def sel(self, out, mask, a, b, tag=""):
        self.add("vector", lambda g: g.select(out, mask, a, b), [mask, a, b], [out], tag=tag or "sel")

    def scan(self, out, d0, d1, initial, op0, op1, eng="vector", tag=""):
        self.add(eng, lambda g: g.tensor_tensor_scan(out, d0, d1, initial, op0, op1),
                 [d0, d1], [out], tag=tag or "scan")

    def raw(self, eng, fn, reads, writes, tag=""):
        self.add(eng, fn, reads, writes, tag=tag)

    def fence_all(self):
        for e in ENGINES:
            op = Op(e, lambda g: g.engine_nop() if hasattr(g, "engine_nop") else None,
                    [], [], False, "fence")
            op.sreads = ()
            self.ops.append(op)

    def emit(self):
        nc = self.nc
        per_eng = {e: [] for e in ENGINES}
        for op in self.ops:
            per_eng[op.eng].append(op)
        writes_by = {}
        reads_by = {}
        edges = [set() for _ in self.ops]
        sedges = [set() for _ in self.ops]
        for i, op in enumerate(self.ops):
            for t in op.reads:
                for (w, j) in writes_by.get(t[0], ()):
                    if _overlap(t, w):
                        edges[i].add(j)
            for t in op.sreads:
                for (w, j) in writes_by.get(t[0], ()):
                    if _overlap(t, w):
                        sedges[i].add(j)
            for t in op.writes:
                for (w, j) in writes_by.get(t[0], ()):
                    if _overlap(t, w):
                        edges[i].add(j)
                for (r, j) in reads_by.get(t[0], ()):
                    if _overlap(t, r):
                        edges[i].add(j)
            for t in list(op.reads) + list(op.sreads):
                reads_by.setdefault(t[0], []).append((t, i))
            for t in op.writes:
                lw = writes_by.setdefault(t[0], [])
                lw[:] = [(w, j) for (w, j) in lw if not _covered(w, t)]
                lw.append((t, i))
                rb = reads_by.get(t[0])
                if rb:
                    rb[:] = [(r, j) for (r, j) in rb if not _covered(r, t)]
        NSLOT = 4
        has_dma = {e: any(op.dma for op in per_eng[e]) for e in ENGINES}
        sems, dsems = {}, {}
        for e in ENGINES:
            sems[e] = nc.alloc_semaphore(f"sem_{e}")
            if has_dma[e]:
                dsems[e] = [nc.alloc_semaphore(f"dsem_{e}{s}") for s in range(NSLOT)]
        idx_all = {id(op): i for i, op in enumerate(self.ops)}
        known = {e: {p: 0 for p in ENGINES} for e in ENGINES}
        knownd = {e: {} for e in ENGINES}
        for e in ENGINES:
            oc = 0
            dcs = [0] * NSLOT
            dc = 0
            for op in per_eng[e]:
                if op.tag == "fence":
                    op._oc = oc
                    continue
                if op.dma:
                    op._slot = dc % NSLOT
                    dcs[op._slot] += 1
                    op._dcs = dcs[op._slot]
                    dc += 1
                else:
                    oc += 1
                op._oc = oc
        final_dma = {}
        for e in ENGINES:
            cnt = [0] * NSLOT
            for op in per_eng[e]:
                if op.dma:
                    cnt[op._slot] += 1
            final_dma[e] = cnt
        final_oc = {e: sum(1 for op in per_eng[e] if (not op.dma and op.tag != "fence")) for e in ENGINES}

        run_oc = {e: 0 for e in ENGINES}
        run_dc = {e: [0] * NSLOT for e in ENGINES}
        for op in self.ops:
            if op.tag == "fence":
                op.fence_oc = dict(run_oc)
                op.fence_dc = {e: list(run_dc[e]) for e in ENGINES}
            elif op.dma:
                run_dc[op.eng][op._slot] += 1
            else:
                run_oc[op.eng] += 1

        def emit_engine(g, e):
            for op in per_eng[e]:
                i = idx_all[id(op)]
                wants, wantsd = {}, {}
                if op.tag == "fence":
                    for pe in ENGINES:
                        if op.fence_oc[pe] and pe != e:
                            wants[pe] = op.fence_oc[pe]
                        for s in range(NSLOT):
                            if op.fence_dc[pe][s]:
                                wantsd[(pe, s)] = op.fence_dc[pe][s] * 16
                for j in edges[i]:
                    p = self.ops[j]
                    if p.dma:
                        k = (p.eng, p._slot)
                        wantsd[k] = max(wantsd.get(k, 0), p._dcs * 16)
                    else:
                        wants[p.eng] = max(wants.get(p.eng, 0), p._oc)
                for j in sedges[i]:
                    p = self.ops[j]
                    if p.dma:
                        k = (p.eng, p._slot)
                        wantsd[k] = max(wantsd.get(k, 0), p._dcs * 16)
                    else:
                        wants[p.eng] = max(wants.get(p.eng, 0), p._oc)
                for pe, v in wants.items():
                    if known[e][pe] < v:
                        g.wait_ge(sems[pe], v)
                        known[e][pe] = v
                for (pe, s), v in wantsd.items():
                    if knownd[e].get((pe, s), 0) < v:
                        g.wait_ge(dsems[pe][s], v)
                        knownd[e][(pe, s)] = v
                if op.tag == "fence":
                    continue
                if op.dma and op._dcs > 1 and knownd[e].get((e, op._slot), 0) < (op._dcs - 1) * 16:
                    g.wait_ge(dsems[e][op._slot], (op._dcs - 1) * 16)
                    knownd[e][(e, op._slot)] = (op._dcs - 1) * 16
                inst = op.fn(g)
                if inst is not None:
                    if op.dma:
                        inst.then_inc(dsems[e][op._slot], 16)
                    else:
                        inst.then_inc(sems[e], 1)

        with nc.Block() as block:
            for e, deco in (("tensor", block.tensor), ("vector", block.vector),
                            ("scalar", block.scalar), ("gpsimd", block.gpsimd),
                            ("sync", block.sync)):
                def make(e=e):
                    def body(g):
                        emit_engine(g, e)
                        if e == "gpsimd":
                            for pe in ENGINES:
                                if has_dma[pe]:
                                    for s in range(NSLOT):
                                        if final_dma[pe][s]:
                                            g.wait_ge(dsems[pe][s], final_dma[pe][s] * 16)
                                if final_oc[pe] and pe != "gpsimd":
                                    g.wait_ge(sems[pe], final_oc[pe])
                    return body
                deco(make())
        return self



def host_consts_A():
    i = np.arange(128)
    c = {}
    c["ident_f32"] = np.eye(128, dtype=np.float32)
    c["causal_add"] = np.where(i[None, :] <= i[:, None], 0.0, NEG).astype(np.float32)
    c["causalT_add"] = np.where(i[:, None] <= i[None, :], 0.0, NEG).astype(np.float32)
    c["iota256_row"] = np.tile(np.arange(1, 257, dtype=np.float32), (128, 1))
    c["iota_row2048"] = np.tile(np.arange(2048, dtype=np.float32), (128, 1))
    c["qidx_cols"] = np.ascontiguousarray(np.arange(2048, dtype=np.float32).reshape(16, 128).T)
    return c


def wA_prep(w):
    d = {}
    for nm, key in (("wq", "L0_attn_w_q"), ("wk", "L0_attn_w_k"), ("wv", "L0_attn_w_v"),
                    ("wo", "L0_attn_w_out")):
        d[nm] = np.ascontiguousarray(np.asarray(w[key], np.float32).reshape(8, 128, 1024).transpose(1, 0, 2))
    d["scorer_w"] = np.ascontiguousarray(np.asarray(w["L0_attn_scorer_w"], np.float32).reshape(8, 128, 1).transpose(1, 0, 2))
    d["Wr"] = np.ascontiguousarray(np.asarray(w["L0_ffn_Wr"], np.float32).reshape(8, 128, 8).transpose(1, 0, 2))
    d["norm_w"] = np.ascontiguousarray(np.asarray(w["L0_attn_norm_w"], np.float32).reshape(8, 128).T)
    d["scorer_b"] = np.asarray(w["L0_attn_scorer_b"], np.float32).reshape(1, 1)
    return d


def build_phaseA():
    nc = bass.Bass("TRN2", target_bir_lowering=False, name="phA")
    B, S, D = 2, 2048, 1024
    H, G = 16, 256
    NT = S // 128
    x = nc.dram_tensor("x", [B * S, D], F32, kind="ExternalInput").ap()
    win = {}
    for nm, shp in (("wq", [128, 8, 1024]), ("wk", [128, 8, 1024]), ("wv", [128, 8, 1024]),
                    ("wo", [128, 8, 1024]), ("scorer_w", [128, 8, 1]), ("scorer_b", [1, 1]),
                    ("Wr", [128, 8, 8]), ("norm_w", [128, 8])):
        win[nm] = nc.dram_tensor(nm, shp, F32, kind="ExternalInput").ap()
    cst = {}
    for nm, shp in (("ident_f32", [128, 128]), ("causal_add", [128, 128]),
                    ("causalT_add", [128, 128]), ("iota256_row", [128, 256]),
                    ("iota_row2048", [128, 2048]), ("qidx_cols", [128, 16])):
        cst[nm] = nc.dram_tensor(nm, shp, F32, kind="ExternalInput").ap()
    xn_out = nc.dram_tensor("xn_bf16", [B * S, D], BF16, kind="ExternalOutput").ap()
    scores_out = nc.dram_tensor("scores", [B * S, 8], F32, kind="ExternalOutput").ap()
    s0_out = nc.dram_tensor("s0_fm", [B, D, S], F32, kind="ExternalOutput").ap()
    h_fm_hbm = nc.dram_tensor("h_fm_hbm", [B, D, S], BF16).ap()
    xn_fm_hbm = nc.dram_tensor("xn_fm_hbm", [B, D, S], F32).ap()
    q_fm_hbm = nc.dram_tensor("q_fm_hbm", [B, D, S], BF16).ap()
    k_fm_hbm = nc.dram_tensor("k_fm_hbm", [B, D, S], BF16).ap()
    k_tm_hbm = nc.dram_tensor("k_tm_hbm", [B, S, D], BF16).ap()
    v_tm_hbm = nc.dram_tensor("v_tm_hbm", [B, S, D], BF16).ap()
    sc_hbm = nc.dram_tensor("sc_hbm", [B, S, 1], F32).ap()
    row_hbm = nc.dram_tensor("row_hbm", [2048, 1], F32).ap()

    p = Prog(nc)
    al = nc.alloc_sbuf_tensor
    wtmp = al("wtmp", [128, 512], F32).ap()
    wtmp2 = al("wtmp2", [128, 512], F32).ap()
    wqb = al("wqb", [128, 8, 1024], BF16).ap()
    wkb = al("wkb", [128, 8, 1024], BF16).ap()
    wvb = al("wvb", [128, 8, 1024], BF16).ap()
    wob = al("wob", [128, 8, 1024], BF16).ap()
    s_scw = al("s_scw", [128, 8, 1], F32).ap()
    s_scb = al("s_scb", [1, 1], F32).ap()
    s_wr = al("s_wr", [128, 8, 8], F32).ap()
    s_nw = al("s_nw", [128, 8], F32).ap()
    s_cst = {nm: al("s_" + nm, list(cst[nm].shape), F32).ap() for nm in cst}
    identb = al("identb", [128, 128], BF16).ap()
    ones_r = al("ones_r", [1, 128], F32).ap()
    scb_col = al("scb_col", [128, 1], F32).ap()
    eps5 = al("eps5", [128, 1], F32).ap()
    eps6 = al("eps6", [128, 1], F32).ap()

    xt = al("xt", [128, 1024], F32).ap()
    xs1 = al("xs1", [128, 1], F32).ap()
    xs2 = al("xs2", [128, 1], F32).ap()
    xnt = al("xnt", [128, 1024], F32).ap()
    xnb = al("xnb", [128, 1024], BF16).ap()
    hst = xt
    hfm_t = al("hfm_t", [128, 1024], F32).ap()
    hfm_b = al("hfm_b", [128, 1024], BF16).ap()
    xnfm_t = al("xnfm_t", [128, 1024], F32).ap()
    lgt = al("lgt", [128, 8], F32).ap()
    lm = al("lm", [128, 1], F32).ap()
    ldn = al("ldn", [128, 1], F32).ap()
    sc_cols = al("sc_cols", [128, 2, 16], F32).ap()

    actbuf = al("actbuf", [128, 8, 2048], BF16).ap()
    scb_b = actbuf.bitcast(F32).rearrange("p a b -> p (a b)")[:, 0:2048]
    vtmh = al("vtmh", [128, 16, 64], BF16).ap()
    qfm_c = al("qfm_c", [128, 2048], BF16).ap()
    kfm_c = al("kfm_c", [128, 2048], BF16).ap()
    ktt = al("ktt", [128, 1024], BF16).ap()
    trc4A = al("trc4A", [128, 4, 128], BF16).ap()
    rank_c = al("rank_c", [128, 1], F32).ap()
    cmp_t = xt
    memb_c = al("memb_c", [128, 16], F32).ap()
    memb_r = al("memb_r", [1, 2048], F32).ap()
    cum_r = al("cum_r", [1, 2048], F32).ap()
    zero1 = al("zero1", [1, 1], F32).ap()
    cum_c = al("cum_c", [128, 16], F32).ap()
    onehf = al("onehf", [128, 256], F32).ap()
    oneh_t = al("oneh_t", [128, 256], BF16).ap()
    pg_c = al("pg_c", [128, 2], F32).ap()
    pg_r = al("pg_r", [1, 256], F32).ap()
    pgb = al("pgb", [128, 256], F32).ap()
    kg_tm = al("kg_tm", [128, 2, 1024], BF16).ap()
    vg_tm = al("vg_tm", [128, 2, 1024], BF16).ap()
    kg_fm = al("kg_fm", [128, 8, 256], BF16).ap()
    gadd = al("gadd", [128, 256], F32).ap()
    gaddT = al("gaddT", [128, 512], F32).ap()
    sml = al("sml", [128, 512], F32).ap()
    smg = al("smg", [128, 256], F32).ap()
    m_a = al("m_a", [128, 1], F32).ap()
    m_b = al("m_b", [128, 1], F32).ap()
    m_c = al("m_c", [128, 1], F32).ap()
    m_col = al("m_col", [128, 16], F32).ap()

    mb_s = al("mb_s", [128, 2, 512], F32).ap()
    cmb = al("cmb", [128, 512], F32).ap()
    pT = al("pT", [128, 512], BF16).ap()
    vden = al("vden", [128, 16, 65], BF16).ap()
    vgden = al("vgden", [128, 2, 65], BF16).ap()

    fac_b = al("fac_b", [64, 512], F32).ap()
    s0t = al("s0t", [128, 512], F32).ap()
    sc_rowt = al("sc_rowt", [1, 2048], F32).ap()
    mneg_r = sc_rowt[0:1, 0:1024]
    den_r = sc_rowt[0:1, 1024:1536]
    fac_r = sc_rowt[0:1, 1536:2048]

    psA = nc.alloc_psum_tensor("psA", [128, 512], F32).ap()
    psB = nc.alloc_psum_tensor("psB", [128, 512], F32).ap()
    psC = nc.alloc_psum_tensor("psC", [128, 512], F32).ap()
    psD = nc.alloc_psum_tensor("psD", [128, 512], F32).ap()
    psE = nc.alloc_psum_tensor("psE", [128, 512], F32).ap()
    psF = nc.alloc_psum_tensor("psF", [128, 512], F32).ap()
    psG = nc.alloc_psum_tensor("psG", [128, 512], F32).ap()
    psH = nc.alloc_psum_tensor("psH", [128, 512], F32).ap()

    _wi = 0
    for nm, dst in (("wq", wqb), ("wk", wkb), ("wv", wvb), ("wo", wob)):
        for c in range(8):
            for hv in range(2):
                wt = wtmp if _wi % 2 == 0 else wtmp2
                _wi += 1
                p.dma("sync", wt, win[nm][:, c, hv * 512:(hv + 1) * 512])
                p.copy(dst[:, c, hv * 512:(hv + 1) * 512], wt)
    p.dma("sync", s_scw, win["scorer_w"])
    p.dma("sync", s_scb, win["scorer_b"])
    p.dma("sync", s_wr, win["Wr"])
    p.dma("sync", s_nw, win["norm_w"])
    for nm in cst:
        p.dma("sync", s_cst[nm], cst[nm])
    p.copy(identb, s_cst["ident_f32"])
    p.memset(ones_r, 1.0)
    p.memset(eps5, 1e-5)
    p.memset(eps6, 1e-6)
    p.mm(psE[:, 0:1], ones_r, s_scb, tag="bscb")
    p.copy(scb_col, psE[:, 0:1])

    def col_to_row(col_ap, row_ap, tag=""):
        p.tr(psE[0:1, 0:128], col_ap, s_cst["ident_f32"], tag=tag or "c2r")
        p.copy(row_ap, psE[0:1, 0:128], eng="scalar")

    for b in range(B):
        for t in range(NT):
            row0 = b * S + t * 128
            p.dma("sync", xt, x[row0:row0 + 128, :])
            p.act(hfm_t, xt, AF.Identity, accum_out=xs1)
            p.ts(xs1, xs1, 1.0 / D, None, ALU.mult)
            p.ts(xnt, xt, xs1[:, 0:1], None, ALU.subtract)
            p.act(hfm_t, xnt, AF.Square, accum_out=xs2)
            p.ts(xs2, xs2, 1.0 / D, None, ALU.mult)
            p.act(xs2, xs2, AF.Ln, bias=eps5[:, 0:1])
            p.act(xs2, xs2, AF.Exp, scale=-0.5)
            p.ts(xnt, xnt, xs2[:, 0:1], None, ALU.mult)
            p.copy(xnb, xnt)
            p.dma("gpsimd", xn_out[row0:row0 + 128, :], xnb)
            p.act(hfm_t, xnt, AF.Square, accum_out=xs1)
            p.ts(xs1, xs1, 1.0 / D, None, ALU.mult)
            p.act(xs1, xs1, AF.Ln, bias=eps6[:, 0:1])
            p.act(xs1, xs1, AF.Exp, scale=-0.5)
            p.ts(hst, xnt, xs1[:, 0:1], None, ALU.mult)
            for c in range(8):
                p.tr(psA[:, :128], xnt[:, c * 128:(c + 1) * 128], s_cst["ident_f32"])
                p.copy(xnfm_t[:, c * 128:(c + 1) * 128], psA[:, :128])
                p.tr(psB[:, :128], hst[:, c * 128:(c + 1) * 128], s_cst["ident_f32"])
                p.ts(hfm_t[:, c * 128:(c + 1) * 128], psB[:, :128], s_nw[:, c:c + 1], None, ALU.mult)
            p.copy(hfm_b, hfm_t)
            p.dma("gpsimd", xn_fm_hbm[b].rearrange("(c p) s -> p c s", p=128)[:, :, t * 128:(t + 1) * 128],
                  xnfm_t.rearrange("p (c q) -> p c q", c=8))
            p.dma("gpsimd", h_fm_hbm[b].rearrange("(c p) s -> p c s", p=128)[:, :, t * 128:(t + 1) * 128],
                  hfm_b.rearrange("p (c q) -> p c q", c=8))
            for c in range(8):
                p.mm(psC[:, 0:8], xnfm_t[:, c * 128:(c + 1) * 128], s_wr[:, c, :],
                     start=(c == 0), stop=(c == 7))
            p.red(lm, psC[:, 0:8], ALU.max, negate=True)
            p.act(lgt, psC[:, 0:8], AF.Exp, bias=lm[:, 0:1], accum_out=ldn)
            p.recip(ldn, ldn)
            p.ts(lgt, lgt, ldn[:, 0:1], None, ALU.mult)
            p.dma("gpsimd", scores_out[row0:row0 + 128, :], lgt)
            for c in range(8):
                p.mm(psD[:, 0:1], hfm_t[:, c * 128:(c + 1) * 128], s_scw[:, c, :],
                     start=(c == 0), stop=(c == 7))
            p.tt(sc_cols[:, b, t:t + 1], psD[:, 0:1], scb_col, ALU.add)
            p.dma("gpsimd", sc_hbm[b, t * 128:(t + 1) * 128, :], sc_cols[:, b, t:t + 1])

    p.fence_all()
    scale = 1.0 / 8.0
    o_fm = actbuf
    for b in range(B):
        p.dma("sync", actbuf, h_fm_hbm[b].rearrange("(c p) s -> p c s", p=128))
        for tc in range(4):
            tsl = slice(tc * 512, (tc + 1) * 512)
            for e in range(8):
                for c in range(8):
                    p.mm(psA, wqb[:, c, e * 128:(e + 1) * 128], actbuf[:, c, tsl],
                         start=(c == 0), stop=(c == 7))
                p.copy(qfm_c[:, 0:512], psA)
                p.dma("gpsimd", q_fm_hbm[b, e * 128:(e + 1) * 128, tsl], qfm_c[:, 0:512])
                for c in range(8):
                    p.mm(psB, wkb[:, c, e * 128:(e + 1) * 128], actbuf[:, c, tsl],
                         start=(c == 0), stop=(c == 7))
                p.copy(kfm_c[:, 0:512], psB)
                p.dma("gpsimd", k_fm_hbm[b, e * 128:(e + 1) * 128, tsl], kfm_c[:, 0:512])
                for j in range(4):
                    p.tr(psC.bitcast(BF16)[:, 0:128], kfm_c[:, j * 128:(j + 1) * 128], identb)
                    p.copy(trc4A[:, j, :], psC.bitcast(BF16)[:, 0:128])
                p.dma("gpsimd", k_tm_hbm[b, tc * 512:(tc + 1) * 512,
                                         e * 128:(e + 1) * 128].rearrange("(j p) q -> p j q", p=128),
                      trc4A)
            for j in range(4):
                t0 = tc * 512 + j * 128
                for c in range(8):
                    p.mm(psD, actbuf[:, c, t0:t0 + 128], wvb[:, c, 0:512],
                         start=(c == 0), stop=(c == 7))
                for c in range(8):
                    p.mm(psE, actbuf[:, c, t0:t0 + 128], wvb[:, c, 512:1024],
                         start=(c == 0), stop=(c == 7))
                p.copy(ktt[:, 0:512], psD)
                p.copy(ktt[:, 512:1024], psE)
                p.dma("gpsimd", v_tm_hbm[b, t0:t0 + 128, :], ktt)
        p.dma("sync", sc_rowt, sc_hbm[b].rearrange("s one -> one s"))
        for j in range(4):
            p.mm(psF, ones_r, sc_rowt[0:1, j * 512:(j + 1) * 512], tag="bcast")
            p.copy(scb_b[:, j * 512:(j + 1) * 512], psF, eng="scalar")
        for t in range(NT):
            p.ts(cmp_t, scb_b[:, 0:1024], sc_cols[:, b, t:t + 1], None, ALU.is_gt)
            p.red(rank_c, cmp_t, ALU.add)
            p.ts(cmp_t, scb_b[:, 1024:2048], sc_cols[:, b, t:t + 1], None, ALU.is_gt)
            p.red(m_a, cmp_t, ALU.add)
            p.tt(rank_c, rank_c, m_a, ALU.add)
            p.ts(memb_c[:, t:t + 1], rank_c, float(G), None, ALU.is_lt)
            col_to_row(memb_c[:, t:t + 1], memb_r[0:1, t * 128:(t + 1) * 128])
        p.memset(zero1, 0.0)
        p.scan(cum_r, memb_r, zero1.broadcast_to((1, 2048)), 0.0, ALU.add, ALU.add)
        p.dma("gpsimd", row_hbm.rearrange("s one -> one s"), cum_r)
        for t in range(NT):
            p.dma("sync", cum_c[:, t:t + 1], row_hbm[t * 128:(t + 1) * 128, :])
        for t in range(NT):
            p.ts(onehf, s_cst["iota256_row"], cum_c[:, t:t + 1], memb_c[:, t:t + 1],
                 ALU.is_equal, ALU.mult)
            p.copy(oneh_t, onehf)
            p.mm(psG[:, 0:1], onehf[:, 0:128], s_cst["qidx_cols"][:, t:t + 1],
                 start=(t == 0), stop=(t == NT - 1), tag="pg0")
            p.mm(psH[:, 0:1], onehf[:, 128:256], s_cst["qidx_cols"][:, t:t + 1],
                 start=(t == 0), stop=(t == NT - 1), tag="pg1")
            p.dma("sync", ktt, k_tm_hbm[b, t * 128:(t + 1) * 128, :])
            p.mm(psA, oneh_t[:, 0:128], ktt[:, 0:512], start=(t == 0), stop=(t == NT - 1), tag="kg00")
            p.mm(psB, oneh_t[:, 0:128], ktt[:, 512:1024], start=(t == 0), stop=(t == NT - 1), tag="kg01")
            p.mm(psD, oneh_t[:, 128:256], ktt[:, 0:512], start=(t == 0), stop=(t == NT - 1), tag="kg10")
            p.mm(psE, oneh_t[:, 128:256], ktt[:, 512:1024], start=(t == 0), stop=(t == NT - 1), tag="kg11")
        p.copy(pg_c[:, 0:1], psG[:, 0:1])
        p.copy(pg_c[:, 1:2], psH[:, 0:1])
        p.copy(kg_tm[:, 0, 0:512], psA)
        p.copy(kg_tm[:, 0, 512:1024], psB)
        p.copy(kg_tm[:, 1, 0:512], psD)
        p.copy(kg_tm[:, 1, 512:1024], psE)
        for t in range(NT):
            p.ts(onehf, s_cst["iota256_row"], cum_c[:, t:t + 1], memb_c[:, t:t + 1],
                 ALU.is_equal, ALU.mult)
            p.copy(oneh_t, onehf)
            p.dma("sync", ktt, v_tm_hbm[b, t * 128:(t + 1) * 128, :])
            p.mm(psA, oneh_t[:, 0:128], ktt[:, 0:512], start=(t == 0), stop=(t == NT - 1), tag="vg00")
            p.mm(psB, oneh_t[:, 0:128], ktt[:, 512:1024], start=(t == 0), stop=(t == NT - 1), tag="vg01")
            p.mm(psD, oneh_t[:, 128:256], ktt[:, 0:512], start=(t == 0), stop=(t == NT - 1), tag="vg10")
            p.mm(psE, oneh_t[:, 128:256], ktt[:, 512:1024], start=(t == 0), stop=(t == NT - 1), tag="vg11")
        p.copy(vg_tm[:, 0, 0:512], psA)
        p.copy(vg_tm[:, 0, 512:1024], psB)
        p.copy(vg_tm[:, 1, 0:512], psD)
        p.copy(vg_tm[:, 1, 512:1024], psE)
        col_to_row(pg_c[:, 0:1], pg_r[0:1, 0:128])
        col_to_row(pg_c[:, 1:2], pg_r[0:1, 128:256])
        p.mm(psF[:, 0:256], ones_r, pg_r, tag="bcastpg")
        p.copy(pgb, psF[:, 0:256], eng="scalar")
        for gc in range(2):
            for c in range(8):
                p.tr(psC.bitcast(BF16)[:, 0:128], kg_tm[:, gc, c * 128:(c + 1) * 128], identb)
                p.copy(kg_fm[:, c, gc * 128:(gc + 1) * 128], psC.bitcast(BF16)[:, 0:128])
        for kt in range(16):
            p.memset(vden[:, kt, 64:65], 1.0)
        for gc in range(2):
            p.memset(vgden[:, gc, 64:65], 1.0)
        for hc in range(8):
            p.dma("sync", qfm_c, q_fm_hbm[b, hc * 128:(hc + 1) * 128, :])
            p.dma("sync", kfm_c, k_fm_hbm[b, hc * 128:(hc + 1) * 128, :])
            for hh in range(2):
                h = 2 * hc + hh
                hp = 64 * hh
                p.dma("sync", vtmh, v_tm_hbm[b].rearrange("(t p) e -> p t e", p=128)[:, :, h * 64:(h + 1) * 64])
                for kt in range(16):
                    p.copy(vden[:, kt, 0:64], vtmh[:, kt, :])
                for gc in range(2):
                    p.copy(vgden[:, gc, 0:64], vg_tm[:, gc, h * 64:(h + 1) * 64])
                for w in range(2):
                    w0 = w * 1024
                    for tq in range(8):
                        t = w * 8 + tq
                        q0 = w0 + tq * 128
                        nfull = tq * 128
                        have_ma = False
                        for kk in range(0, nfull, 512):
                            nn = min(512, nfull - kk)
                            p.mm(psA[:, :nn], qfm_c[hp:hp + 64, q0:q0 + 128],
                                 kfm_c[hp:hp + 64, w0 + kk:w0 + kk + nn])
                            p.red(m_b if have_ma else m_a, psA[:, :nn], ALU.max)
                            if have_ma:
                                p.tt(m_a, m_a, m_b, ALU.max)
                            have_ma = True
                        p.mm(psA[:, :128], qfm_c[hp:hp + 64, q0:q0 + 128], kfm_c[hp:hp + 64, q0:q0 + 128])
                        p.stt(sml[:, :128], psA[:, :128], 1.0, s_cst["causal_add"], ALU.mult, ALU.add)
                        p.red(m_b, sml[:, :128], ALU.max)
                        if have_ma:
                            p.tt(m_a, m_a, m_b, ALU.max)
                        else:
                            p.copy(m_a, m_b)
                        p.mm(psB[:, 0:256], qfm_c[hp:hp + 64, q0:q0 + 128], kg_fm[hp:hp + 64, hc, :])
                        p.ts(gadd, pgb, s_cst["qidx_cols"][:, t:t + 1], NEG, ALU.is_gt, ALU.mult)
                        p.stt(smg, psB[:, 0:256], 1.0, gadd, ALU.mult, ALU.add)
                        p.red(m_c, smg, ALU.max)
                        p.tt(m_a, m_a, m_c, ALU.max)
                        p.ts(m_col[:, t:t + 1], m_a, scale, None, ALU.mult)
                    for tq in range(8):
                        t = w * 8 + tq
                        p.tr(psE[0:1, 0:128], m_col[:, t:t + 1], s_cst["ident_f32"], tag="c2rm")
                        p.ts(mneg_r[0:1, tq * 128:(tq + 1) * 128], psE[0:1, 0:128], -1.0, None,
                             ALU.mult, eng="scalar")
                    for qc in range(2):
                        p.mm(psF, ones_r, mneg_r[0:1, qc * 512:(qc + 1) * 512], tag="bcm")
                        p.copy(mb_s[:, qc, :], psF, eng="scalar")
                    for qc in range(2):
                        qa = w0 + qc * 512
                        for gc in range(2):
                            p.mm(psB, kg_fm[hp:hp + 64, hc, gc * 128:(gc + 1) * 128],
                                 qfm_c[hp:hp + 64, qa:qa + 512], tag="mmgT")
                            p.ts(gaddT, s_cst["iota_row2048"][:, qa:qa + 512], pg_c[:, gc:gc + 1],
                                 NEG, ALU.is_lt, ALU.mult)
                            p.tt(cmb, gaddT, mb_s[:, qc, :], ALU.add)
                            p.stt(sml, psB, scale, cmb, ALU.mult, ALU.add)
                            p.act(pT, sml, AF.Exp)
                            p.mm(psC[0:65, :], vgden[:, gc, :], pT, start=(gc == 0), stop=False, tag="avg")
                        valid_kts = [kt for kt in range(8) if max(kt * 128, qc * 512) < (qc + 1) * 512]
                        for kt in valid_kts:
                            k0 = w0 + kt * 128
                            lo = max(kt * 128, qc * 512)
                            nn = (qc + 1) * 512 - lo
                            off = lo - qc * 512
                            qb0 = w0 + lo
                            p.mm(psB[:, :nn], kfm_c[hp:hp + 64, k0:k0 + 128], qfm_c[hp:hp + 64, qb0:qb0 + nn],
                                 tag="mmlT")
                            if lo == kt * 128:
                                p.tt(cmb[:, :128], s_cst["causalT_add"], mb_s[:, qc, off:off + 128], ALU.add)
                                if nn > 128:
                                    p.copy(cmb[:, 128:nn], mb_s[:, qc, off + 128:off + nn])
                                p.stt(sml[:, :nn], psB[:, :nn], scale, cmb[:, :nn], ALU.mult, ALU.add)
                            else:
                                p.stt(sml[:, :nn], psB[:, :nn], scale, mb_s[:, qc, off:off + nn],
                                      ALU.mult, ALU.add)
                            p.act(pT[:, :nn], sml[:, :nn], AF.Exp)
                            p.mm(psC[0:65, off:off + nn], vden[:, w * 8 + kt, :], pT[:, :nn],
                                 start=False, stop=(kt == valid_kts[-1]), tag="avl")
                        p.copy(den_r, psC[64:65, 0:512])
                        p.recip(fac_r, den_r)
                        p.mm(psF, ones_r, fac_r, tag="bcf")
                        p.copy(fac_b, psF[0:64, :], eng="scalar")
                        p.tt(o_fm[hp:hp + 64, hc, qa:qa + 512], psC[0:64, :], fac_b, ALU.mult)

        for tc in range(4):
            tsl = slice(tc * 512, (tc + 1) * 512)
            for e in range(8):
                for c in range(8):
                    p.mm(psA, wob[:, c, e * 128:(e + 1) * 128], o_fm[:, c, tsl],
                         start=(c == 0), stop=(c == 7))
                p.dma("sync", s0t, xn_fm_hbm[b, e * 128:(e + 1) * 128, tsl])
                p.tt(s0t, s0t, psA, ALU.add)
                p.dma("gpsimd", s0_out[b, e * 128:(e + 1) * 128, tsl], s0t)
    p.emit()
    return nc


def run_phaseA(inputs, n_cores=8):
    w = inputs
    cA = host_consts_A()
    wA = wA_prep(w)
    x = np.asarray(w["L0_attn_x"], np.float32).reshape(16 * 2048, 1024)
    nc = build_phaseA()
    in_maps = []
    for c in range(n_cores):
        m = {"x": np.ascontiguousarray(x[c * 4096:(c + 1) * 4096])}
        m.update({k: v for k, v in wA.items()})
        m.update({k: v for k, v in cA.items()})
        in_maps.append(m)
    res = run_bass_kernel_spmd(nc, in_maps, core_ids=list(range(n_cores)))
    return res.results



def hostB_consts():
    tok = np.arange(32768, dtype=np.float32).reshape(128, 256)
    return {"iota_tokB": np.ascontiguousarray(tok),
            "iota_pB": np.arange(128, dtype=np.float32).reshape(128, 1),
            "identB": np.eye(128, dtype=np.float32)}


def wB_prep(w, e):
    return {
        "W1e": np.ascontiguousarray(np.asarray(w["L0_ffn_W1"][e], np.float32).reshape(8, 128, 4096).transpose(1, 0, 2)),
        "W2e": np.ascontiguousarray(np.asarray(w["L0_ffn_W2"][e], np.float32).reshape(32, 128, 1024).transpose(1, 0, 2)),
        "b1e": np.ascontiguousarray(np.asarray(w["L0_ffn_b1"][e], np.float32).reshape(32, 128).T),
        "b2e": np.asarray(w["L0_ffn_b2"][e], np.float32).reshape(1, 1024),
    }


def build_phaseB():
    nc = bass.Bass("TRN2", target_bir_lowering=False, name="phB")
    N, D, K = 32768, 1024, 8192
    NC = 64
    xn_all = nc.dram_tensor("xn_all", [N, D], BF16, kind="ExternalInput").ap()
    sc_e = nc.dram_tensor("sc_e", [128, 256], F32, kind="ExternalInput").ap()
    W1 = nc.dram_tensor("W1e", [128, 8, 4096], F32, kind="ExternalInput").ap()
    W2 = nc.dram_tensor("W2e", [128, 32, 1024], F32, kind="ExternalInput").ap()
    b1 = nc.dram_tensor("b1e", [128, 32], F32, kind="ExternalInput").ap()
    b2 = nc.dram_tensor("b2e", [1, 1024], F32, kind="ExternalInput").ap()
    iota_tok = nc.dram_tensor("iota_tokB", [128, 256], F32, kind="ExternalInput").ap()
    iota_p = nc.dram_tensor("iota_pB", [128, 1], F32, kind="ExternalInput").ap()
    identd = nc.dram_tensor("identB", [128, 128], F32, kind="ExternalInput").ap()
    Z = nc.dram_tensor("Z", [N, D], BF16, kind="ExternalOutput").ap()
    pair_hbm = nc.dram_tensor("pair_hbm", [K + 128, 2], F32).ap()
    rowb_hbm = nc.dram_tensor("rowb_hbm", [128, 1], F32).ap()

    p = Prog(nc)
    al = nc.alloc_sbuf_tensor
    W1b = al("W1b", [128, 8, 4096], BF16).ap()
    W2b = al("W2b", [128, 32, 1024], BF16).ap()
    wtmp = al("wtmpB", [128, 2048], F32).ap()
    _hbA = None
    s_b1 = al("s_b1", [128, 32], F32).ap()
    s_b2r = al("s_b2r", [1, 1024], F32).ap()
    b2b = al("b2b", [128, 1024], F32).ap()
    s_sc = al("s_sc", [128, 256], F32).ap()
    s_iota = al("s_iota", [128, 256], F32).ap()
    s_id = al("s_idB", [128, 128], F32).ap()
    identbB = al("identbB", [128, 128], BF16).ap()
    ones_rB = al("ones_rB", [1, 128], F32).ap()
    ones_cB = al("ones_cB", [128, 1], F32).ap()
    lo_c = al("lo_c", [128, 1], F32).ap()
    hi_c = al("hi_c", [128, 1], F32).ap()
    th_c = al("th_c", [128, 1], F32).ap()
    cnt_c = al("cnt_c", [128, 1], F32).ap()
    cgt_c = al("cgt_c", [128, 1], F32).ap()
    tmpa_c = al("tmpa_c", [128, 1], F32).ap()
    cmpb = al("cmpb", [128, 256], F32).ap()
    memb = al("memb", [128, 256], F32).ap()
    ws = al("ws", [128, 256], F32).ap()
    rowtot = al("rowtot", [128, 1], F32).ap()
    rowrow = al("rowrow", [1, 128], F32).ap()
    rowcum = al("rowcum", [1, 128], F32).ap()
    excl_c = al("excl_c", [128, 1], F32).ap()
    slots = al("slots", [128, 256], F32).ap()
    slot_i = al("slot_i", [128, 256], I32).ap()
    zero1B = al("zero1B", [1, 1], F32).ap()
    zcolB = al("zcolB", [128, 1], F32).ap()
    ovf_c = al("ovf_c", [128, 1], F32).ap()
    s_iotap = al("s_iotap", [128, 1], F32).ap()
    pairt2 = [al(f"pairt{s}", [128, 2], F32).ap() for s in range(4)]
    idcol2 = [al(f"idcol{s}", [128, 1], I32).ap() for s in range(4)]
    gcol2 = [al(f"gcol{s}", [128, 1], F32).ap() for s in range(4)]
    xg2 = [al(f"xg{s}", [128, 1024], BF16).ap() for s in range(2)]
    xgf2 = al("xgf2", [128, 8, 256], BF16).ap()
    xgf2b = al("xgf2b", [128, 8, 256], BF16).ap()
    hbuf2 = al("hbuf2", [128, 32, 256], BF16).ap()
    ybuf2 = [al(f"ybuf{s}", [128, 1024], BF16).ap() for s in range(4)]
    ytmp = al("ytmp", [128, 512], F32).ap()
    ytmp2 = al("ytmp2", [128, 512], F32).ap()
    ztile = al("ztile", [128, 4, 1024], BF16).ap()
    _hbA = wtmp.bitcast(BF16).rearrange("p (h x) -> p h x", h=16)
    _hbB = ztile.rearrange("p a b -> p (a b)").rearrange("p (h x) -> p h x", h=16)

    def hbuf2b(hc):
        return (_hbA if hc < 16 else _hbB)[:, hc % 16, :]

    psA = nc.alloc_psum_tensor("psA", [128, 512], F32).ap()
    psB = nc.alloc_psum_tensor("psB", [128, 512], F32).ap()
    psC = nc.alloc_psum_tensor("psC", [128, 512], F32).ap()
    psD = nc.alloc_psum_tensor("psD", [128, 512], F32).ap()
    psE = nc.alloc_psum_tensor("psE", [128, 512], F32).ap()
    psF = nc.alloc_psum_tensor("psF", [128, 512], F32).ap()
    psG = nc.alloc_psum_tensor("psG", [128, 512], F32).ap()
    psH = nc.alloc_psum_tensor("psH", [128, 512], F32).ap()

    for c in range(8):
        for hv in range(2):
            p.dma("sync", wtmp, W1[:, c, hv * 2048:(hv + 1) * 2048])
            p.copy(W1b[:, c, hv * 2048:(hv + 1) * 2048], wtmp)
    for c in range(32):
        p.dma("sync", wtmp[:, 0:1024], W2[:, c, :])
        p.copy(W2b[:, c, :], wtmp[:, 0:1024])
    p.dma("sync", s_b1, b1)
    p.dma("sync", s_b2r, b2)
    p.dma("sync", s_sc, sc_e)
    p.dma("sync", s_iota, iota_tok)
    p.dma("sync", s_iotap, iota_p)
    p.dma("sync", s_id, identd)
    p.copy(identbB, s_id)
    p.memset(ones_rB, 1.0)
    p.memset(ones_cB, 1.0)
    p.memset(zero1B, 0.0)
    p.memset(zcolB, 0.0)
    p.mm(psA[:, 0:1024].rearrange("p q -> p q") if False else psA, ones_rB, s_b2r[0:1, 0:512], tag="b2b0")
    p.copy(b2b[:, 0:512], psA)
    p.mm(psA, ones_rB, s_b2r[0:1, 512:1024], tag="b2b1")
    p.copy(b2b[:, 512:1024], psA)
    p.memset(ztile, 0.0)
    for t in range(64):
        p.dma("auto", Z[t * 512:(t + 1) * 512, :].rearrange("(j p) q -> p j q", p=128), ztile)

    p.fence_all()
    p.memset(lo_c, 0.0)
    p.memset(hi_c, 1.0)
    p.memset(th_c, 0.5)
    for it in range(40):
        p.ts(cmpb, s_sc, th_c[:, 0:1], None, ALU.is_gt)
        p.red(rowtot, cmpb, ALU.add)
        p.mm(psB[0:1, 0:1], rowtot, ones_cB, tag="cnt")
        p.mm(psC[:, 0:1], ones_rB, psB[0:1, 0:1], tag="cntb") if False else None
        p.copy(rowrow[0:1, 0:1], psB[0:1, 0:1])
        p.mm(psC[:, 0:1], ones_rB, rowrow[0:1, 0:1], tag="cntb")
        p.copy(cnt_c, psC[:, 0:1])
        p.ts(cgt_c, cnt_c, float(K), None, ALU.is_ge)
        p.tt(tmpa_c, th_c, lo_c, ALU.subtract)
        p.stt(lo_c, tmpa_c, cgt_c[:, 0:1], lo_c, ALU.mult, ALU.add)
        p.tt(tmpa_c, hi_c, th_c, ALU.subtract)
        p.stt(hi_c, tmpa_c, cgt_c[:, 0:1], th_c, ALU.mult, ALU.add)
        p.tt(th_c, lo_c, hi_c, ALU.add)
        p.ts(th_c, th_c, 0.5, None, ALU.mult)
    p.ts(memb, s_sc, lo_c[:, 0:1], None, ALU.is_gt)
    p.scan(ws, memb, zcolB.broadcast_to((128, 256)), 0.0, ALU.add, ALU.add)
    p.red(rowtot, memb, ALU.add)
    p.tr(psD[0:1, 0:128], rowtot, s_id, tag="c2rB")
    p.copy(rowrow, psD[0:1, 0:128], eng="scalar")
    p.scan(rowcum, rowrow, zero1B.broadcast_to((1, 128)), 0.0, ALU.add, ALU.add)
    p.tt(rowcum, rowcum, rowrow, ALU.subtract)
    p.dma("gpsimd", rowb_hbm.rearrange("s one -> one s"), rowcum)
    p.dma("sync", excl_c, rowb_hbm)
    p.ts(slots, ws, excl_c[:, 0:1], -1.0, ALU.add, ALU.add)
    p.ts(ovf_c, s_iotap, float(K), None, ALU.add)
    p.tt(cmpb, slots, ovf_c.broadcast_to((128, 256)), ALU.subtract)
    p.tt(cmpb, cmpb, memb, ALU.mult)
    p.tt(slots, cmpb, ovf_c.broadcast_to((128, 256)), ALU.add)
    p.copy(slot_i, slots)
    for f in range(256):
        pt = pairt2[f % 2]
        p.copy(pt[:, 0:1], s_iota[:, f:f + 1])
        p.copy(pt[:, 1:2], s_sc[:, f:f + 1])
        p.scatter(pair_hbm, pt, slot_i[:, f:f + 1], tag="scpair")
    for op in p.ops:
        if op.tag == "scpair":
            op.dma = True
    p.fence_all()
    for pr in range(NC // 2):
        par = pr % 2
        xgf_p = xgf2 if par == 0 else xgf2b
        pass
        ptr = psA if par == 0 else psF
        w1banks = (psB, psD) if par == 0 else (psG, psH)
        for s in range(2):
            ch = pr * 2 + s
            q = par * 2 + s
            p.dma("sync", pairt2[q], pair_hbm[ch * 128:(ch + 1) * 128, :])
            p.copy(idcol2[q], pairt2[q][:, 0:1])
            p.copy(gcol2[q], pairt2[q][:, 1:2])
            p.gather(xg2[s], xn_all, idcol2[q][:, 0:1], tag="gx")
            for c in range(8):
                p.tr(ptr.bitcast(BF16)[:, 0:128], xg2[s][:, c * 128:(c + 1) * 128], identbB)
                p.copy(xgf_p[:, c, s * 128:(s + 1) * 128], ptr.bitcast(BF16)[:, 0:128])
        def hb_p(hc, par=par):
            return hbuf2[:, hc, :] if par == 0 else hbuf2b(hc)
        for hc in range(32):
            pb = w1banks[hc % 2]
            for c in range(8):
                p.mm(pb[:, 0:256], W1b[:, c, hc * 128:(hc + 1) * 128], xgf_p[:, c, :],
                     start=(c == 0), stop=(c == 7))
            p.act(hb_p(hc), pb[:, 0:256], AF.Gelu, bias=s_b1[:, hc:hc + 1])
        for s in range(2):
            q = par * 2 + s
            for half in range(2):
                pc = psC if half == 0 else psE
                yt = ytmp if half == 0 else ytmp2
                for hc in range(32):
                    p.mm(pc, hb_p(hc)[:, s * 128:(s + 1) * 128],
                         W2b[:, hc, half * 512:(half + 1) * 512],
                         start=(hc == 0), stop=(hc == 31))
                p.tt(yt, pc, b2b[:, half * 512:(half + 1) * 512], ALU.add)
                p.ts(ybuf2[q][:, half * 512:(half + 1) * 512], yt, gcol2[q][:, 0:1], None, ALU.mult)
            p.scatter(Z, ybuf2[q], idcol2[q][:, 0:1], tag="scz")
    p.emit()
    return nc


def run_phaseB(inputs, xn_all, scores_all, n_cores=8):
    w = inputs
    cB = hostB_consts()
    nc = build_phaseB()
    in_maps = []
    for e in range(n_cores):
        m = dict(cB)
        m.update(wB_prep(w, e))
        m["xn_all"] = xn_all
        m["sc_e"] = np.ascontiguousarray(scores_all[:, e].reshape(128, 256))
        in_maps.append(m)
    res = run_bass_kernel_spmd(nc, in_maps, core_ids=list(range(n_cores)))
    return res.results



def wC_prep(w):
    Wq = np.asarray(w["L1_attn_Wqkv"], np.float32)[:, 0:1024]
    return {
        "wqkv_q": np.ascontiguousarray(Wq.reshape(8, 128, 1024).transpose(1, 0, 2)),
        "bq_c": np.ascontiguousarray(np.asarray(w["L1_attn_bqkv"], np.float32)[0:1024].reshape(8, 128).T),
        "wlif": np.asarray(w["L1_attn_Wlif"], np.float32),
        "blif_r": np.asarray(w["L1_attn_blif"], np.float32).reshape(1, 16),
    }


def build_phaseC():
    nc = bass.Bass("TRN2", target_bir_lowering=False, name="phC")
    B, S, D = 2, 2048, 1024
    x = nc.dram_tensor("x", [B * S, D], F32, kind="ExternalInput").ap()
    s0 = nc.dram_tensor("s0_fm", [B, D, S], F32, kind="ExternalInput").ap()
    Zp = nc.dram_tensor("Zp", [8, B * S, D], BF16, kind="ExternalInput").ap()
    wq = nc.dram_tensor("wqkv_q", [128, 8, 1024], F32, kind="ExternalInput").ap()
    bq = nc.dram_tensor("bq_c", [128, 8], F32, kind="ExternalInput").ap()
    wlif = nc.dram_tensor("wlif", [16, 64], F32, kind="ExternalInput").ap()
    blif = nc.dram_tensor("blif_r", [1, 16], F32, kind="ExternalInput").ap()
    identd = nc.dram_tensor("identC", [128, 128], F32, kind="ExternalInput").ap()
    x1_out = nc.dram_tensor("x1_fm", [B, D, S], F32, kind="ExternalOutput").ap()
    xn1_out = nc.dram_tensor("xn1_fm", [B, D, S], BF16, kind="ExternalOutput").ap()
    xn1f_out = nc.dram_tensor("xn1f_fm", [B, D, S], F32, kind="ExternalOutput").ap()
    spk_out = nc.dram_tensor("spikes", [32, S], F32, kind="ExternalOutput").ap()
    I_hbm = nc.dram_tensor("I_hbm", [32, S], F32).ap()

    p = Prog(nc)
    al = nc.alloc_sbuf_tensor
    s_id = al("s_idC", [128, 128], F32).ap()
    wqb = al("wqbC", [128, 8, 1024], BF16).ap()
    wtmp = al("wtmpC", [128, 1024], F32).ap()
    s_bq = al("s_bq", [128, 8], F32).ap()
    s_wlif = al("s_wlif", [16, 64], F32).ap()
    wlifT2 = al("wlifT2", [128, 16], F32).ap()
    s_blif = al("s_blif", [1, 16], F32).ap()
    eps5C = al("eps5C", [128, 1], F32).ap()
    xt = al("xtC", [128, 1024], F32).ap()
    zt = al("ztC", [128, 1024], BF16).ap()
    acc = al("accC", [128, 1024], F32).ap()
    g0 = al("g0C", [128, 1024], F32).ap()
    s0tm = al("s0tm", [128, 1024], F32).ap()
    x1t = al("x1t", [128, 1024], F32).ap()
    xn1t = al("xn1t", [128, 1024], F32).ap()
    fmt = al("fmtC", [128, 1024], F32).ap()
    fmb = al("fmbC", [128, 1024], BF16).ap()
    fmt2 = al("fmt2C", [128, 1024], F32).ap()
    st1 = al("st1C", [128, 1], F32).ap()
    st2 = al("st2C", [128, 1], F32).ap()
    xnf = al("xnfC", [128, 8, 512], BF16).ap()
    qe = al("qeC", [128, 512], F32).ap()
    irow = al("irowC", [1, 512], F32).ap()
    Iall = al("IallC", [32, 2048], F32).ap()
    Mtraj = al("MtrajC", [32, 2048], F32).ap()
    mcol = al("mcolC", [32, 1], F32).ap()
    spkt = al("spktC", [32, 2048], F32).ap()

    psA = nc.alloc_psum_tensor("psA", [128, 512], F32).ap()
    psB = nc.alloc_psum_tensor("psB", [128, 512], F32).ap()
    psE = nc.alloc_psum_tensor("psE", [128, 512], F32).ap()

    p.dma("sync", s_id, identd)
    for c in range(8):
        p.dma("sync", wtmp, wq[:, c, :])
        p.copy(wqb[:, c, :], wtmp)
    p.dma("sync", s_bq, bq)
    p.dma("sync", s_wlif, wlif)
    p.dma("sync", s_blif, blif)
    p.memset(eps5C, 1e-5)
    p.tr(psE[0:64, 0:16], s_wlif, s_id[0:16, 0:16], tag="trwlif")
    p.copy(wlifT2[0:64, :], psE[0:64, 0:16])
    p.copy(wlifT2[64:128, :], psE[0:64, 0:16])

    NT = 16
    for b in range(B):
        for t in range(NT):
            row0 = b * S + t * 128
            p.dma("sync", xt, x[row0:row0 + 128, :])
            p.dma("sync", zt, Zp[0, row0:row0 + 128, :])
            p.copy(acc, zt)
            for e in range(1, 8):
                p.dma("sync", zt, Zp[e, row0:row0 + 128, :])
                p.tt(acc, acc, zt, ALU.add)
            p.act(g0, acc, AF.Silu)
            p.dma("sync", fmt.rearrange("p (c q) -> p c q", c=8),
                  s0[b].rearrange("(c p) s -> p c s", p=128)[:, :, t * 128:(t + 1) * 128])
            for c in range(8):
                p.tr(psA[:, 0:128], fmt[:, c * 128:(c + 1) * 128], s_id, tag="trs0")
                p.copy(s0tm[:, c * 128:(c + 1) * 128], psA[:, 0:128])
            p.tt(x1t, s0tm, g0, ALU.mult)
            p.tt(x1t, x1t, xt, ALU.add)
            p.act(fmt, x1t, AF.Identity, accum_out=st1)
            p.ts(st1, st1, 1.0 / D, None, ALU.mult)
            p.ts(xn1t, x1t, st1[:, 0:1], None, ALU.subtract)
            p.act(fmt, xn1t, AF.Square, accum_out=st2)
            p.ts(st2, st2, 1.0 / D, None, ALU.mult)
            p.act(st2, st2, AF.Ln, bias=eps5C[:, 0:1])
            p.act(st2, st2, AF.Exp, scale=-0.5)
            p.ts(xn1t, xn1t, st2[:, 0:1], None, ALU.mult)
            for c in range(8):
                p.tr(psA[:, 0:128], x1t[:, c * 128:(c + 1) * 128], s_id)
                p.copy(fmt[:, c * 128:(c + 1) * 128], psA[:, 0:128])
                p.tr(psB[:, 0:128], xn1t[:, c * 128:(c + 1) * 128], s_id)
                p.copy(fmb[:, c * 128:(c + 1) * 128], psB[:, 0:128])
                p.copy(fmt2[:, c * 128:(c + 1) * 128], psB[:, 0:128])
            p.dma("gpsimd", x1_out[b].rearrange("(c p) s -> p c s", p=128)[:, :, t * 128:(t + 1) * 128],
                  fmt.rearrange("p (c q) -> p c q", c=8))
            p.dma("gpsimd", xn1_out[b].rearrange("(c p) s -> p c s", p=128)[:, :, t * 128:(t + 1) * 128],
                  fmb.rearrange("p (c q) -> p c q", c=8))
            p.dma("gpsimd", xn1f_out[b].rearrange("(c p) s -> p c s", p=128)[:, :, t * 128:(t + 1) * 128],
                  fmt2.rearrange("p (c q) -> p c q", c=8))
    p.fence_all()
    for b in range(B):
        for tc in range(4):
            tsl = slice(tc * 512, (tc + 1) * 512)
            p.dma("sync", xnf, xn1_out[b].rearrange("(c p) s -> p c s", p=128)[:, :, tsl])
            for e in range(8):
                for c in range(8):
                    p.mm(psA, wqb[:, c, e * 128:(e + 1) * 128], xnf[:, c, :],
                         start=(c == 0), stop=(c == 7))
                p.ts(qe, psA, s_bq[:, e:e + 1], None, ALU.add)
                for hh in range(2):
                    h = 2 * e + hh
                    hp = 64 * hh
                    p.mm(psB[0:1, :], wlifT2[hp:hp + 64, h:h + 1], qe[hp:hp + 64, :], tag="iq")
                    p.ts(irow, psB[0:1, :], s_blif[0:1, h:h + 1], None, ALU.add)
                    p.dma("gpsimd", I_hbm[b * 16 + h:b * 16 + h + 1, tsl], irow)
    p.fence_all()
    p.dma("sync", Iall, I_hbm)
    p.memset(mcol, 0.0)
    for t in range(2048):
        p.stt(Mtraj[:, t:t + 1], mcol, 0.9, Iall[:, t:t + 1], ALU.mult, ALU.add)
        p.stt(mcol, Mtraj[:, t:t + 1], 1.0, Mtraj[:, t:t + 1], ALU.is_lt, ALU.mult)
    p.ts(spkt, Mtraj, 1.0, None, ALU.is_ge)
    p.dma("gpsimd", spk_out, spkt)
    p.emit()
    return nc


def run_phaseC(inputs, x_shards, s0_shards, Zall, n_cores=8):
    w = inputs
    wC = wC_prep(w)
    nc = build_phaseC()
    in_maps = []
    for c in range(n_cores):
        m = dict(wC)
        m["identC"] = np.eye(128, dtype=np.float32)
        m["x"] = x_shards[c]
        m["s0_fm"] = s0_shards[c]
        m["Zp"] = np.ascontiguousarray(Zall[:, c * 4096:(c + 1) * 4096, :])
        in_maps.append(m)
    res = run_bass_kernel_spmd(nc, in_maps, core_ids=list(range(n_cores)))
    return res.results



def hostC2_consts():
    cos, sin, rsc = _rot_tables()
    i = np.arange(128)
    return {
        "cos_rsc": np.ascontiguousarray((cos * rsc).astype(np.float32)),
        "sin_rsc": np.ascontiguousarray((sin * rsc).astype(np.float32)),
        "cos_div": np.ascontiguousarray((cos / rsc).astype(np.float32)),
        "sin_div": np.ascontiguousarray((sin / rsc).astype(np.float32)),
        "identC2": np.eye(128, dtype=np.float32),
        "causal_addC2": np.where(i[None, :] <= i[:, None], 0.0, NEG).astype(np.float32),
        "causalT_addC2": np.where(i[:, None] <= i[None, :], 0.0, NEG).astype(np.float32),
        "cam512C2": np.concatenate(
            [np.where(i[:, None] <= i[None, :], 0.0, NEG).astype(np.float32),
             np.zeros((128, 384), np.float32)], axis=1),
        "cat4C2": np.tile(np.where(i[None, :] <= i[:, None], 0.0, NEG).astype(np.float32), (1, 4)),
        "catzC2": np.concatenate(
            [np.where(i[:, None] <= i[None, :], 0.0, NEG).astype(np.float32),
             np.zeros((128, 384), np.float32)], axis=1),
        "zcaC2": np.concatenate(
            [np.zeros((128, 384), np.float32),
             np.where(i[None, :] <= i[:, None], 0.0, NEG).astype(np.float32)], axis=1),
    }


def _rot_tables(n=2048, DH=64, WINDOW=512):
    half = np.arange(0, DH, 2, dtype=np.float32)
    inv_freq = 1.0 / 10000.0 ** (half / DH)
    t = np.arange(n, dtype=np.float32)
    freqs = t[:, None] * inv_freq[None, :]
    freqs = np.concatenate([freqs, freqs], axis=-1)
    base = (half + 0.4 * DH) / (1.4 * DH)
    power = (t - n // 2) / float(WINDOW // 2)
    scale = base[None, :] ** power[:, None]
    scale = np.concatenate([scale, scale], axis=-1).astype(np.float32)
    return np.cos(freqs).astype(np.float32), np.sin(freqs).astype(np.float32), scale


def wC2_prep(w):
    d = {}
    for l in range(2):
        d[f"wqkv1_{l}"] = np.ascontiguousarray(np.asarray(w["L1_ffn_wqkv"][l], np.float32).reshape(8, 128, 1536).transpose(1, 0, 2))
        d[f"wo1_{l}"] = np.ascontiguousarray(np.asarray(w["L1_ffn_wo"][l], np.float32).reshape(4, 128, 1024).transpose(1, 0, 2))
        w1 = np.asarray(w["L1_ffn_w1"][l], np.float32)
        w1a = np.zeros((1024, 2816), np.float32); w1a[:, :2730] = w1[:, :2730]
        w1g = np.zeros((1024, 2816), np.float32); w1g[:, :2730] = w1[:, 2730:]
        d[f"w1a_{l}"] = np.ascontiguousarray(w1a.reshape(8, 128, 2816).transpose(1, 0, 2))
        d[f"w1g_{l}"] = np.ascontiguousarray(w1g.reshape(8, 128, 2816).transpose(1, 0, 2))
        w2 = np.zeros((2816, 1024), np.float32); w2[:2730] = np.asarray(w["L1_ffn_w2"][l], np.float32)
        d[f"w2_{l}"] = np.ascontiguousarray(w2.reshape(22, 128, 1024).transpose(1, 0, 2))
        b1 = np.asarray(w["L1_ffn_b1"][l], np.float32)
        b1a = np.zeros(2816, np.float32); b1a[:2730] = b1[:2730]
        b1g = np.zeros(2816, np.float32); b1g[:2730] = b1[2730:]
        d[f"b1a_{l}"] = np.ascontiguousarray(b1a.reshape(22, 128).T)
        d[f"b1g_{l}"] = np.ascontiguousarray(b1g.reshape(22, 128).T)
        d[f"b2_{l}"] = np.ascontiguousarray(np.asarray(w["L1_ffn_b2"][l], np.float32).reshape(8, 128).T)
        d[f"lg_{l}"] = np.ascontiguousarray(np.asarray(w["L1_ffn_ln_g"][l], np.float32).reshape(8, 128).T)
        d[f"lb_{l}"] = np.ascontiguousarray(np.asarray(w["L1_ffn_ln_b"][l], np.float32).reshape(8, 128).T)
        d[f"qs_{l}"] = np.asarray(w["L1_ffn_q_scale"][l], np.float32).reshape(1, 64)
        d[f"ks_{l}"] = np.asarray(w["L1_ffn_k_scale"][l], np.float32).reshape(1, 64)
    return d


def build_phaseC2():
    nc = bass.Bass("TRN2", target_bir_lowering=False, name="phC2")
    B, S, D = 2, 2048, 1024
    INNER, H = 512, 8
    xn1f = nc.dram_tensor("xn1f_fm", [B, D, S], F32, kind="ExternalInput").ap()
    ins = {}
    for l in range(2):
        for nm, shp in ((f"wqkv1_{l}", [128, 8, 1536]), (f"wo1_{l}", [128, 4, 1024]),
                        (f"w1a_{l}", [128, 8, 2816]), (f"w1g_{l}", [128, 8, 2816]),
                        (f"w2_{l}", [128, 22, 1024]), (f"b1a_{l}", [128, 22]),
                        (f"b1g_{l}", [128, 22]), (f"b2_{l}", [128, 8]),
                        (f"lg_{l}", [128, 8]), (f"lb_{l}", [128, 8]),
                        (f"qs_{l}", [1, 64]), (f"ks_{l}", [1, 64])):
            ins[nm] = nc.dram_tensor(nm, shp, F32, kind="ExternalInput").ap()
    for nm, shp in (("cos_rsc", [S, 64]), ("sin_rsc", [S, 64]), ("cos_div", [S, 64]),
                    ("sin_div", [S, 64]), ("identC2", [128, 128]),
                    ("causal_addC2", [128, 128]), ("causalT_addC2", [128, 128]),
                    ("cam512C2", [128, 512]), ("cat4C2", [128, 512]),
                    ("catzC2", [128, 512]), ("zcaC2", [128, 512])):
        ins[nm] = nc.dram_tensor(nm, shp, F32, kind="ExternalInput").ap()
    g1_out = nc.dram_tensor("g1_fm", [B, D, S], BF16, kind="ExternalOutput").ap()
    xfa = nc.dram_tensor("xfa", [B, D, S], F32).ap()
    xfb = nc.dram_tensor("xfb", [B, D, S], F32).ap()
    sh_hbm = nc.dram_tensor("sh_hbm", [B, D, S], BF16).ap()
    shf_hbm = nc.dram_tensor("shf_hbm", [B, D, S], F32).ap()
    qf_hbm = nc.dram_tensor("qf_hbm", [B, INNER, S], BF16).ap()
    kf_hbm = nc.dram_tensor("kf_hbm", [B, INNER, S], BF16).ap()
    vf_hbm = nc.dram_tensor("vf_hbm", [B, S, INNER], BF16).ap()
    tab_hbm = nc.dram_tensor("tab_hbm", [4, S, 64], F32).ap()
    o_hbm = nc.dram_tensor("o_hbm", [INNER, S], BF16).ap()


    p = Prog(nc)
    al = nc.alloc_sbuf_tensor
    warena = al("warena", [128, 67584], BF16).ap()
    s_id = al("s_idC2", [128, 128], F32).ap()
    identb = al("identbC2", [128, 128], BF16).ap()
    s_ca = al("s_caC2", [128, 128], F32).ap()
    s_cat = al("s_catC2", [128, 128], F32).ap()
    s_cam = al("s_camC2", [128, 512], F32).ap()
    s_cat4 = al("s_cat4C2", [128, 512], F32).ap()
    s_catz = al("s_catzC2", [128, 512], F32).ap()
    s_zca = al("s_zcaC2", [128, 512], F32).ap()
    wtmp = None
    ones_r = al("ones_rC2", [1, 128], F32).ap()
    ones_c = al("ones_cC2", [128, 1], F32).ap()
    eps5 = al("eps5C2", [128, 1], F32).ap()
    qsb = al("qsbC2", [128, 64], F32).ap()
    ksb = al("ksbC2", [128, 64], F32).ap()
    rotq = al("rotqC2", [128, 64], F32).ap()
    rotk = al("rotkC2", [128, 64], F32).ap()
    zc = al("zcC2", [128, 4], BF16).ap()
    zcf = al("zcfC2", [128, 4], F32).ap()
    s_small = {}
    for l in range(2):
        for nm in (f"b1a_{l}", f"b1g_{l}"):
            s_small[nm] = al("s_" + nm, [128, 22], F32).ap()
        for nm in (f"b2_{l}", f"lg_{l}", f"lb_{l}"):
            s_small[nm] = al("s_" + nm, [128, 8], F32).ap()
        for nm in (f"qs_{l}", f"ks_{l}"):
            s_small[nm] = al("s_" + nm, [1, 64], F32).ap()
    shj = None
    sarena = al("sarenaC2", [128, 8704], F32).ap()
    tabraw = sarena[:, 0:1024].rearrange("p (t d) -> p t d", t=16)
    shj = sarena.bitcast(BF16)[:, 6144:7168].rearrange("p (c q) -> p c q", c=8)
    wtmp = sarena[:, 7040:8448]
    qkvt = sarena[:, 0:1536]
    rotb = sarena[:, 1536:2048]
    tmpa = sarena[:, 2048:2560]
    tmpb = sarena[:, 2560:3072]
    ssb = al("ssbC2", [128, 8], F32).ap()
    nrm = al("nrmC2", [128, 8], F32).ap()
    qout = al("qoutC2", [128, 512], BF16).ap()
    qfm_c = al("qfm_cC2", [128, 2048], BF16).ap()
    kfm_c = al("kfm_cC2", [128, 2048], BF16).ap()
    o4 = al("o4C2", [128, 4, 512], BF16).ap()
    sml = sarena[:, 3072:3584]
    m_a = al("m_aC2", [128, 1], F32).ap()
    m_b = al("m_bC2", [128, 1], F32).ap()
    m_col = al("m_colC2", [128, 4], F32).ap()
    rowbuf = al("rowbufC2", [1, 1536], F32).ap()
    mneg_r = rowbuf[0:1, 0:512]
    den_r = rowbuf[0:1, 512:1024]
    fac_r = rowbuf[0:1, 1024:1536]
    mb_s = sarena[:, 4096:4608]
    cmb = sarena[:, 3584:4096]
    xcur = al("xcurC2", [128, 512], F32).ap()
    sqt = sarena[:, 4608:5120]
    sbf = sarena.bitcast(BF16)
    glu = sbf[:, 0:11264].rearrange("p (h x) -> p h x", h=22)
    hln = sbf[:, 11264:15360].rearrange("p (h x) -> p h x", h=8)
    abuf = sarena[:, 7680:8192]
    gbuf = sarena[:, 8192:8704]
    mub = qfm_c.bitcast(F32)[:, 0:512]
    rsb = qfm_c.bitcast(F32)[:, 512:1024]
    msq = kfm_c.bitcast(F32)[:, 0:512]
    abuf2 = kfm_c.bitcast(F32)[:, 512:1024]
    gbuf2 = o4.bitcast(F32).rearrange("p a b -> p (a b)")[:, 0:512]
    stat = rowbuf

    psA = nc.alloc_psum_tensor("psA", [128, 512], F32).ap()
    psB = nc.alloc_psum_tensor("psB", [128, 512], F32).ap()
    psC = nc.alloc_psum_tensor("psC", [128, 512], F32).ap()
    psD = nc.alloc_psum_tensor("psD", [128, 512], F32).ap()
    psE = nc.alloc_psum_tensor("psE", [128, 512], F32).ap()
    psF = nc.alloc_psum_tensor("psF", [128, 512], F32).ap()
    psG = nc.alloc_psum_tensor("psG", [128, 512], F32).ap()
    psH = nc.alloc_psum_tensor("psH", [128, 512], F32).ap()

    _dbase = [16384]

    def dal(nb):
        bb = (_dbase[0] + 3) & ~3
        _dbase[0] = bb + nb
        assert _dbase[0] <= 67584, _dbase[0]
        return bb

    wscf = warena.bitcast(F32)

    def dbf(n):
        bb = dal(n)
        return warena[:, bb:bb + n]

    def df32(n):
        bb = dal(2 * n)
        return wscf[:, bb // 2:bb // 2 + n]

    shj2 = [dbf(1024).rearrange("p (c q) -> p c q", c=8) for _ in range(2)]
    qkvt2 = [df32(1536) for _ in range(2)]
    sqt2 = [df32(512) for _ in range(2)]
    rotb2 = [df32(512) for _ in range(2)]
    tmpa2 = [df32(512) for _ in range(2)]
    tmpb2 = [df32(512) for _ in range(2)]
    tab42 = [df32(256).rearrange("p (t d) -> p t d", t=4) for _ in range(2)]
    qout2 = [dbf(512) for _ in range(2)]
    trc42 = [dbf(512).rearrange("p (c q) -> p c q", c=4) for _ in range(2)]
    ssb2 = [df32(8) for _ in range(2)]
    nrm2 = [df32(8) for _ in range(2)]
    qkb2 = [dbf(1024) for _ in range(2)]
    vb2 = [dbf(512) for _ in range(2)]
    vtmh2 = [dbf(1024).rearrange("p (t e) -> p t e", t=16) for _ in range(2)]
    smlj2 = [[df32(512) for _ in range(4)] for _ in range(2)]
    mb_s2 = [df32(512) for _ in range(2)]
    pTrow2 = [dbf(2560).rearrange("p (s x) -> p s x", s=4) for _ in range(2)]
    ptile2 = [dbf(1024).rearrange("p (s x) -> p s x", s=8) for _ in range(2)]
    obuf2 = [dbf(512) for _ in range(2)]
    m_a2 = [df32(4) for _ in range(2)]
    m_b2_ = [df32(4) for _ in range(2)]
    m_col2 = [df32(4) for _ in range(2)]
    mneg82 = [df32(4) for _ in range(2)]
    den_a2 = [df32(4) for _ in range(2)]
    den_b2 = [df32(4) for _ in range(2)]
    fac42 = [df32(4) for _ in range(2)]
    xcur2 = [df32(512) for _ in range(2)]
    o4b = dbf(2048).rearrange("p (c s) -> p c s", c=4)

    def zipper(bodyA, bodyB, skew=0):
        mk = len(p.ops)
        bodyA()
        A = p.ops[mk:]
        del p.ops[mk:]
        bodyB()
        Bl = p.ops[mk:]
        del p.ops[mk:]
        ia = ib = 0
        while ia < min(skew, len(A)):
            p.ops.append(A[ia])
            ia += 1
        while ia < len(A) or ib < len(Bl):
            if ia < len(A):
                p.ops.append(A[ia])
                ia += 1
            if ib < len(Bl):
                p.ops.append(Bl[ib])
                ib += 1

    p.dma("sync", s_id, ins["identC2"])
    p.copy(identb, s_id)
    p.dma("sync", s_ca, ins["causal_addC2"])
    p.dma("sync", s_cat, ins["causalT_addC2"])
    p.dma("sync", s_cam, ins["cam512C2"])
    p.dma("sync", s_cat4, ins["cat4C2"])
    p.dma("sync", s_catz, ins["catzC2"])
    p.dma("sync", s_zca, ins["zcaC2"])
    p.memset(ones_r, 1.0)
    p.memset(ones_c, 1.0)
    p.memset(eps5, 1e-5)
    p.memset(zc, 0.0)
    p.memset(zcf, 0.0)
    for l in range(2):
        for nm in (f"b1a_{l}", f"b1g_{l}", f"b2_{l}", f"lg_{l}", f"lb_{l}", f"qs_{l}", f"ks_{l}"):
            p.dma("sync", s_small[nm], ins[nm])

    def wv(off, sub, width):
        return warena[:, off + sub * width:off + (sub + 1) * width]
    OFF_QKV = 0
    OFF_WO = 8 * 1536
    OFF_W1A = 0
    OFF_W1G = 8 * 2816
    OFF_W2 = 16 * 2816

    def shift_copy(dst, srcb, b, cast):
        p.dma("gpsimd" if cast else "sync", dst[b, 0:512, :], srcb[b, 0:512, :])
        p.dma("gpsimd" if cast else "sync", dst[b, 512:1024, 1:2048], srcb[b, 512:1024, 0:2047])
        zz = zc if cast else zcf
        p.dma("gpsimd", dst[b, 512:1024, 0:1].rearrange("(a c) one -> a (c one)", c=4), zz, slow=True)

    for l in range(2):
        src = xn1f if l == 0 else xfb
        for c in range(8):
            for hv in range(2):
                p.dma("sync", wtmp[:, 0:768], ins[f"wqkv1_{l}"][:, c, hv * 768:(hv + 1) * 768])
                p.copy(wv(OFF_QKV, c, 1536)[:, hv * 768:(hv + 1) * 768], wtmp[:, 0:768])
        for c in range(4):
            p.dma("sync", wtmp[:, 0:1024], ins[f"wo1_{l}"][:, c, :])
            p.copy(wv(OFF_WO, c, 1024), wtmp[:, 0:1024])
        p.mm(psA[:, 0:64], ones_r, s_small[f"qs_{l}"], tag="qsb")
        p.copy(qsb, psA[:, 0:64])
        p.mm(psA[:, 0:64], ones_r, s_small[f"ks_{l}"], tag="ksb")
        p.copy(ksb, psA[:, 0:64])
        p.copy(rotq[:, 0:32], qsb[:, 32:64])
        p.copy(rotq[:, 32:64], qsb[:, 0:32])
        p.copy(rotk[:, 0:32], ksb[:, 32:64])
        p.copy(rotk[:, 32:64], ksb[:, 0:32])
        for tbl, cstn, mulb in ((0, "cos_rsc", qsb), (1, "sin_rsc", rotq),
                                (2, "cos_div", ksb), (3, "sin_div", rotk)):
            p.dma("sync", tabraw, ins[cstn].rearrange("(t p) d -> p t d", p=128))
            for tt_ in range(16):
                p.tt(tabraw[:, tt_, :], tabraw[:, tt_, :], mulb, ALU.mult)
            p.dma("gpsimd", tab_hbm[tbl].rearrange("(t p) d -> p t d", p=128), tabraw)
        for b in range(B):
            shift_copy(sh_hbm, src, b, cast=True)

            def qkv_body(t):
                r = t % 2
                shj_ = shj2[r]
                tab4_ = tab42[r]
                qkvt_ = qkvt2[r]
                sqt_ = sqt2[r]
                ssb_ = ssb2[r]
                nrm_ = nrm2[r]
                rotb_ = rotb2[r]
                tmpa_ = tmpa2[r]
                tmpb_ = tmpb2[r]
                qkb_ = qkb2[r]
                vb_ = vb2[r]
                trc4_ = trc42[r]
                ps3 = ((psA, psB, psC), (psD, psE, psG))[r]
                p.dma("sync", shj_, sh_hbm[b].rearrange("(c p) s -> p c s", p=128)[:, :, t * 128:(t + 1) * 128])
                p.dma("scalar", tab4_, tab_hbm[:, t * 128:(t + 1) * 128, :].rearrange("tb p d -> p tb d"))
                for third in range(3):
                    ps = ps3[third]
                    for c in range(8):
                        p.mm(ps, shj_[:, c, :],
                             wv(OFF_QKV, c, 1536)[:, third * 512:(third + 1) * 512],
                             start=(c == 0), stop=(c == 7))
                p.copy(qkvt_[:, 0:512], ps3[0], eng="scalar")
                p.copy(qkvt_[:, 512:1024], ps3[1], eng="scalar")
                p.copy(vb_, ps3[2])
                p.dma("gpsimd", vf_hbm[b, t * 128:(t + 1) * 128, :], vb_)
                for base, scb, t1, t2 in ((0, qsb, 0, 1), (512, ksb, 2, 3)):
                    seg = qkvt_[:, base:base + 512]
                    segv = seg.rearrange("p (h x) -> p h x", h=8)
                    p.act(sqt_, seg, AF.Square)
                    p.red(ssb_, sqt_.rearrange("p (h x) -> p h x", h=8), ALU.add)
                    p.act(nrm_, ssb_, AF.Sqrt)
                    p.ts(nrm_, nrm_, 1e-12, None, ALU.max)
                    p.recip(nrm_, nrm_)
                    for h in range(8):
                        if h % 2 == 0:
                            p.ts(seg[:, h * 64:(h + 1) * 64], seg[:, h * 64:(h + 1) * 64],
                                 nrm_[:, h:h + 1], None, ALU.mult)
                        else:
                            p.act(seg[:, h * 64:(h + 1) * 64], seg[:, h * 64:(h + 1) * 64],
                                  AF.Identity, scale=nrm_[:, h:h + 1])
                    rotv = rotb_.rearrange("p (h x) -> p h x", h=8)
                    p.ts(rotv[:, :, 0:32], segv[:, :, 32:64], -1.0, None, ALU.mult)
                    p.copy(rotv[:, :, 32:64], segv[:, :, 0:32], eng="scalar")
                    tb1 = tab4_[:, t1, :].unsqueeze(1).broadcast_to((128, 8, 64))
                    tb2 = tab4_[:, t2, :].unsqueeze(1).broadcast_to((128, 8, 64))
                    p.tt(tmpa_.rearrange("p (h x) -> p h x", h=8), segv, tb1, ALU.mult)
                    p.tt(tmpb_.rearrange("p (h x) -> p h x", h=8), rotv, tb2, ALU.mult)
                    p.tt(qkb_[:, base:base + 512], tmpa_, tmpb_, ALU.add)
                for base, dsth in ((0, qf_hbm), (512, kf_hbm)):
                    for c in range(4):
                        pcol = r * 512 + c * 128
                        p.tr(psH.bitcast(BF16)[:, pcol:pcol + 128],
                             qkb_[:, base + c * 128:base + (c + 1) * 128], identb)
                        p.copy(trc4_[:, c, :], psH.bitcast(BF16)[:, pcol:pcol + 128])
                    p.dma("gpsimd", dsth[b, 0:512, t * 128:(t + 1) * 128].rearrange("(c p) q -> p c q", p=128),
                          trc4_)

            zipper(lambda: [qkv_body(t) for t in range(0, 16, 2)],
                   lambda: [qkv_body(t) for t in range(1, 16, 2)])

            def att_body(hc, hh):
                h = 2 * hc + hh
                hp = 64 * hh
                pS = ((psA, psB), (psD, psE))[hh]
                pav = (psC, psG)[hh]
                ptr = (psF, psH)[hh]
                vtmh_ = vtmh2[hh]
                smlj = smlj2[hh]
                smld = mb_s2[hh]
                pTrow_ = pTrow2[hh]
                ptile_ = ptile2[hh]
                obuf_ = obuf2[hh]
                m_a4 = m_a2[hh]
                m_d4 = m_b2_[hh]
                m_col = m_col2[hh]
                mneg8 = mneg82[hh]
                den_a = den_a2[hh]
                den_b = den_b2[hh]
                fac4 = fac42[hh]
                p.dma("sync", vtmh_, vf_hbm[b].rearrange("(t p) e -> p t e", p=128)[:, :, h * 64:(h + 1) * 64])
                tr_slot = [0]
                for w in range(4):
                    qa = w * 512
                    for j in range(4):
                        q0 = (w * 4 + j) * 128
                        p.mm(pav[:, j * 128:(j + 1) * 128], qfm_c[hp:hp + 64, q0:q0 + 128],
                             kfm_c[hp:hp + 64, q0:q0 + 128])
                    p.stt(smld, pav, 1.0, s_cat4, ALU.mult, ALU.add)
                    if w == 0:
                        p.memset(m_a4[:, 0:1], NEG)
                        p.memset(den_a[:, 0:1], 0.0)
                        for j in range(1, 4):
                            q0 = j * 128
                            p.mm(pS[j % 2][:, 0:q0], qfm_c[hp:hp + 64, q0:q0 + 128],
                                 kfm_c[hp:hp + 64, 0:q0])
                            p.copy(smlj[j][:, 0:q0], pS[j % 2][:, 0:q0], eng="scalar")
                            p.red(m_a4[:, j:j + 1], smlj[j][:, 0:q0], ALU.max)
                    else:
                        for j in range(4):
                            q0 = (w * 4 + j) * 128
                            p.mm(pS[j % 2], qfm_c[hp:hp + 64, q0:q0 + 128],
                                 kfm_c[hp:hp + 64, q0 - 512:q0])
                            p.stt(smlj[j], pS[j % 2], 1.0, s_cam, ALU.mult, ALU.add)
                            p.red(m_a4[:, j:j + 1], smlj[j], ALU.max)
                    p.red(m_d4, smld.rearrange("p (j x) -> p j x", j=4), ALU.max)
                    p.tt(m_col, m_a4, m_d4, ALU.max)
                    p.ts(mneg8, m_col, -8.0, None, ALU.mult)
                    for j in range(4):
                        if w == 0:
                            if j > 0:
                                p.act(pTrow_[:, j, 0:j * 128], smlj[j][:, 0:j * 128], AF.Exp,
                                      bias=mneg8[:, j:j + 1], scale=8.0, accum_out=den_a[:, j:j + 1])
                        else:
                            p.act(pTrow_[:, j, 0:512], smlj[j], AF.Exp,
                                  bias=mneg8[:, j:j + 1], scale=8.0, accum_out=den_a[:, j:j + 1])
                        p.act(pTrow_[:, j, 512:640], smld[:, j * 128:(j + 1) * 128], AF.Exp,
                              bias=mneg8[:, j:j + 1], scale=8.0, accum_out=den_b[:, j:j + 1])
                    p.tt(den_a, den_a, den_b, ALU.add)
                    p.recip(fac4, den_a)
                    for j in range(4):
                        t = w * 4 + j
                        nm_ = (t if w == 0 else 4)
                        p.act(pTrow_[:, j, 0:640], pTrow_[:, j, 0:640], AF.Identity,
                              scale=fac4[:, j:j + 1])
                        kts_j = list(range(t - nm_, t)) + [t]
                        srcs = [pTrow_[:, j, m * 128:(m + 1) * 128] for m in range(nm_)] +                                [pTrow_[:, j, 512:640]]
                        tiles = []
                        for m in range(len(kts_j)):
                            sl = tr_slot[0] % 8
                            tr_slot[0] += 1
                            p.tr(ptr.bitcast(BF16)[:, sl * 128:(sl + 1) * 128], srcs[m], identb)
                            p.copy(ptile_[:, sl, :], ptr.bitcast(BF16)[:, sl * 128:(sl + 1) * 128],
                                   eng=("vector", "scalar")[m % 2])
                            tiles.append(sl)
                        for m, kt in enumerate(kts_j):
                            p.mm(pav[0:64, j * 128:(j + 1) * 128], vtmh_[:, kt, :], ptile_[:, tiles[m], :],
                                 start=(m == 0), stop=(m == len(kts_j) - 1), tag="avl2")
                    p.copy(obuf_[0:64, :], pav[0:64, 0:512])
                    p.dma("gpsimd", o_hbm[h * 64:(h + 1) * 64, qa:qa + 512], obuf_[0:64, :])

            for hc in range(4):
                p.dma("sync", qfm_c, qf_hbm[b, hc * 128:(hc + 1) * 128, :])
                p.dma("sync", kfm_c, kf_hbm[b, hc * 128:(hc + 1) * 128, :])
                zipper(lambda: att_body(hc, 0), lambda: att_body(hc, 1))
            for tc in range(4):
                tsl = slice(tc * 512, (tc + 1) * 512)
                o4v = o4 if tc % 2 == 0 else o4b
                p.dma("sync", o4v, o_hbm.rearrange("(c p) s -> p c s", p=128)[:, :, tsl])

                def wo_body(es):
                    for e in es:
                        ps = (psA, psD)[e % 2]
                        xc2 = xcur2[e % 2]
                        p.dma("sync", xc2, src[b, e * 128:(e + 1) * 128, tsl])
                        for c in range(4):
                            p.mm(ps, wv(OFF_WO, c, 1024)[:, e * 128:(e + 1) * 128], o4v[:, c, :],
                                 start=(c == 0), stop=(c == 3))
                        p.tt(xc2, xc2, ps, ALU.add)
                        p.dma("gpsimd", xfa[b, e * 128:(e + 1) * 128, tsl], xc2)

                zipper(lambda: wo_body([0, 2, 4, 6]), lambda: wo_body([1, 3, 5, 7]))
        p.fence_all()
        for c in range(8):
            for hv in range(2):
                p.dma("sync", wtmp, ins[f"w1a_{l}"][:, c, hv * 1408:(hv + 1) * 1408])
                p.copy(wv(OFF_W1A, c, 2816)[:, hv * 1408:(hv + 1) * 1408], wtmp)
                p.dma("sync", wtmp, ins[f"w1g_{l}"][:, c, hv * 1408:(hv + 1) * 1408])
                p.copy(wv(OFF_W1G, c, 2816)[:, hv * 1408:(hv + 1) * 1408], wtmp)
        for c in range(22):
            p.dma("sync", wtmp[:, 0:1024], ins[f"w2_{l}"][:, c, :])
            p.copy(wv(OFF_W2, c, 1024), wtmp[:, 0:1024])
        for b in range(B):
            shift_copy(shf_hbm, xfa, b, cast=False)
            for tc in range(4):
                tsl = slice(tc * 512, (tc + 1) * 512)
                xc = xcur[:, 0:512]
                sq = msq
                for c in range(8):
                    p.dma("sync", xc, shf_hbm[b, c * 128:(c + 1) * 128, tsl])
                    p.mm(psA[0:1, 0:512], ones_c, xc, start=(c == 0), stop=(c == 7), tag="mu")
                    p.act(sq, xc, AF.Square)
                    p.mm(psB[0:1, 0:512], ones_c, sq, start=(c == 0), stop=(c == 7), tag="ssq")
                p.ts(stat[0:1, 0:512], psA[0:1, 0:512], 1.0 / D, None, ALU.mult)
                p.ts(stat[0:1, 512:1024], psB[0:1, 0:512], 1.0 / D, None, ALU.mult)
                p.act(stat[0:1, 1024:1536], stat[0:1, 0:512], AF.Square)
                p.tt(stat[0:1, 512:1024], stat[0:1, 512:1024], stat[0:1, 1024:1536], ALU.subtract)
                p.act(stat[0:1, 512:1024], stat[0:1, 512:1024], AF.Ln, bias=eps5[0:1, 0:1])
                p.act(stat[0:1, 512:1024], stat[0:1, 512:1024], AF.Exp, scale=-0.5)
                p.mm(psD[:, 0:512], ones_r, stat[0:1, 0:512], tag="mub")
                p.copy(mub, psD[:, 0:512], eng="scalar")
                p.mm(psD[:, 0:512], ones_r, stat[0:1, 512:1024], tag="rsb")
                p.copy(rsb, psD[:, 0:512], eng="scalar")
                for c in range(8):
                    p.dma("sync", xc, shf_hbm[b, c * 128:(c + 1) * 128, tsl])
                    p.tt(sq, xc, mub, ALU.subtract)
                    p.tt(sq, sq, rsb, ALU.mult)
                    p.ts(hln[:, c, :], sq, s_small[f"lg_{l}"][:, c:c + 1],
                         s_small[f"lb_{l}"][:, c:c + 1], ALU.mult, ALU.add)
                for hc in range(22):
                    pa = psA if hc % 2 == 0 else psE
                    pb = psB if hc % 2 == 0 else psF
                    ab = abuf if hc % 2 == 0 else abuf2
                    gb = gbuf if hc % 2 == 0 else gbuf2
                    for c in range(8):
                        p.mm(pa[:, 0:512], wv(OFF_W1A, c, 2816)[:, hc * 128:(hc + 1) * 128], hln[:, c, :],
                             start=(c == 0), stop=(c == 7))
                    for c in range(8):
                        p.mm(pb[:, 0:512], wv(OFF_W1G, c, 2816)[:, hc * 128:(hc + 1) * 128], hln[:, c, :],
                             start=(c == 0), stop=(c == 7))
                    p.ts(ab, pa[:, 0:512], s_small[f"b1a_{l}"][:, hc:hc + 1], None, ALU.add)
                    p.act(gb, pb[:, 0:512], AF.Gelu, bias=s_small[f"b1g_{l}"][:, hc:hc + 1])
                    p.tt(glu[:, hc, :], ab, gb, ALU.mult)
                for e in range(8):
                    pc = psC if e % 2 == 0 else psD
                    for hc in range(22):
                        p.mm(pc[:, 0:512], wv(OFF_W2, hc, 1024)[:, e * 128:(e + 1) * 128], glu[:, hc, :],
                             start=(hc == 0), stop=(hc == 21))
                    p.ts(sq, pc[:, 0:512], s_small[f"b2_{l}"][:, e:e + 1], None, ALU.add)
                    p.dma("sync", xc, xfa[b, e * 128:(e + 1) * 128, tsl])
                    p.tt(xc, xc, sq, ALU.add)
                    if l == 1:
                        p.act(xc, xc, AF.Silu)
                        p.copy(qout[:, 0:512], xc)
                        p.dma("gpsimd", g1_out[b, e * 128:(e + 1) * 128, tsl], qout[:, 0:512])
                    else:
                        p.dma("gpsimd", xfb[b, e * 128:(e + 1) * 128, tsl], xc)
    p.emit()
    return nc


def run_phaseC2(inputs, xn1f_shards, n_cores=8):
    w = inputs
    wC2 = wC2_prep(w)
    cC2 = hostC2_consts()
    nc = build_phaseC2()
    in_maps = []
    for c in range(n_cores):
        m = dict(wC2)
        m.update(cC2)
        m["xn1f_fm"] = xn1f_shards[c]
        in_maps.append(m)
    res = run_bass_kernel_spmd(nc, in_maps, core_ids=list(range(n_cores)))
    return res.results



def hostD_consts():
    i = np.arange(128)[:, None]
    j = np.arange(128)[None, :]
    gcols = np.array([0, 1, 2, 3] + list(range(64, 2048, 64)), np.int64)
    c = {}
    c["identD"] = np.eye(128, dtype=np.float32)
    def addm(valid):
        return np.where(valid, 0.0, NEG).astype(np.float32)
    k_, q_ = i, j
    same = (k_ <= q_) | ((k_ % 64) == 0)
    mT_same = addm(same)
    mT_same0 = addm(same | (k_ < 4))
    prev = (k_ >= q_) | ((k_ % 64) == 0)
    mT_prev = addm(prev)
    mT_prev1 = addm(prev | (k_ < 4))
    c["mT_same0"] = mT_same0
    c["mT_ps"] = np.ascontiguousarray(np.concatenate([mT_prev, mT_same], axis=1))
    c["mT_ps1"] = np.ascontiguousarray(np.concatenate([mT_prev1, mT_same], axis=1))
    c["m_s0s"] = np.ascontiguousarray(mT_same0.T) * 0.125
    c["m_ps"] = np.ascontiguousarray(np.concatenate([mT_prev.T, mT_same.T], axis=1)) * 0.125
    c["m_ps1"] = np.ascontiguousarray(np.concatenate([mT_prev1.T, mT_same.T], axis=1)) * 0.125
    gT = np.zeros((35, 2048), np.float32)
    for t in range(16):
        lo, hi = 128 * (t - 1), 128 * (t + 1)
        excl = (gcols >= lo) & (gcols < hi)
        gT[excl, t * 128:(t + 1) * 128] = NEG
    c["gT"] = gT
    c["g_rm"] = np.ascontiguousarray(gT.T) * 0.125
    sel = np.zeros((256, 16), np.float32)
    r = np.arange(256)
    sel[r, r % 16] = 1.0
    c["selD"] = sel
    return c


def wD_prep(w):
    Wq = np.asarray(w["L1_attn_Wqkv"], np.float32)
    return {
        "wqkvD": np.ascontiguousarray(Wq.reshape(8, 128, 3072).transpose(1, 0, 2)),
        "bqkD": np.ascontiguousarray(np.asarray(w["L1_attn_bqkv"], np.float32)[0:2048].reshape(16, 128).T),
        "bvD": np.asarray(w["L1_attn_bqkv"], np.float32)[2048:3072].reshape(1, 1024),
        "woutD": np.ascontiguousarray(np.asarray(w["L1_attn_Wout"], np.float32).reshape(8, 128, 1024).transpose(1, 0, 2)),
        "boutD": np.ascontiguousarray(np.asarray(w["L1_attn_bout"], np.float32).reshape(8, 128).T),
    }


def build_phaseD():
    nc = bass.Bass("TRN2", target_bir_lowering=False, name="phD")
    B, S, D = 2, 2048, 1024
    H = 16
    x1 = nc.dram_tensor("x1_fm", [B, D, S], F32, kind="ExternalInput").ap()
    xn1 = nc.dram_tensor("xn1_fm", [B, D, S], BF16, kind="ExternalInput").ap()
    g1 = nc.dram_tensor("g1_fm", [B, D, S], BF16, kind="ExternalInput").ap()
    spk_all = nc.dram_tensor("spk_all", [256, S], F32, kind="ExternalInput").ap()
    wqkv = nc.dram_tensor("wqkvD", [128, 8, 3072], F32, kind="ExternalInput").ap()
    bqk = nc.dram_tensor("bqkD", [128, 16], F32, kind="ExternalInput").ap()
    bv = nc.dram_tensor("bvD", [1, 1024], F32, kind="ExternalInput").ap()
    wout = nc.dram_tensor("woutD", [128, 8, 1024], F32, kind="ExternalInput").ap()
    bout = nc.dram_tensor("boutD", [128, 8], F32, kind="ExternalInput").ap()
    cst = {}
    for nm, shp in (("identD", [128, 128]), ("mT_same0", [128, 128]),
                    ("mT_ps", [128, 256]), ("mT_ps1", [128, 256]),
                    ("m_s0s", [128, 128]), ("m_ps", [128, 256]), ("m_ps1", [128, 256]),
                    ("gT", [35, 2048]), ("g_rm", [2048, 35]), ("selD", [256, 16])):
        cst[nm] = nc.dram_tensor(nm, shp, F32, kind="ExternalInput").ap()
    out = nc.dram_tensor("out", [B * S, D], F32, kind="ExternalOutput").ap()
    qf_hbm = nc.dram_tensor("qfD", [B, D, S], BF16).ap()
    kf_hbm = nc.dram_tensor("kfD", [B, D, S], BF16).ap()
    vt_hbm = nc.dram_tensor("vtD", [B, S, D], BF16).ap()
    vdh2 = [nc.dram_tensor(f"vdenD{r}", [S, 65], BF16).ap() for r in range(2)]

    p = Prog(nc)
    al = nc.alloc_sbuf_tensor
    s_id = al("s_idD", [128, 128], F32).ap()
    identb = al("identbD", [128, 128], BF16).ap()
    s_cst = {}
    for nm in cst:
        if nm in ("identD", "selD", "g_rm"):
            continue
        shp = list(cst[nm].shape)
        s_cst[nm] = al("s_" + nm, shp, F32).ap()
    g_rm_sb = al("s_g_rm", [128, 16, 35], F32).ap()
    wqb = al("wqbD", [128, 8, 3072], BF16).ap()
    wob = al("wobD", [128, 8, 1024], BF16).ap()
    wtmp = al("wtmpD", [128, 1024], F32).ap()
    s_bqk = al("s_bqkD", [128, 16], F32).ap()
    s_bvr = al("s_bvrD", [1, 1024], F32).ap()
    bvb = al("bvbD", [128, 1024], F32).ap()
    s_bout = al("s_boutD", [128, 8], F32).ap()
    ones_r = al("ones_rD", [1, 128], F32).ap()
    spkm = al("spkmD", [16, 2048], F32).ap()
    spkT = al("spkTD", [128, 16, 16], F32).ap()
    xnf = al("xnfD", [128, 8, 512], BF16).ap()
    prt2 = [al(f"prtD{r}", [128, 512], F32).ap() for r in range(2)]
    prb2 = [al(f"prbD{r}", [128, 512], BF16).ap() for r in range(2)]
    qfm2 = [al(f"qfm_cD{r}", [128, 2048], BF16).ap() for r in range(2)]
    kfm2 = [al(f"kfm_cD{r}", [128, 2048], BF16).ap() for r in range(2)]
    vtmh2 = [al(f"vtmhD{r}", [128, 16, 64], BF16).ap() for r in range(2)]
    vden2 = [al(f"vdenD_s{r}", [128, 16, 65], BF16).ap() for r in range(2)]
    vcden2 = [al(f"vcdenD{r}", [35, 65], BF16).ap() for r in range(2)]
    kc2 = [al(f"kcD{r}", [128, 35], BF16).ap() for r in range(2)]
    sml2 = [al(f"smlD{r}", [128, 256], F32).ap() for r in range(2)]
    smlg2 = [al(f"smlgD{r}", [35, 128], F32).ap() for r in range(2)]
    smg2 = [al(f"smgD{r}", [128, 35], F32).ap() for r in range(2)]
    m_b2 = [al(f"m_bD{r}", [128, 1], F32).ap() for r in range(2)]
    m_c2 = [al(f"m_cD{r}", [128, 1], F32).ap() for r in range(2)]
    m_col2 = [al(f"m_colD{r}", [128, 1], F32).ap() for r in range(2)]
    rowb2 = [al(f"rowbD{r}", [1, 640], F32).ap() for r in range(2)]
    cmb2 = [al(f"cmbD{r}", [128, 256], F32).ap() for r in range(2)]
    cmbg2 = [al(f"cmbgD{r}", [35, 128], F32).ap() for r in range(2)]
    pT2 = [al(f"pTD{r}", [128, 256], BF16).ap() for r in range(2)]
    pTg2 = [al(f"pTgD{r}", [35, 128], BF16).ap() for r in range(2)]
    fac_b2 = [al(f"fac_bD{r}", [64, 128], F32).ap() for r in range(2)]
    o_fm = al("o_fmD", [128, 8, 2048], BF16).ap()
    x1t2 = [al(f"x1tD{r}", [128, 512], F32).ap() for r in range(2)]
    g1t2 = [al(f"g1tD{r}", [128, 512], BF16).ap() for r in range(2)]
    outt2 = [al(f"outtD{r}", [128, 512], F32).ap() for r in range(2)]

    psA = nc.alloc_psum_tensor("psA", [128, 512], F32).ap()
    psB = nc.alloc_psum_tensor("psB", [128, 512], F32).ap()
    psC = nc.alloc_psum_tensor("psC", [128, 512], F32).ap()
    psD = nc.alloc_psum_tensor("psD", [128, 512], F32).ap()
    psE = nc.alloc_psum_tensor("psE", [128, 512], F32).ap()
    psF = nc.alloc_psum_tensor("psF", [128, 512], F32).ap()
    psG = nc.alloc_psum_tensor("psG", [128, 512], F32).ap()
    psAs2 = (psA, psG)
    psBs2 = (psB, psD)
    psCs2 = (psC, psE)

    p.dma("sync", s_id, cst["identD"])
    p.copy(identb, s_id)
    for nm in s_cst:
        p.dma("sync", s_cst[nm], cst[nm])
    p.dma("sync", g_rm_sb, cst["g_rm"].rearrange("(t p) g -> p t g", p=128))
    for c in range(8):
        for third in range(3):
            p.dma("sync", wtmp, wqkv[:, c, third * 1024:(third + 1) * 1024])
            p.copy(wqb[:, c, third * 1024:(third + 1) * 1024], wtmp)
        p.dma("sync", wtmp, wout[:, c, :])
        p.copy(wob[:, c, :], wtmp)
    p.dma("sync", s_bqk, bqk)
    p.dma("sync", s_bvr, bv)
    p.dma("sync", s_bout, bout)
    p.memset(ones_r, 1.0)
    p.mm(psA, ones_r, s_bvr[0:1, 0:512], tag="bvb0")
    p.copy(bvb[:, 0:512], psA)
    p.mm(psA, ones_r, s_bvr[0:1, 512:1024], tag="bvb1")
    p.copy(bvb[:, 512:1024], psA)
    spk_sb = al("spk_sbD", [128, 2048], F32).ap()
    sel_sb = al("sel_sbD", [128, 2, 16], F32).ap()
    p.dma("sync", sel_sb, cst["selD"].rearrange("(c p) h -> p c h", p=128))
    for ch in range(2):
        for qc in range(4):
            p.dma("sync", spk_sb[:, qc * 512:(qc + 1) * 512],
                  spk_all[ch * 128:(ch + 1) * 128, qc * 512:(qc + 1) * 512])
        for qc in range(4):
            ps = (psB, psC, psD, psE)[qc]
            p.mm(ps[0:16, :], sel_sb[:, ch, :], spk_sb[:, qc * 512:(qc + 1) * 512],
                 start=(ch == 0), stop=(ch == 1), tag=f"spk{qc}")
            if ch == 1:
                p.ts(spkm[:, qc * 512:(qc + 1) * 512], ps[0:16, :], 0.0, None, ALU.is_gt)
    for t in range(16):
        p.tr(psE[0:128, 0:16], spkm[:, t * 128:(t + 1) * 128], s_id[0:16, 0:16], tag="trspk")
        p.copy(spkT[:, t, :], psE[0:128, 0:16])
    for b in range(B):
        for tc in range(4):
            tsl = slice(tc * 512, (tc + 1) * 512)
            p.dma("sync", xnf, xn1[b].rearrange("(c p) s -> p c s", p=128)[:, :, tsl])
            for e in range(16):
                ps = (psA, psB)[e % 2]
                pr = prt2[e % 2]
                pbb = prb2[e % 2]
                for c in range(8):
                    p.mm(ps, wqb[:, c, e * 128:(e + 1) * 128], xnf[:, c, :],
                         start=(c == 0), stop=(c == 7))
                p.ts(pr, ps, s_bqk[:, e:e + 1], None, ALU.add)
                p.copy(pbb, pr)
                dsth = qf_hbm if e < 8 else kf_hbm
                ee = e % 8
                p.dma("gpsimd", dsth[b, ee * 128:(ee + 1) * 128, tsl], pbb)
            for j in range(4):
                t0 = tc * 512 + j * 128
                for half in range(2):
                    ps = psD if half == 0 else psE
                    pr = prt2[half]
                    pbb = prb2[half]
                    for c in range(8):
                        p.mm(ps, xnf[:, c, j * 128:(j + 1) * 128],
                             wqb[:, c, 2048 + half * 512:2048 + (half + 1) * 512],
                             start=(c == 0), stop=(c == 7))
                    p.tt(pr, ps, bvb[:, half * 512:(half + 1) * 512], ALU.add)
                    p.copy(pbb, pr)
                    p.dma("gpsimd", vt_hbm[b, t0:t0 + 128, half * 512:(half + 1) * 512], pbb)
        p.fence_all()
        for r in range(2):
            for kt in range(16):
                p.memset(vden2[r][:, kt, 64:65], 1.0)
        for hc in range(8):
            qf_c = qfm2[hc % 2]
            kf_c = kfm2[hc % 2]
            p.dma("sync", qf_c, qf_hbm[b, hc * 128:(hc + 1) * 128, :])
            p.dma("sync", kf_c, kf_hbm[b, hc * 128:(hc + 1) * 128, :])
            for hh in range(2):
                h = 2 * hc + hh
                ph = h % 2
                kc = kc2[ph]
                vtmh = vtmh2[ph]
                vden = vden2[ph]
                vcden = vcden2[ph]
                vdh = vdh2[ph]
                hp = 64 * hh
                p.copy(kc[hp:hp + 64, 0:4], kf_c[hp:hp + 64, 0:4], eng="scalar")
                p.copy(kc[hp:hp + 64, 4:35],
                       kf_c.rearrange("p (a c) -> p a c", c=64)[hp:hp + 64, 1:32, 0], eng="scalar")
                p.dma("sync", vtmh, vt_hbm[b].rearrange("(t p) e -> p t e", p=128)[:, :, h * 64:(h + 1) * 64])
                for kt in range(16):
                    p.act(vden[:, kt, 0:64], vtmh[:, kt, :], AF.Identity, scale=spkT[:, kt, h:h + 1])
                    p.copy(vden[:, kt, 64:65], spkT[:, kt, h:h + 1], eng="scalar")
                p.dma("gpsimd", vdh.rearrange("(t p) e -> p t e", p=128), vden)
                p.dma("sync", vcden[0:4, :], vdh[0:4, :])
                p.dma("sync", vcden[4:35, :], vdh.rearrange("(a c) e -> a c e", c=64)[1:32, 0, :])

                def S1(t):
                    par = t % 2
                    pa, pb = psAs2[par], psBs2[par]
                    q0 = t * 128
                    qv = qf_c[hp:hp + 64, q0:q0 + 128]
                    if t > 0:
                        p.mm(pa[:, 0:256], qv, kf_c[hp:hp + 64, q0 - 128:q0 + 128], tag="st2")
                        p.mm(pb[:, 128:256], kf_c[hp:hp + 64, q0 - 128:q0], qv, tag="nT")
                    else:
                        p.mm(pa[:, 128:256], qv, kf_c[hp:hp + 64, 0:128], tag="st2")
                    p.mm(pa[:, 256:291], qv, kc[hp:hp + 64, :], tag="stg")
                    p.mm(pb[0:35, 0:128], kc[hp:hp + 64, :], qv, tag="gT")
                    p.mm(pb[:, 256:384], kf_c[hp:hp + 64, q0:q0 + 128], qv, tag="nT")

                def V1(t):
                    par = t % 2
                    pa = psAs2[par]
                    sml, smg = sml2[par], smg2[par]
                    m_b, m_c, m_col = m_b2[par], m_c2[par], m_col2[par]
                    if t == 0:
                        p.stt(sml[:, 128:256], pa[:, 128:256], 0.125, s_cst["m_s0s"], ALU.mult, ALU.add)
                        p.red(m_b, sml[:, 128:256], ALU.max)
                    else:
                        mk = "m_ps1" if t == 1 else "m_ps"
                        p.stt(sml, pa[:, 0:256], 0.125, s_cst[mk], ALU.mult, ALU.add)
                        p.red(m_b, sml, ALU.max)
                    p.stt(smg, pa[:, 256:291], 0.125, g_rm_sb[:, t, :], ALU.mult, ALU.add)
                    p.red(m_c, smg, ALU.max)
                    p.tt(m_col, m_b, m_c, ALU.max)

                def X(t):
                    par = t % 2
                    pa, pb, pc = psAs2[par], psBs2[par], psCs2[par]
                    sml, smlg = sml2[par], smlg2[par]
                    cmb, cmbg = cmb2[par], cmbg2[par]
                    pT, pTg = pT2[par], pTg2[par]
                    m_col = m_col2[par]
                    mneg2 = rowb2[par][0:1, 0:256]
                    q0 = t * 128
                    fb = par * 256
                    p.tr(pa[0:1, 384:512], m_col, s_id, tag="c2rD")
                    p.ts(mneg2[0:1, 0:128], pa[0:1, 384:512], -1.0, None, ALU.mult, eng="scalar")
                    p.ts(mneg2[0:1, 128:256], pa[0:1, 384:512], -1.0, None, ALU.mult, eng="scalar")
                    p.mm(psF[:, fb:fb + 256], ones_r, mneg2, tag="bcmD")
                    p.tt(cmbg, s_cst["gT"][:, q0:q0 + 128], psF[0:35, fb:fb + 128], ALU.add)
                    p.stt(smlg, pb[0:35, 0:128], 0.125, cmbg, ALU.mult, ALU.add)
                    p.act(pTg, smlg, AF.Exp)
                    p.mm(pc[0:65, 0:128], vcden, pTg, start=True, stop=False, tag="avgD")
                    if t == 0:
                        p.tt(cmb[:, 128:256], s_cst["mT_same0"], psF[:, fb + 128:fb + 256], ALU.add)
                        p.stt(sml[:, 128:256], pb[:, 256:384], 0.125, cmb[:, 128:256], ALU.mult, ALU.add)
                        p.act(pT[:, 128:256], sml[:, 128:256], AF.Exp)
                        p.mm(pc[0:65, 0:128], vden[:, 0, :], pT[:, 128:256],
                             start=False, stop=True, tag="avnD")
                    else:
                        mk = "mT_ps1" if t == 1 else "mT_ps"
                        p.tt(cmb, s_cst[mk], psF[:, fb:fb + 256], ALU.add)
                        p.stt(sml, pb[:, 128:384], 0.125, cmb, ALU.mult, ALU.add)
                        p.act(pT, sml, AF.Exp)
                        p.mm(pc[0:65, 0:128], vden[:, t - 1, :], pT[:, 0:128],
                             start=False, stop=False, tag="avnD")
                        p.mm(pc[0:65, 0:128], vden[:, t, :], pT[:, 128:256],
                             start=False, stop=True, tag="avnD")

                def N(t):
                    par = t % 2
                    pb, pc = psBs2[par], psCs2[par]
                    den_r = rowb2[par][0:1, 256:384]
                    fac_r = rowb2[par][0:1, 384:512]
                    rv_r = rowb2[par][0:1, 512:640]
                    fac_b = fac_b2[par]
                    q0 = t * 128
                    p.copy(den_r, pc[64:65, 0:128], eng="scalar")
                    p.ts(rv_r, den_r, 0.0, None, ALU.is_gt)
                    p.ts(fac_r, den_r, 1e-30, None, ALU.max)
                    p.recip(fac_r, fac_r)
                    p.tt(fac_r, fac_r, rv_r, ALU.mult)
                    p.mm(pb[:, 384:512], ones_r, fac_r, tag="bcfD")
                    p.copy(fac_b, pb[0:64, 384:512], eng="scalar")
                    p.tt(o_fm[hp:hp + 64, hc, q0:q0 + 128], pc[0:64, 0:128], fac_b, ALU.mult)

                S1(0)
                V1(0)
                for t in range(16):
                    if t + 1 < 16:
                        S1(t + 1)
                    X(t)
                    if t + 1 < 16:
                        V1(t + 1)
                    N(t)
        for tc in range(4):
            tsl = slice(tc * 512, (tc + 1) * 512)
            for e in range(8):
                par = e % 2
                ps = (psA, psB)[par]
                pr = prt2[par]
                x1t = x1t2[par]
                g1t = g1t2[par]
                outt = outt2[par]
                p.dma("sync", x1t, x1[b, e * 128:(e + 1) * 128, tsl])
                p.dma("sync", g1t, g1[b, e * 128:(e + 1) * 128, tsl])
                for c in range(8):
                    p.mm(ps, wob[:, c, e * 128:(e + 1) * 128], o_fm[:, c, tsl],
                         start=(c == 0), stop=(c == 7))
                p.ts(pr, ps, s_bout[:, e:e + 1], None, ALU.add)
                p.tt(pr, pr, g1t, ALU.mult)
                p.tt(pr, pr, x1t, ALU.add)
                for j in range(4):
                    pt = (psD, psE)[j % 2]
                    p.tr(pt[:, 0:128], pr[:, j * 128:(j + 1) * 128], s_id)
                    p.copy(outt[:, j * 128:(j + 1) * 128], pt[:, 0:128])
                p.dma("gpsimd", out[b * S + tc * 512:b * S + (tc + 1) * 512,
                                    e * 128:(e + 1) * 128].rearrange("(j p) q -> p j q", p=128),
                      outt.rearrange("p (j q) -> p j q", j=4))
    p.emit()
    return nc


def run_phaseD(inputs, x1_shards, xn1_shards, g1_shards, spikes_all, n_cores=8):
    w = inputs
    wD = wD_prep(w)
    cD = hostD_consts()
    nc = build_phaseD()
    in_maps = []
    for c in range(n_cores):
        m = dict(wD)
        m.update(cD)
        m["x1_fm"] = x1_shards[c]
        m["xn1_fm"] = xn1_shards[c]
        m["g1_fm"] = g1_shards[c]
        m["spk_all"] = spikes_all
        in_maps.append(m)
    res = run_bass_kernel_spmd(nc, in_maps, core_ids=list(range(n_cores)))
    return res.results



def kernel(**inputs):
    n = 8
    w = inputs
    resA = run_phaseA(w, n_cores=n)
    xn_all = np.concatenate([resA[c]["xn_bf16"] for c in range(n)], axis=0)
    scores_all = np.concatenate([resA[c]["scores"] for c in range(n)], axis=0)
    resB = run_phaseB(w, xn_all, scores_all, n_cores=n)
    Zall = np.stack([resB[e]["Z"] for e in range(n)], axis=0)
    x = np.asarray(w["L0_attn_x"], np.float32).reshape(16 * 2048, 1024)
    x_shards = [np.ascontiguousarray(x[c * 4096:(c + 1) * 4096]) for c in range(n)]
    s0_shards = [resA[c]["s0_fm"] for c in range(n)]
    resC = run_phaseC(w, x_shards, s0_shards, Zall, n_cores=n)
    xn1f_shards = [resC[c]["xn1f_fm"] for c in range(n)]
    resC2 = run_phaseC2(w, xn1f_shards, n_cores=n)
    spk_all = np.concatenate([resC[c]["spikes"] for c in range(n)], axis=0)
    x1_shards = [resC[c]["x1_fm"] for c in range(n)]
    xn1_shards = [resC[c]["xn1_fm"] for c in range(n)]
    g1_shards = [resC2[c]["g1_fm"] for c in range(n)]
    resD = run_phaseD(w, x1_shards, xn1_shards, g1_shards, spk_all, n_cores=n)
    out = np.concatenate([resD[c]["out"] for c in range(n)], axis=0)
    return out.reshape(16, 2048, 1024).astype(np.float32)
